# version 17
# baseline (speedup 1.0000x reference)
import numpy as np
import concourse.bass as bass
import concourse.mybir as mybir
from concourse.bass_utils import run_bass_kernel_spmd
from contextlib import ExitStack

F32 = mybir.dt.float32
F32R = mybir.dt.float32r
BF16 = mybir.dt.bfloat16
U32 = mybir.dt.uint32
AF = mybir.ActivationFunctionType
ALU = mybir.AluOpType
AX = mybir.AxisListType

D = 1024
INNER = 512
NT = 18
NQT = 16
TOK_IN = NT * 128
TOK_OWN = 2048
NE = 8
CAP = 384
TOKW = CAP
NRT = CAP // 128
FF = 4096
SORT_ROWS = NE * CAP
TRASH = SORT_ROWS
XS_ROWS = SORT_ROWS + TOK_OWN
AUGW = 1028
NEG = -1.0e30
GRP = [(0, 4), (4, 4), (8, 4), (12, 4), (16, 2)]


class _Sem:
    __slots__ = ("h", "n")

    def __init__(self, h):
        self.h = h
        self.n = 0

    def inc(self, instr, k=1):
        instr.then_inc(self.h, k)
        self.n += k
        return instr


def _build_nc():
    nc = bass.Bass("TRN2", target_bir_lowering=False)

    x_in = nc.dram_tensor("x_in", [TOK_IN, D], F32, kind="ExternalInput").ap()
    wqkv = nc.dram_tensor("wqkv", [D, 3 * INNER], F32, kind="ExternalInput").ap()
    wout = nc.dram_tensor("wout", [INNER, D], F32, kind="ExternalInput").ap()
    wr_d = nc.dram_tensor("wr", [D, NE], F32, kind="ExternalInput").ap()
    br_d = nc.dram_tensor("br", [NE, 1], F32, kind="ExternalInput").ap()
    w1_d = nc.dram_tensor("w1", [NE, D, FF], F32, kind="ExternalInput").ap()
    b1_d = nc.dram_tensor("b1", [128, NE * 32], F32, kind="ExternalInput").ap()
    w2_d = nc.dram_tensor("w2", [NE, FF, D], F32, kind="ExternalInput").ap()
    b2_d = nc.dram_tensor("b2", [128, NE * 8], F32, kind="ExternalInput").ap()
    ident_d = nc.dram_tensor("ident", [128, 128], F32, kind="ExternalInput").ap()
    lstrict_d = nc.dram_tensor("lstrict", [128, 128], F32, kind="ExternalInput").ap()
    masks_d = nc.dram_tensor("masks", [3, 128, 384], F32, kind="ExternalInput").ap()
    ropecq_d = nc.dram_tensor("ropecq", [128, TOK_IN], F32, kind="ExternalInput").ap()
    ropesq_d = nc.dram_tensor("ropesq", [128, TOK_IN], F32, kind="ExternalInput").ap()
    ropeck_d = nc.dram_tensor("ropeck", [128, TOK_IN], F32, kind="ExternalInput").ap()
    ropesk_d = nc.dram_tensor("ropesk", [128, TOK_IN], F32, kind="ExternalInput").ap()
    iota8_d = nc.dram_tensor("iota8", [128, NE], F32, kind="ExternalInput").ap()
    onesr_d = nc.dram_tensor("onesr", [128, 128], F32, kind="ExternalInput").ap()

    xsort = nc.dram_tensor("xsort", [XS_ROWS, AUGW], F32, kind="Internal").ap()
    out_d = nc.dram_tensor("out", [TOK_OWN, D], F32, kind="ExternalOutput").ap()

    es_top = ExitStack()
    es_qkv = ExitStack()
    es_attn = ExitStack()
    es_p2 = ExitStack()
    es_ffn = ExitStack()

    def sem(name):
        return _Sem(es_top.enter_context(nc.semaphore(name)))

    s_ld = sem("s_ld")
    s_ldg = sem("s_ldg")
    s_xl = [sem(f"s_xl{i}") for i in range(5)]
    s_swp = sem("s_swp")
    s_xg = [sem(f"s_xg{i}") for i in range(2)]
    s_w = [sem(f"s_w{i}") for i in range(4)]
    s_g = [sem(f"s_g{i}") for i in range(2)]
    s_scat = sem("s_scat")
    s_outd = [sem(f"s_outd{i}") for i in range(2)]
    s_init = sem("s_init")
    s_b12 = sem("s_b12")

    t_tp = sem("t_tp")
    t_qk = sem("t_qk")
    t_v = sem("t_v")
    v_e = sem("v_e")
    a_e = sem("a_e")
    g_ms = sem("g_ms")
    v_rope = sem("v_rope")
    t_s = sem("t_s")
    t_mt = sem("t_mt")
    v_mr = sem("v_mr")
    t_mb = sem("t_mb")
    v_me = sem("v_me")
    a_ex = sem("a_ex")
    t_pv = sem("t_pv")
    v_rz = sem("v_rz")
    t_zb = sem("t_zb")
    a_zb = sem("a_zb")
    v_at = sem("v_at")
    t_wo = sem("t_wo")
    v_z1 = sem("v_z1")
    a_rs = sem("a_rs")
    v_x1 = sem("v_x1")
    t_x1t = sem("t_x1t")
    a_xe = sem("a_xe")
    t_sc = sem("t_sc")
    a_sc = sem("a_sc")
    t_sct = sem("t_sct")
    v_tm = sem("v_tm")
    v_oh = sem("v_oh")
    v_ro = sem("v_ro")
    t_rk = sem("t_rk")
    v_sl = sem("v_sl")
    g_io = sem("g_io")
    t_wq = sem("t_wq")
    t_f1 = sem("t_f1")
    a_g = sem("a_g")
    t_f2 = sem("t_f2")
    a_y = sem("a_y")
    t_yt = sem("t_yt")
    v_zz = sem("v_zz")
    a_r2 = sem("a_r2")
    v_ln = sem("v_ln")
    t_tp3 = sem("t_tp3")
    v_e3 = sem("v_e3")

    psum = es_top.enter_context(nc.psum_tensor("ps", [128, 4096], F32))

    def bank(k, c0=0, w=512, p0=0, p1=128):
        return psum[p0:p1, k * 512 + c0:k * 512 + c0 + w]

    def sb(es, name, shape, dt=F32):
        return es.enter_context(nc.sbuf_tensor(name, shape, dt))

    wstream01 = sb(es_top, "wstream01", [128, 2 * 8192], BF16)

    def wslot(i):
        buf = wstream01 if i % 4 < 2 else wstream23
        s = (i % 4) % 2
        return buf[:, s * 8192:(s + 1) * 8192]

    def wq_src(i):
        ee, j = i // 8, i % 8
        if j < 4:
            return w1_d.rearrange("E (dc p) (qq f) -> E qq p dc f", p=128, qq=4)[ee, j]
        return w2_d.rearrange("E (qq fs p) dd -> E qq p fs dd", qq=4, p=128)[ee, j - 4]

    def wq_load(e, i):
        s_w[i % 4].inc(e.dma_start(
            wslot(i).rearrange("p (a b) -> p a b", a=8), wq_src(i)), 16)

    qrot = sb(es_attn, "qrot", [128, 4 * TOK_IN], BF16)
    krot = sb(es_attn, "krot", [128, 4 * TOK_IN], BF16)
    v_sb = sb(es_attn, "v_sb", [128, NT * 512], BF16)
    ident_sb = sb(es_attn, "ident_sb", [128, 128], F32)
    identb_sb = sb(es_attn, "identb_sb", [128, 128], BF16)
    onesr_sb = sb(es_attn, "onesr_sb", [128, 128], F32R)
    wr_sb = sb(es_attn, "wr_sb", [128, 8 * NE], F32)
    br_sb = sb(es_attn, "br_sb", [NE, 1], F32)
    iota8_sb = sb(es_attn, "iota8_sb", [128, NE], F32)
    lstrict_sb = sb(es_attn, "lstrict_sb", [128, 128], F32R)
    bnst = sb(es_attn, "bnst", [128, 12], F32)
    bnag = sb(es_attn, "bnag", [128, 2], F32)
    rstd_sb = sb(es_attn, "rstd_sb", [128, 1], F32)
    stm = sb(es_attn, "stm", [128, 32], F32)
    mx8 = sb(es_attn, "mx8", [128, 8], F32)
    idx8 = sb(es_attn, "idx8", [128, 8], U32)
    eidf = sb(es_attn, "eidf", [128, 1], F32)
    oh_sb = sb(es_attn, "oh_sb", [128, 8], F32R)
    runoff = sb(es_attn, "runoff", [1, 16], F32)
    rowoff = sb(es_attn, "rowoff", [1, 8], F32R)
    scr8 = sb(es_attn, "scr8", [128, 8], F32)
    ranksel = sb(es_attn, "ranksel", [128, 1], F32)
    ovf_sb = sb(es_attn, "ovf_sb", [128, 1], F32)
    scr1 = sb(es_attn, "scr1", [128, 1], F32)
    scr2 = sb(es_attn, "scr2", [128, 1], F32)
    scr3 = sb(es_attn, "scr3", [128, 1], F32)
    slotf = sb(es_attn, "slotf", [128, 1], F32)
    slotf2 = sb(es_attn, "slotf2", [128, 1], F32)
    slotu = sb(es_attn, "slotu", [128, 8], U32)
    tokidf = sb(es_attn, "tokidf", [128, 4], F32)
    initrow = sb(es_attn, "initrow", [128, AUGW], F32)
    eps_sb = sb(es_attn, "eps_sb", [128, 1], F32)
    lnv_sb = sb(es_attn, "lnv_sb", [128, 1], F32)

    wqkv_sb = sb(es_qkv, "wqkv_sb", [128, 8 * 1536], BF16)
    ropecq_sb = sb(es_qkv, "ropecq_sb", [128, TOK_IN], BF16)
    ropesq_sb = sb(es_qkv, "ropesq_sb", [128, TOK_IN], BF16)
    ropeck_sb = sb(es_qkv, "ropeck_sb", [128, TOK_IN], BF16)
    ropesk_sb = sb(es_qkv, "ropesk_sb", [128, TOK_IN], BF16)
    xload = sb(es_qkv, "xload", [128, 5 * D], BF16)
    xt_sb = sb(es_qkv, "xt_sb", [128, 8 * 512], BF16)
    qkraw = sb(es_qkv, "qkraw", [128, 8 * 512], BF16)
    qswap = sb(es_qkv, "qswap", [128, 8 * 512], BF16)
    rt1 = sb(es_qkv, "rt1", [128, 512], BF16)
    rt2 = sb(es_qkv, "rt2", [128, 512], BF16)

    with nc.Block() as blk:
        @blk.sync
        def _(e):
            s_ld.inc(e.dma_start(wr_sb[:].rearrange("p (dc e) -> p dc e", dc=8),
                                 wr_d.rearrange("(dc p) e -> p dc e", p=128)), 16)
            s_ld.inc(e.dma_start(br_sb[:], br_d), 16)
            s_ld.inc(e.dma_start(ident_sb[:], ident_d), 16)
            s_ld.inc(e.dma_start(lstrict_sb[:], lstrict_d.bitcast(F32R)), 16)
            s_ld.inc(e.dma_start(iota8_sb[:], iota8_d), 16)
            s_ld.inc(e.dma_start(onesr_sb[:], onesr_d.bitcast(F32R)), 16)

        @blk.gpsimd
        def _(e):
            e.dma_start(wqkv_sb[:].rearrange("p (dc f) -> p dc f", dc=8),
                        wqkv.rearrange("(dc p) f -> p dc f", p=128)).then_inc(s_ldg.h, 16)
            s_ldg.n += 16
            e.dma_start(identb_sb[:], ident_d).then_inc(s_ldg.h, 16)
            s_ldg.n += 16
            for t in range(5):
                s_xl[t].inc(e.dma_start(xload[:, t * D:(t + 1) * D],
                                        x_in[t * 128:(t + 1) * 128, :]), 16)
            s_ldg.inc(e.dma_start(ropecq_sb[:], ropecq_d), 16)
            s_ldg.inc(e.dma_start(ropesq_sb[:], ropesq_d), 16)
            s_ldg.inc(e.dma_start(ropeck_sb[:], ropeck_d), 16)
            s_ldg.inc(e.dma_start(ropesk_sb[:], ropesk_d), 16)
            e.memset(runoff[:], 0.0)
            e.memset(eps_sb[:], 1.0e-5)
            e.memset(initrow[:, 0:1024], 0.0)
            g_ms.inc(e.memset(initrow[:, 1024:AUGW], 1.0e9))

    for gi, (t0, ntl) in enumerate(GRP):
        gw = ntl * 128
        gtok = t0 * 128
        nrounds = 2 * ntl
        tp0, ve0, ae0, qk0, tv0 = t_tp.n, v_e.n, a_e.n, t_qk.n, t_v.n
        swp_target = s_swp.n + 64

        with nc.Block() as blk:
            @blk.scalar
            def _(e, gi=gi, gw=gw, qk0=qk0, ae0=ae0):
                for fc in range(8):
                    e.wait_ge(t_qk.h, qk0 + fc + 1)
                    a_e.inc(e.copy(qkraw[:, fc * 512:fc * 512 + gw], bank(2 + (fc % 4), 0, gw)))
                e.wait_ge(a_e.h, ae0 + 8)
                for bi in range(4):
                    src_lo = [32, 0, 96, 64][bi]
                    s_swp.inc(e.dma_start(qswap[bi * 32:(bi + 1) * 32, 0:8 * 512],
                                          qkraw[src_lo:src_lo + 32, 0:8 * 512]), 16)

            @blk.tensor
            def _(e, gi=gi, t0=t0, ntl=ntl, gw=gw, nrounds=nrounds, tp0=tp0, ve0=ve0,
                  ae0=ae0):
                if gi == 0:
                    e.wait_ge(s_ld.h, 6 * 16)
                    e.wait_ge(s_ldg.h, 6 * 16)
                    e.wait_ge(g_ms.h, 1)
                for r in range(nrounds):
                    t = t0 + r // 2
                    half = r % 2
                    e.wait_ge(s_xl[t % 5].h, 16 * (t // 5 + 1))
                    if r >= 2:
                        e.wait_ge(v_e.h, ve0 + (r - 1))
                    for j in range(4):
                        dc = half * 4 + j
                        ii = e.transpose(bank(r % 2, j * 64, 64).bitcast(BF16),
                                         xload[:, (t % 5) * D + dc * 128:(t % 5) * D + (dc + 1) * 128],
                                         identb_sb[:])
                    t_tp.inc(ii)
                e.wait_ge(v_e.h, ve0 + nrounds)
                for fc in range(8):
                    bk = 2 + (fc % 4)
                    if fc >= 4:
                        e.wait_ge(a_e.h, ae0 + (fc - 4) + 1)
                    for dc in range(8):
                        ii = e.matmul(bank(bk, 0, gw),
                                      wqkv_sb[:, dc * 1536 + fc * 128:dc * 1536 + (fc + 1) * 128],
                                      xt_sb[:, dc * 512:dc * 512 + gw],
                                      start=(dc == 0), stop=(dc == 7))
                    t_qk.inc(ii)
                for k in range(ntl):
                    bk = 6 + (k % 2)
                    if k >= 2:
                        e.wait_ge(v_e.h, ve0 + nrounds + (k - 2) + 1)
                    for dc in range(8):
                        ii = e.matmul(bank(bk, 0, 512),
                                      xt_sb[:, dc * 512 + k * 128:dc * 512 + (k + 1) * 128],
                                      wqkv_sb[:, dc * 1536 + 1024:dc * 1536 + 1536],
                                      start=(dc == 0), stop=(dc == 7))
                    t_v.inc(ii)

            @blk.vector
            def _(e, t0=t0, ntl=ntl, gw=gw, gtok=gtok, nrounds=nrounds, tp0=tp0,
                  tv0=tv0, swp_target=swp_target):
                for r in range(nrounds):
                    half = r % 2
                    toff = (r // 2) * 128
                    e.wait_ge(t_tp.h, tp0 + r + 1)
                    dst = xt_sb[:].rearrange("p (dc t) -> p dc t", dc=8)[
                        :, 4 * half:4 * half + 4, toff:toff + 128]
                    v_e.inc(e.tensor_copy(dst, bank(r % 2, 0, 256).bitcast(BF16).rearrange("p (j t) -> p j t", j=4)))
                for k in range(ntl):
                    e.wait_ge(t_v.h, tv0 + k + 1)
                    v_e.inc(e.tensor_copy(v_sb[:, (t0 + k) * 512:(t0 + k + 1) * 512],
                                          bank(6 + (k % 2), 0, 512)))
                e.wait_ge(s_swp.h, swp_target)
                for c in range(8):
                    e.wait_ge(v_rope.h, v_rope.n)
                    src = qkraw[:, c * 512:c * 512 + gw]
                    swp = qswap[:, c * 512:c * 512 + gw]
                    cos = (ropecq_sb if c < 4 else ropeck_sb)[:, gtok:gtok + gw]
                    sin = (ropesq_sb if c < 4 else ropesk_sb)[:, gtok:gtok + gw]
                    cc = c % 4
                    dstt = qrot if c < 4 else krot
                    dst = dstt[:, cc * TOK_IN + gtok:cc * TOK_IN + gtok + gw]
                    e.tensor_tensor(rt1[:, 0:gw], src, cos, op=ALU.mult)
                    v_rope.inc(e.tensor_tensor(rt2[:, 0:gw], swp, sin, op=ALU.mult))
                    e.wait_ge(v_rope.h, v_rope.n)
                    v_rope.inc(e.tensor_tensor(dst, rt1[:, 0:gw], rt2[:, 0:gw], op=ALU.add))

            @blk.gpsimd
            def _(e, gi=gi):
                for t in range(4 * gi + 5, min(4 * gi + 9, NT)):
                    e.wait_ge(t_tp.h, 2 * (t - 5) + 2)
                    s_xl[t % 5].inc(e.dma_start(xload[:, (t % 5) * D:(t % 5 + 1) * D],
                                                x_in[t * 128:(t + 1) * 128, :]), 16)
                n = [5, 5, 5, 5, 4][gi]
                done = [0, 5, 10, 15, 20][gi]
                if gi == 0:
                    e.wait_ge(g_ms.h, 1)
                for k in range(n):
                    r0 = (done + k) * 128
                    s_init.inc(e.dma_start(xsort[r0:r0 + 128, :], initrow[:]), 16)

    es_qkv.close()

    masks_sb = sb(es_p2, "masks_sb", [128, 3 * 384], BF16)
    wout_sb = sb(es_p2, "wout_sb", [128, 8 * INNER], BF16)
    e_sb = sb(es_p2, "e_sb", [128, 2 * 1536], BF16)
    e2_sb = sb(es_p2, "e2_sb", [128, 2 * 1536], BF16)
    pt_sb = sb(es_p2, "pt_sb", [128, 2 * 1536], BF16)
    zacc_sb = sb(es_p2, "zacc_sb", [128, 8], F32)
    attnT = sb(es_p2, "attnT", [128, 512], BF16)
    invz_sb = sb(es_p2, "invz_sb", [128, 8], F32)
    xg_sb = sb(es_p2, "xg_sb", [128, 2 * D], F32)
    z1_sb = sb(es_p2, "z1_sb", [128, D], F32)
    x1aug = sb(es_p2, "x1aug", [128, 4 * AUGW], F32)
    x1T = sb(es_p2, "x1T", [128, 8 * 512], F32)
    scev = sb(es_p2, "scev", [8, 512], F32)

    with nc.Block() as blk:
        @blk.scalar
        def _(e):
            for q in range(2):
                s_xg[q].inc(e.dma_start(xg_sb[:, q * D:(q + 1) * D],
                                        x_in[(q + 2) * 128:(q + 3) * 128, :]), 16)

        @blk.gpsimd
        def _(e):
            s_ldg.inc(e.dma_start(masks_sb[:].rearrange("p (c k) -> p c k", c=3),
                                 masks_d.rearrange("c p k -> p c k")), 16)
            s_ldg.inc(e.dma_start(wout_sb[:].rearrange("p (c f) -> p c f", c=4),
                                  wout.rearrange("(c p) f -> p c f", p=128)), 16)
            for j in range(4):
                ii = e.memset(x1aug[:, j * AUGW + 1024:j * AUGW + 1028], 0.0)
            g_ms.inc(ii)
            for i in range(2):
                wq_load(e, i)

    def cls_of(t):
        return 1 if t == 2 else (2 if t == 3 else 0)

    def p2_S(e, q):
        t = q + 2
        cls = cls_of(t)
        if q == 0:
            e.wait_ge(s_ld.h, 6 * 16)
            e.wait_ge(s_ldg.h, 8 * 16)
        for w in range(2):
            if w == 1:
                e.wait_ge(a_ex.h, 8 * q + 4)
            elif q >= 1:
                e.wait_ge(a_ex.h, 8 * q)
                e.wait_ge(a_zb.h, 2 * q)
            for j in range(4):
                h = 4 * w + j
                pb = (h % 2) * 64
                ch = h // 2
                e.matmul(bank(j, 0, 384),
                         qrot[pb:pb + 64, ch * TOK_IN + t * 128:ch * TOK_IN + (t + 1) * 128],
                         krot[pb:pb + 64, ch * TOK_IN + (t - 2) * 128:ch * TOK_IN + (t + 1) * 128],
                         start=True, stop=False)
            for j in range(4):
                ii = e.matmul(bank(j, 0, 384), identb_sb[:],
                              masks_sb[:, cls * 384:(cls + 1) * 384],
                              start=False, stop=True)
            t_s.inc(ii)

    def p2_PTPV(e, q):
        t = q + 2
        P = q % 2
        for w in range(2):
            if q >= 2:
                e.wait_ge(a_xe.h, 2 * q - 2)
            for j in range(4):
                e.wait_ge(v_me.h, 8 * q + 4 * w + j + 1)
                for b in range(3):
                    c0b = (j % 2) * 384 + b * 128
                    ii = e.transpose(
                        bank(4 + 2 * P + j // 2, c0b // 2, 64).bitcast(BF16),
                        e2_sb[:, w * 1536 + j * 384 + b * 128:w * 1536 + j * 384 + (b + 1) * 128],
                        identb_sb[:])
                t_mt.inc(ii)
            e.wait_ge(a_ex.h, 8 * q + 8)
            e.wait_ge(v_at.h, 2 * q + w + 1)
            if w == 1:
                e.wait_ge(a_zb.h, 2 * q + 1)
            for j in range(4):
                h = 4 * w + j
                p0 = (j % 2) * 64
                for b in range(3):
                    ii = e.matmul(bank(j // 2, 384, 128, p0=p0, p1=p0 + 64),
                                  v_sb[:, (t - 2 + b) * 512 + h * 64:(t - 2 + b) * 512 + (h + 1) * 64],
                                  pt_sb[:, w * 1536 + j * 384 + b * 128:w * 1536 + j * 384 + (b + 1) * 128],
                                  start=(b == 0), stop=(b == 2))
            t_pv.inc(ii)

    def p2_Wout(e, q):
        P = q % 2
        e.wait_ge(a_zb.h, 2 * q + 2)
        for dh in range(2):
            for c in range(4):
                ii = e.matmul(bank(4 + 2 * P + dh, 0, 512),
                              attnT[:, c * 128:(c + 1) * 128],
                              wout_sb[:, c * 1024 + dh * 512:c * 1024 + (dh + 1) * 512],
                              start=(c == 0), stop=(c == 3))
            t_wo.inc(ii)

    def p2_x1T(e, q):
        P = q % 2
        e.wait_ge(v_x1.h, q + 1)
        for half in range(2):
            for j in range(4):
                dc = 4 * half + j
                ii = e.transpose(bank(4 + 2 * P + half, j * 128, 128),
                                 x1aug[:, (q % 4) * AUGW + dc * 128:(q % 4) * AUGW + (dc + 1) * 128],
                                 ident_sb[:])
            t_x1t.inc(ii)

    def p2_softmax_v(e, q):
        P = q % 2
        for w in range(2):
            e.wait_ge(a_ex.h, 8 * q + 4 * (w + 1))
            if q >= 1 and w == 0:
                e.wait_ge(v_me.h, 8 * q)
            v_rz.inc(e.reciprocal(invz_sb[:, 4 * w:4 * w + 4],
                                  zacc_sb[:, 4 * w:4 * w + 4]))
            e.wait_ge(v_rz.h, 2 * q + w + 1)
            for j in range(4):
                v_me.inc(e.tensor_scalar(
                    e2_sb[:, w * 1536 + j * 384:w * 1536 + (j + 1) * 384],
                    e_sb[:, w * 1536 + j * 384:w * 1536 + (j + 1) * 384],
                    invz_sb[:, 4 * w + j:4 * w + j + 1], None, op0=ALU.mult))
    def p2_evac_v(e, q):
        P = q % 2
        for w in range(2):
            e.wait_ge(t_mt.h, 8 * q + 4 * (w + 1))
            if w == 0:
                e.wait_ge(t_pv.h, 2 * q)
            srcv = psum[:, (4 + 2 * P) * 512:(6 + 2 * P) * 512].bitcast(BF16)\
                .rearrange("p (bk c) -> p bk c", bk=2)[:, :, 0:768]
            dstv = pt_sb[:, w * 1536:(w + 1) * 1536].rearrange("p (bk c) -> p bk c", bk=2)
            v_at.inc(e.tensor_copy(dstv, srcv))
        for w in range(2):
            e.wait_ge(t_pv.h, 2 * q + w + 1)
            if w == 0 and q >= 1:
                e.wait_ge(t_wo.h, 2 * q)
            srct = psum[:, 0:1024].rearrange("p (j c) -> p j c", j=2)[:, :, 384:512]
            a_zb.inc(e.tensor_copy(
                attnT[:, w * 256:(w + 1) * 256].rearrange("p (j c) -> p j c", j=2), srct))

    def p2_ln1_v(e, q):
        P = q % 2
        for dh in range(2):
            e.wait_ge(t_wo.h, 2 * q + dh + 1)
            e.wait_ge(s_xg[q % 2].h, 16 * (q // 2 + 1))
            if dh == 0 and q >= 1:
                e.wait_ge(v_x1.h, q)
            v_z1.inc(e.tensor_tensor(z1_sb[:, dh * 512:(dh + 1) * 512],
                                     xg_sb[:, (q % 2) * D + dh * 512:(q % 2) * D + (dh + 1) * 512],
                                     bank(4 + 2 * P + dh, 0, 512), op=ALU.add))
        e.wait_ge(v_z1.h, 4 * q + 2)
        e.bn_stats(bnst[:, 0:6], z1_sb[:, 0:512])
        v_z1.inc(e.bn_stats(bnst[:, 6:12], z1_sb[:, 512:1024]))
        e.wait_ge(v_z1.h, 4 * q + 3)
        if q >= 1:
            e.wait_ge(a_rs.h, 2 * q)
        v_z1.inc(e.bn_aggr(bnag[:], bnst[:]))
        e.wait_ge(a_rs.h, 2 * (q + 1))
        if q >= 4:
            e.wait_ge(s_scat.h, 16 * 4 * (q // 4))
        v_x1.inc(e.tensor_scalar(x1aug[:, (q % 4) * AUGW:(q % 4) * AUGW + 1024],
                                 z1_sb[:], bnag[:, 0:1], rstd_sb[:],
                                 op0=ALU.subtract, op1=ALU.mult))

    def p2_exps_a(e, q):
        for w in range(2):
            e.wait_ge(t_s.h, 2 * q + w + 1)
            if w == 0 and q >= 1:
                e.wait_ge(v_me.h, 8 * q)
            for j in range(4):
                a_ex.inc(e.activation(
                    e_sb[:, w * 1536 + j * 384:w * 1536 + (j + 1) * 384],
                    bank(j, 0, 384), AF.Exp,
                    accum_out=zacc_sb[:, 4 * w + j:4 * w + j + 1]))

    def p2_tail_a(e, q):
        e.wait_ge(v_z1.h, 4 * q + 4)
        a_rs.inc(e.activation(lnv_sb[:], bnag[:, 1:2], AF.Ln, bias=eps_sb[:]))
        e.wait_ge(a_rs.h, a_rs.n)
        if q >= 1:
            e.wait_ge(v_x1.h, q)
        a_rs.inc(e.activation(rstd_sb[:], lnv_sb[:], AF.Exp, scale=-0.5))

    def p2_x1e_a(e, q):
        P = q % 2
        for half in range(2):
            e.wait_ge(t_x1t.h, 2 * q + half + 1)
            dst = x1T[:].rearrange("p (dc x) -> p dc x", dc=8)[
                :, 4 * half:4 * half + 4, (q % 4) * 128:(q % 4) * 128 + 128]
            a_xe.inc(e.copy(dst, bank(4 + 2 * P + half, 0, 512).rearrange("p (j x) -> p j x", j=4)))
        if q + 2 < NQT:
            s_xg[(q + 2) % 2].inc(
                e.dma_start(xg_sb[:, ((q + 2) % 2) * D:((q + 2) % 2 + 1) * D],
                            x_in[(q + 4) * 128:(q + 5) * 128, :]), 16)

    for g in range(4):
        g0 = 4 * g
        with nc.Block() as blk:
            @blk.tensor
            def _(e, g0=g0):
                for i in range(4):
                    q = g0 + i
                    if i >= 1:
                        p2_PTPV(e, q - 1)
                    p2_S(e, q)
                    if i >= 1:
                        p2_Wout(e, q - 1)
                    if i >= 2:
                        p2_x1T(e, q - 2)
                p2_PTPV(e, g0 + 3)
                p2_Wout(e, g0 + 3)
                p2_x1T(e, g0 + 2)
                p2_x1T(e, g0 + 3)

            @blk.vector
            def _(e, g0=g0):
                for i in range(4):
                    q = g0 + i
                    p2_softmax_v(e, q)
                    if i >= 1:
                        p2_ln1_v(e, q - 1)
                    p2_evac_v(e, q)
                p2_ln1_v(e, g0 + 3)

            @blk.scalar
            def _(e, g0=g0):
                for i in range(4):
                    q = g0 + i
                    p2_exps_a(e, q)
                    if i >= 1:
                        p2_tail_a(e, q - 1)
                    if i >= 2:
                        p2_x1e_a(e, q - 2)
                p2_tail_a(e, g0 + 3)
                p2_x1e_a(e, g0 + 2)
                p2_x1e_a(e, g0 + 3)

            @blk.gpsimd
            def _(e, g0=g0):
                for i in range(4):
                    q = g0 + i
                    if q >= 4:
                        e.wait_ge(s_scat.h, 16 * 4 * (q // 4))
                    e.iota(x1aug[:, (q % 4) * AUGW + 1024:(q % 4) * AUGW + 1025], [[1, 1]],
                           base=q * 128, channel_multiplier=1,
                           allow_small_or_imprecise_dtypes=True)
                    g_io.inc(e.iota(tokidf[:, q % 4:q % 4 + 1], [[1, 1]], base=q * 128,
                                    channel_multiplier=1,
                                    allow_small_or_imprecise_dtypes=True))

        with nc.Block() as blk:
            @blk.tensor
            def _(e, g=g):
                e.wait_ge(a_xe.h, 8 * (g + 1) * 1)
                for dc in range(8):
                    ii = e.matmul(bank(0, 0, 512, p0=0, p1=8),
                                  wr_sb[:, dc * 8:(dc + 1) * 8],
                                  x1T[:, dc * 512:(dc + 1) * 512],
                                  start=(dc == 0), stop=(dc == 7))
                t_sc.inc(ii)
                e.wait_ge(a_sc.h, g + 1)
                for k in range(4):
                    ii = e.transpose(bank(1, k * 128, 8, p0=0, p1=128),
                                     scev[0:8, k * 128:(k + 1) * 128], ident_sb[0:8, 0:8])
                t_sct.inc(ii)
                for k in range(4):
                    qq = 4 * g + k
                    e.wait_ge(v_oh.h, 4 * qq + 4)
                    e.wait_ge(v_ro.h, qq + 1)
                    if k > 0:
                        e.wait_ge(v_sl.h, 7 * qq - 5)
                    e.matmul(bank(2, k * 32, 8, p0=0, p1=128), lstrict_sb[:],
                             oh_sb[:], start=True, stop=False)
                    e.matmul(bank(2, k * 32, 8, p0=0, p1=128), onesr_sb[0:1, 0:128],
                             rowoff[:], start=False, stop=True)
                    t_rk.inc(e.matmul(bank(2, k * 32 + 16, 8, p0=0, p1=1),
                                      onesr_sb[:, 0:1], oh_sb[:],
                                      start=True, stop=True))

            @blk.scalar
            def _(e, g=g):
                e.wait_ge(t_sc.h, g + 1)
                a_sc.inc(e.activation(scev[:], bank(0, 0, 512, p0=0, p1=8),
                                      AF.Identity, bias=br_sb[:]))

            @blk.vector
            def _(e, g=g):
                e.wait_ge(t_sct.h, g + 1)
                for k in range(4):
                    v_tm.inc(e.tensor_copy(stm[:, k * 8:(k + 1) * 8],
                                           bank(1, k * 128, 8, p0=0, p1=128)))
                for k in range(4):
                    qq = 4 * g + k
                    e.wait_ge(v_tm.h, 4 * g + 4)
                    if qq > 0:
                        e.wait_ge(t_rk.h, qq)
                    v_oh.inc(e.max(mx8[:], stm[:, k * 8:(k + 1) * 8]))
                    e.wait_ge(v_oh.h, 4 * qq + 1)
                    v_oh.inc(e.max_index(idx8[:], mx8[:], stm[:, k * 8:(k + 1) * 8]))
                    e.wait_ge(v_oh.h, 4 * qq + 2)
                    v_oh.inc(e.tensor_copy(eidf[:], idx8[:, 0:1]))
                    e.wait_ge(v_oh.h, 4 * qq + 3)
                    v_oh.inc(e.tensor_scalar(oh_sb[:], iota8_sb[:], eidf[:], None,
                                             op0=ALU.is_equal))
                    if qq > 0:
                        e.wait_ge(v_sl.h, 7 * qq - 5)
                    v_ro.inc(e.tensor_copy(rowoff[:], runoff[:, (qq % 2) * 8:(qq % 2) * 8 + 8]))
                    e.wait_ge(t_rk.h, qq + 1)
                    v_sl.inc(e.tensor_tensor(scr8[:], oh_sb[:].bitcast(F32),
                                             bank(2, k * 32, 8, p0=0, p1=128),
                                             op=ALU.mult))
                    e.wait_ge(v_sl.h, 7 * qq + 1)
                    e.reduce_sum(ranksel[:], scr8[:], axis=AX.X)
                    v_sl.inc(e.tensor_tensor(runoff[:, ((qq + 1) % 2) * 8:((qq + 1) % 2) * 8 + 8],
                                             runoff[:, (qq % 2) * 8:(qq % 2) * 8 + 8],
                                             bank(2, k * 32 + 16, 8, p0=0, p1=1),
                                             op=ALU.add))
                    e.wait_ge(v_sl.h, 7 * qq + 2)
                    e.tensor_scalar(ovf_sb[:], ranksel[:], float(CAP), None, op0=ALU.is_ge)
                    v_sl.inc(e.tensor_scalar(scr1[:], eidf[:], float(CAP), None, op0=ALU.mult))
                    e.wait_ge(v_sl.h, 7 * qq + 3)
                    e.wait_ge(g_io.h, qq + 1)
                    e.tensor_tensor(slotf[:], ranksel[:], scr1[:], op=ALU.add)
                    v_sl.inc(e.tensor_scalar(scr2[:], tokidf[:, qq % 4:qq % 4 + 1],
                                             float(TRASH), None, op0=ALU.add))
                    e.wait_ge(v_sl.h, 7 * qq + 4)
                    v_sl.inc(e.tensor_tensor(scr3[:], scr2[:], slotf[:], op=ALU.subtract))
                    e.wait_ge(v_sl.h, 7 * qq + 5)
                    v_sl.inc(e.tensor_scalar(slotf2[:], scr3[:], ovf_sb[:], slotf[:],
                                             op0=ALU.mult, op1=ALU.add))
                    e.wait_ge(v_sl.h, 7 * qq + 6)
                    if g >= 2:
                        e.wait_ge(s_scat.h, 16 * 4 * (g - 1))
                    v_sl.inc(e.tensor_copy(slotu[:, (g % 2) * 4 + k:(g % 2) * 4 + k + 1],
                                           slotf2[:]))

            @blk.gpsimd
            def _(e, g=g):
                for k in range(4):
                    qq = 4 * g + k
                    e.wait_ge(v_sl.h, 7 * qq + 7)
                    e.wait_ge(v_x1.h, qq + 1)
                    if qq == 0:
                        e.wait_ge(s_init.h, 24 * 16)
                        e.wait_ge(g_ms.h, 2)
                    s_scat.inc(e.indirect_dma_start(
                        xsort[:, :],
                        bass.IndirectOffsetOnAxis(ap=slotu[:, (g % 2) * 4 + k:(g % 2) * 4 + k + 1], axis=0),
                        x1aug[:, (qq % 4) * AUGW:(qq % 4) * AUGW + AUGW], None), 16)

    es_p2.close()
    es_attn.close()

    wstream23 = sb(es_ffn, "wstream23", [128, 2 * 8192], BF16)
    b1_sb = sb(es_ffn, "b1_sb", [128, NE * 32], F32)
    b2_sb = sb(es_ffn, "b2_sb", [128, NE * 8], F32)
    x_sel = sb(es_ffn, "x_sel", [128, 2 * NRT * AUGW], F32)
    xselT = sb(es_ffn, "xselT", [128, 8 * TOKW], BF16)
    hT = sb(es_ffn, "hT", [128, 32 * TOKW], BF16)
    yT = sb(es_ffn, "yT", [128, 8 * TOKW], BF16)
    z_sb = sb(es_ffn, "z_sb", [128, AUGW], F32)
    zn_sb = sb(es_ffn, "zn_sb", [128, 2 * 1024], F32)
    bnst2 = sb(es_ffn, "bnst2", [128, 12], F32)
    bnag2 = sb(es_ffn, "bnag2", [128, 4], F32)
    rstd2 = sb(es_ffn, "rstd2", [128, 2], F32)
    slotO = sb(es_ffn, "slotO", [128, 2], U32)
    ident2_sb = sb(es_ffn, "ident2_sb", [128, 128], F32)
    identb2_sb = sb(es_ffn, "identb2_sb", [128, 128], BF16)
    eps3_sb = sb(es_ffn, "eps3_sb", [128, 1], F32)
    rx_sb = sb(es_ffn, "rx_sb", [128, 1], F32)
    rf_sb = sb(es_ffn, "rf_sb", [128, 1], F32)
    ru_sb = sb(es_ffn, "ru_sb", [128, 1], F32)
    ry_sb = sb(es_ffn, "ry_sb", [128, 1], F32)
    rt1_sb = sb(es_ffn, "rt1_sb", [128, 1], F32)

    XS4 = NRT * AUGW

    with nc.Block() as blk:
        @blk.sync
        def _(e):
            e.wait_ge(s_scat.h, 16 * 16)
            s_b12.inc(e.dma_start(b1_sb[:], b1_d), 16)
            s_b12.inc(e.dma_start(b2_sb[:], b2_d), 16)
            s_ld.inc(e.dma_start(ident2_sb[:], ident_d), 16)

        @blk.scalar
        def _(e):
            e.wait_ge(s_scat.h, 16 * 16)
            for ee in range(2):
                for rt in range(NRT):
                    s_g[ee % 2].inc(e.dma_start(
                        x_sel[:, ee * XS4 + rt * AUGW:ee * XS4 + (rt + 1) * AUGW],
                        xsort[ee * CAP + rt * 128:ee * CAP + (rt + 1) * 128, :]), 16)

        @blk.gpsimd
        def _(e):
            e.wait_ge(s_scat.h, 16 * 16)
            s_ldg.inc(e.dma_start(identb2_sb[:], ident_d), 16)
            for i in range(2, 4):
                wq_load(e, i)
            g_ms.inc(e.memset(eps3_sb[:], 1.0e-5))

    def p3_gelu_a(e, ee, fcg):
        e.wait_ge(t_f1.h, 32 * ee + fcg + 1)
        if fcg == 0:
            if ee == 0:
                e.wait_ge(s_b12.h, 32)
                e.wait_ge(g_ms.h, 3)
            else:
                e.wait_ge(t_f2.h, 4 * ee)
        a_g.inc(e.activation(hT[:, fcg * TOKW:(fcg + 1) * TOKW],
                             bank(2 + (fcg % 2), 0, TOKW), AF.Gelu,
                             bias=b1_sb[:, ee * 32 + fcg:ee * 32 + fcg + 1]))

    with nc.Block() as blk:
        @blk.scalar
        def _(e):
            for fcg in range(32):
                p3_gelu_a(e, 0, fcg)
            for ee in range(NE):
                for dc in range(8):
                    e.wait_ge(t_f2.h, 4 * (ee + 1))
                    if dc == 0:
                        e.wait_ge(t_yt.h, NRT * ee)
                    a_y.inc(e.activation(yT[:, dc * TOKW:(dc + 1) * TOKW],
                                         bank(dc, 0, TOKW), AF.Identity,
                                         bias=b2_sb[:, ee * 8 + dc:ee * 8 + dc + 1]))
                if ee + 1 < NE:
                    for fcg in range(32):
                        p3_gelu_a(e, ee + 1, fcg)
                if ee + 2 < NE:
                    e.wait_ge(v_ln.h, NRT * (ee + 1))
                    par2 = (ee + 2) % 2
                    for rt in range(NRT):
                        s_g[par2].inc(e.dma_start(
                            x_sel[:, par2 * XS4 + rt * AUGW:par2 * XS4 + (rt + 1) * AUGW],
                            xsort[(ee + 2) * CAP + rt * 128:(ee + 2) * CAP + (rt + 1) * 128, :]), 16)

        @blk.tensor
        def _(e):
            for ee in range(NE):
                par = ee % 2
                if ee == 0:
                    e.wait_ge(s_ld.h, 7 * 16)
                    e.wait_ge(s_ldg.h, 9 * 16)
                e.wait_ge(s_g[ee % 2].h, 16 * NRT * (ee // 2 + 1))
                if ee >= 1:
                    e.wait_ge(a_y.h, 8 * ee - 4)
                for rt in range(NRT):
                    if rt >= 2:
                        e.wait_ge(v_e3.h, 2 * NRT * ee + 2 * (rt - 1))
                    for dc in range(8):
                        ii = e.transpose(bank(2 * (rt % 2) + dc // 4, (dc % 4) * 128, 128),
                                         x_sel[:, par * XS4 + rt * AUGW + dc * 128:par * XS4 + rt * AUGW + (dc + 1) * 128],
                                         ident2_sb[:])
                    t_tp3.inc(ii)
                e.wait_ge(v_e3.h, 2 * NRT * (ee + 1))
                for j in range(4):
                    i = 8 * ee + j
                    e.wait_ge(s_w[i % 4].h, 16 * (i // 4 + 1))
                    for fc in range(8):
                        fcg = 8 * j + fc
                        if fcg >= 2:
                            e.wait_ge(a_g.h, 32 * ee + fcg - 1)
                        for dc in range(8):
                            ii = e.matmul(bank(2 + (fcg % 2), 0, TOKW),
                                          wslot(i)[:, dc * 1024 + fc * 128:
                                                   dc * 1024 + (fc + 1) * 128],
                                          xselT[:, dc * TOKW:(dc + 1) * TOKW],
                                          start=(dc == 0), stop=(dc == 7))
                        t_f1.inc(ii)
                e.wait_ge(a_g.h, 32 * (ee + 1))
                if ee >= 1:
                    e.wait_ge(v_zz.h, 15 * (NRT * ee - 1) + 1)
                for j in range(4):
                    i = 8 * ee + 4 + j
                    e.wait_ge(s_w[i % 4].h, 16 * (i // 4 + 1))
                    for fc in range(8):
                        fcg = 8 * j + fc
                        for dc in range(8):
                            ii = e.matmul(bank(dc, 0, TOKW),
                                          wslot(i)[:, fc * 1024 + dc * 128:
                                                   fc * 1024 + (dc + 1) * 128],
                                          hT[:, fcg * TOKW:(fcg + 1) * TOKW],
                                          start=(fcg == 0), stop=(fcg == 31))
                    t_f2.inc(ii)
                e.wait_ge(a_y.h, 8 * (ee + 1))
                for rt in range(NRT):
                    if rt >= 2:
                        e.wait_ge(v_zz.h, 15 * (NRT * ee + rt - 2) + 1)
                    for dc in range(8):
                        ii = e.transpose(bank(4 + 2 * (rt % 2) + dc // 4, (dc % 4) * 64, 64).bitcast(BF16),
                                         yT[:, dc * TOKW + rt * 128:dc * TOKW + (rt + 1) * 128],
                                         identb2_sb[:])
                    t_yt.inc(ii)

        @blk.vector
        def _(e):
            for rt in range(NRT):
                e.wait_ge(t_tp3.h, rt + 1)
                for half in range(2):
                    dst = xselT[:].rearrange("p (dc x) -> p dc x", dc=8)[
                        :, 4 * half:4 * half + 4, rt * 128:(rt + 1) * 128]
                    v_e3.inc(e.tensor_copy(dst, bank(2 * (rt % 2) + half, 0, 512).rearrange("p (j x) -> p j x", j=4)))
            for ee in range(NE):
                par = ee % 2
                for rt in range(NRT):
                    R = NRT * ee + rt
                    Z0 = 15 * R
                    e.wait_ge(t_yt.h, NRT * ee + rt + 1)
                    if R >= 1:
                        e.wait_ge(v_ln.h, R)
                    for dh in range(2):
                        e.tensor_tensor(z_sb[:, dh * 512:(dh + 1) * 512],
                                        x_sel[:, par * XS4 + rt * AUGW + dh * 512:par * XS4 + rt * AUGW + (dh + 1) * 512],
                                        bank(4 + 2 * (rt % 2) + dh, 0, 256).bitcast(BF16), op=ALU.add)
                    v_zz.inc(e.tensor_copy(z_sb[:, 1024:1025],
                                           x_sel[:, par * XS4 + rt * AUGW + 1024:par * XS4 + rt * AUGW + 1025]))
                    e.wait_ge(v_zz.h, Z0 + 1)
                    e.bn_stats(bnst2[:, 0:6], z_sb[:, 0:512])
                    v_zz.inc(e.bn_stats(bnst2[:, 6:12], z_sb[:, 512:1024]))
                    e.wait_ge(v_zz.h, Z0 + 2)
                    v_zz.inc(e.bn_aggr(bnag2[:, 0:2], bnst2[:]))
                    e.wait_ge(v_zz.h, Z0 + 3)
                    v_zz.inc(e.tensor_scalar(rx_sb[:], bnag2[:, 1:2], 1.0e-5, None, op0=ALU.add))
                    e.wait_ge(v_zz.h, Z0 + 4)
                    v_zz.inc(e.tensor_copy(rf_sb[:], rx_sb[:].bitcast(U32)))
                    e.wait_ge(v_zz.h, Z0 + 5)
                    v_zz.inc(e.tensor_scalar(ru_sb[:], rf_sb[:], -0.5, 1597463007.0,
                                             op0=ALU.mult, op1=ALU.add))
                    e.wait_ge(v_zz.h, Z0 + 6)
                    v_zz.inc(e.tensor_copy(ry_sb[:].bitcast(U32), ru_sb[:]))
                    for _ in range(2):
                        e.wait_ge(v_zz.h, v_zz.n)
                        v_zz.inc(e.tensor_tensor(rt1_sb[:], ry_sb[:], ry_sb[:], op=ALU.mult))
                        e.wait_ge(v_zz.h, v_zz.n)
                        v_zz.inc(e.tensor_tensor(rt1_sb[:], rt1_sb[:], rx_sb[:], op=ALU.mult))
                        e.wait_ge(v_zz.h, v_zz.n)
                        v_zz.inc(e.tensor_scalar(rt1_sb[:], rt1_sb[:], -0.5, 1.5,
                                                 op0=ALU.mult, op1=ALU.add))
                        e.wait_ge(v_zz.h, v_zz.n)
                        v_zz.inc(e.tensor_tensor(ry_sb[:], ry_sb[:], rt1_sb[:], op=ALU.mult))
                    e.wait_ge(v_zz.h, Z0 + 15)
                    if R >= 2:
                        e.wait_ge(s_outd[R % 2].h, 16 * (R // 2))
                    e.tensor_scalar(zn_sb[:, (R % 2) * 1024:(R % 2) * 1024 + 1024],
                                    z_sb[:, 0:1024], bnag2[:, 0:1], ry_sb[:],
                                    op0=ALU.subtract, op1=ALU.mult)
                    v_ln.inc(e.tensor_copy(slotO[:, R % 2:R % 2 + 1], z_sb[:, 1024:1025]))
                if ee + 1 < NE:
                    for rt in range(NRT):
                        e.wait_ge(t_tp3.h, NRT * (ee + 1) + rt + 1)
                        for half in range(2):
                            dst = xselT[:].rearrange("p (dc x) -> p dc x", dc=8)[
                                :, 4 * half:4 * half + 4, rt * 128:(rt + 1) * 128]
                            v_e3.inc(e.tensor_copy(dst, bank(2 * (rt % 2) + half, 0, 512).rearrange("p (j x) -> p j x", j=4)))

        @blk.gpsimd
        def _(e):
            for ee in range(NE):
                for i in range(8 * ee + 4, min(8 * ee + 12, 64)):
                    ip = i - 4
                    eep, jp = ip // 8, ip % 8
                    if jp < 4:
                        e.wait_ge(t_f1.h, 32 * eep + 8 * (jp + 1))
                    else:
                        e.wait_ge(t_f2.h, 4 * eep + (jp - 4) + 1)
                    wq_load(e, i)
                for rt in range(NRT):
                    R = NRT * ee + rt
                    e.wait_ge(v_ln.h, NRT * ee + rt + 1)
                    s_outd[R % 2].inc(e.indirect_dma_start(
                        out_d[:, :],
                        bass.IndirectOffsetOnAxis(ap=slotO[:, R % 2:R % 2 + 1], axis=0),
                        zn_sb[:, (R % 2) * 1024:(R % 2) * 1024 + 1024], None,
                        bounds_check=TOK_OWN - 1, oob_is_err=False), 16)

    with nc.Block() as blk:
        @blk.gpsimd
        def _(e):
            e.wait_ge(s_outd[0].h, 16 * 12)
            e.wait_ge(s_outd[1].h, 16 * 12)

    es_ffn.close()
    es_top.close()
    return nc


def _host_consts(core):
    b, s = core // 2, core % 2
    r = np.arange(128)[:, None]
    m = np.arange(128)[None, :]
    blkA = np.where(m > r, 0.0, NEG).astype(np.float32)
    blkB = np.zeros((128, 128), np.float32)
    blkC = np.where(m <= r, 0.0, NEG).astype(np.float32)
    negb = np.full((128, 128), NEG, np.float32)
    generic = np.concatenate([blkA, blkB, blkC], axis=1)
    if s == 0:
        edge0 = np.concatenate([negb, negb, blkC], axis=1)
        edge1 = np.concatenate([negb, blkB, blkC], axis=1)
    else:
        edge0 = generic
        edge1 = generic
    masks = np.stack([generic, edge0, edge1])

    pos = (s * 2048 - 256 + np.arange(TOK_IN)).astype(np.float32)
    inv = (1.0 / 10000.0 ** (np.arange(0, 64, 2) / 64.0)).astype(np.float32)
    dmod = np.arange(128) % 64
    fidx = dmod % 32
    ang = pos[None, :] * inv[fidx][:, None]
    ropec = np.cos(ang).astype(np.float32)
    sign = np.where(dmod < 32, -1.0, 1.0).astype(np.float32)
    ropes = (sign[:, None] * np.sin(ang)).astype(np.float32)
    return masks, ropec * 0.125, ropes * 0.125, ropec, ropes


def kernel(L0_attn_x, L0_attn_Wqkv, L0_attn_Wout, L0_ffn_wr, L0_ffn_br,
           L0_ffn_w1, L0_ffn_b1, L0_ffn_w2, L0_ffn_b2):
    x = np.ascontiguousarray(L0_attn_x, dtype=np.float32)
    ident = np.eye(128, dtype=np.float32)
    lstrict = np.triu(np.ones((128, 128), np.float32), 1)
    iota8 = np.broadcast_to(np.arange(NE, dtype=np.float32), (128, NE)).copy()
    onesr = np.ones((128, 128), np.float32)
    shared = dict(
        wqkv=np.ascontiguousarray(L0_attn_Wqkv, np.float32),
        wout=np.ascontiguousarray(L0_attn_Wout, np.float32),
        wr=np.ascontiguousarray(L0_ffn_wr, np.float32),
        br=np.ascontiguousarray(L0_ffn_br, np.float32).reshape(NE, 1),
        w1=np.ascontiguousarray(L0_ffn_w1, np.float32),
        b1=np.ascontiguousarray(np.asarray(L0_ffn_b1, np.float32).reshape(NE, 32, 128).transpose(2, 0, 1).reshape(128, NE * 32)),
        w2=np.ascontiguousarray(L0_ffn_w2, np.float32),
        b2=np.ascontiguousarray(np.asarray(L0_ffn_b2, np.float32).reshape(NE, 8, 128).transpose(2, 0, 1).reshape(128, NE * 8)),
        ident=ident, lstrict=lstrict, iota8=iota8, onesr=onesr,
    )
    in_maps = []
    for c in range(8):
        b, s = c // 2, c % 2
        own = x[b, s * 2048:(s + 1) * 2048]
        halo = np.zeros((256, D), np.float32) if s == 0 else x[b, 1792:2048]
        x_c = np.concatenate([halo, own], axis=0)
        masks, rcq, rsq, rck, rsk = _host_consts(c)
        m = dict(shared)
        m.update(x_in=np.ascontiguousarray(x_c), masks=masks,
                 ropecq=rcq, ropesq=rsq, ropeck=rck, ropesk=rsk)
        in_maps.append(m)

    nc = _build_nc()
    res = run_bass_kernel_spmd(nc, in_maps, core_ids=list(range(8)))
    out = np.empty((4, 4096, D), np.float32)
    for c in range(8):
        out[c // 2, (c % 2) * 2048:(c % 2 + 1) * 2048] = res.results[c]["out"]
    return out


# version 18
# speedup vs baseline: 1.1322x; 1.1322x over previous
import numpy as np
import concourse.bass as bass
import concourse.mybir as mybir
from concourse.bass_utils import run_bass_kernel_spmd
from contextlib import ExitStack

F32 = mybir.dt.float32
F32R = mybir.dt.float32r
BF16 = mybir.dt.bfloat16
U32 = mybir.dt.uint32
AF = mybir.ActivationFunctionType
ALU = mybir.AluOpType
AX = mybir.AxisListType

D = 1024
INNER = 512
NT = 18
NQT = 16
TOK_IN = NT * 128
TOK_OWN = 2048
NE = 8
CAP = 384
TOKW = CAP
NRT = CAP // 128
FF = 4096
SORT_ROWS = NE * CAP
TRASH = SORT_ROWS
XS_ROWS = SORT_ROWS + TOK_OWN
AUGW = 1028
NEG = -1.0e30
GRP = [(0, 4), (4, 4), (8, 4), (12, 4), (16, 2)]


class _Sem:
    __slots__ = ("h", "n")

    def __init__(self, h):
        self.h = h
        self.n = 0

    def inc(self, instr, k=1):
        instr.then_inc(self.h, k)
        self.n += k
        return instr


def _build_nc():
    nc = bass.Bass("TRN2", target_bir_lowering=False)

    x_in = nc.dram_tensor("x_in", [TOK_IN, D], F32, kind="ExternalInput").ap()
    wqkv = nc.dram_tensor("wqkv", [D, 3 * INNER], F32, kind="ExternalInput").ap()
    wout = nc.dram_tensor("wout", [INNER, D], F32, kind="ExternalInput").ap()
    wr_d = nc.dram_tensor("wr", [D, NE], F32, kind="ExternalInput").ap()
    br_d = nc.dram_tensor("br", [NE, 1], F32, kind="ExternalInput").ap()
    w1_d = nc.dram_tensor("w1", [NE, D, FF], F32, kind="ExternalInput").ap()
    b1_d = nc.dram_tensor("b1", [128, NE * 32], F32, kind="ExternalInput").ap()
    w2_d = nc.dram_tensor("w2", [NE, FF, D], F32, kind="ExternalInput").ap()
    b2_d = nc.dram_tensor("b2", [128, NE * 8], F32, kind="ExternalInput").ap()
    ident_d = nc.dram_tensor("ident", [128, 128], F32, kind="ExternalInput").ap()
    lstrict_d = nc.dram_tensor("lstrict", [128, 128], F32, kind="ExternalInput").ap()
    masks_d = nc.dram_tensor("masks", [3, 128, 384], F32, kind="ExternalInput").ap()
    ropecq_d = nc.dram_tensor("ropecq", [128, TOK_IN], F32, kind="ExternalInput").ap()
    ropesq_d = nc.dram_tensor("ropesq", [128, TOK_IN], F32, kind="ExternalInput").ap()
    ropeck_d = nc.dram_tensor("ropeck", [128, TOK_IN], F32, kind="ExternalInput").ap()
    ropesk_d = nc.dram_tensor("ropesk", [128, TOK_IN], F32, kind="ExternalInput").ap()
    iota8_d = nc.dram_tensor("iota8", [128, NE], F32, kind="ExternalInput").ap()
    onesr_d = nc.dram_tensor("onesr", [128, 128], F32, kind="ExternalInput").ap()

    xsort = nc.dram_tensor("xsort", [XS_ROWS, AUGW], F32, kind="Internal").ap()
    out_d = nc.dram_tensor("out", [TOK_OWN, D], F32, kind="ExternalOutput").ap()

    es_top = ExitStack()
    es_qkv = ExitStack()
    es_attn = ExitStack()
    es_p2 = ExitStack()
    es_ffn = ExitStack()

    def sem(name):
        return _Sem(es_top.enter_context(nc.semaphore(name)))

    s_ld = sem("s_ld")
    s_ldg = sem("s_ldg")
    s_xl = [sem(f"s_xl{i}") for i in range(5)]
    s_swp = sem("s_swp")
    s_xg = [sem(f"s_xg{i}") for i in range(2)]
    s_w = [sem(f"s_w{i}") for i in range(6)]
    s_g = [sem(f"s_g{i}") for i in range(2)]
    s_scat = sem("s_scat")
    s_outd = [sem(f"s_outd{i}") for i in range(2)]
    s_init = sem("s_init")
    s_b12 = sem("s_b12")

    t_tp = sem("t_tp")
    t_qk = sem("t_qk")
    t_v = sem("t_v")
    v_e = sem("v_e")
    a_e = sem("a_e")
    g_ms = sem("g_ms")
    v_rope = sem("v_rope")
    t_s = sem("t_s")
    t_mt = sem("t_mt")
    v_mr = sem("v_mr")
    t_mb = sem("t_mb")
    v_me = sem("v_me")
    a_ex = sem("a_ex")
    t_pv = sem("t_pv")
    v_rz = sem("v_rz")
    t_zb = sem("t_zb")
    a_zb = sem("a_zb")
    v_at = sem("v_at")
    t_wo = sem("t_wo")
    v_z1 = sem("v_z1")
    a_rs = sem("a_rs")
    v_x1 = sem("v_x1")
    t_x1t = sem("t_x1t")
    a_xe = sem("a_xe")
    t_sc = sem("t_sc")
    a_sc = sem("a_sc")
    t_sct = sem("t_sct")
    v_tm = sem("v_tm")
    v_oh = sem("v_oh")
    v_ro = sem("v_ro")
    t_rk = sem("t_rk")
    v_sl = sem("v_sl")
    g_io = sem("g_io")
    t_wq = sem("t_wq")
    t_f1 = sem("t_f1")
    a_g = sem("a_g")
    t_f2 = sem("t_f2")
    a_y = sem("a_y")
    t_yt = sem("t_yt")
    v_zz = sem("v_zz")
    a_r2 = sem("a_r2")
    v_ln = sem("v_ln")
    t_tp3 = sem("t_tp3")
    v_e3 = sem("v_e3")

    psum = es_top.enter_context(nc.psum_tensor("ps", [128, 4096], F32))

    def bank(k, c0=0, w=512, p0=0, p1=128):
        return psum[p0:p1, k * 512 + c0:k * 512 + c0 + w]

    def sb(es, name, shape, dt=F32):
        return es.enter_context(nc.sbuf_tensor(name, shape, dt))

    wstream01 = sb(es_top, "wstream01", [128, 2 * 8192], BF16)

    def wslot(i):
        m = i % 6
        buf = wstream01 if m < 2 else (wstream23 if m < 4 else wstream45)
        s = m % 2
        return buf[:, s * 8192:(s + 1) * 8192]

    def wq_src(i):
        ee, j = i // 8, i % 8
        if j < 4:
            return w1_d.rearrange("E (dc p) (qq f) -> E qq p dc f", p=128, qq=4)[ee, j]
        return w2_d.rearrange("E (qq fs p) dd -> E qq p fs dd", qq=4, p=128)[ee, j - 4]

    def wq_load(e, i):
        s_w[i % 6].inc(e.dma_start(
            wslot(i).rearrange("p (a b) -> p a b", a=8), wq_src(i)), 16)

    qrot = sb(es_attn, "qrot", [128, 4 * TOK_IN], BF16)
    krot = sb(es_attn, "krot", [128, 4 * TOK_IN], BF16)
    v_sb = sb(es_attn, "v_sb", [128, NT * 512], BF16)
    ident_sb = sb(es_attn, "ident_sb", [128, 128], F32)
    identb_sb = sb(es_attn, "identb_sb", [128, 128], BF16)
    onesr_sb = sb(es_attn, "onesr_sb", [128, 128], F32R)
    wr_sb = sb(es_attn, "wr_sb", [128, 8 * NE], F32)
    br_sb = sb(es_attn, "br_sb", [NE, 1], F32)
    iota8_sb = sb(es_attn, "iota8_sb", [128, NE], F32)
    lstrict_sb = sb(es_attn, "lstrict_sb", [128, 128], F32R)
    bnst = sb(es_attn, "bnst", [128, 12], F32)
    bnag = sb(es_attn, "bnag", [128, 2], F32)
    rstd_sb = sb(es_attn, "rstd_sb", [128, 1], F32)
    stm = sb(es_attn, "stm", [128, 32], F32)
    mx8 = sb(es_attn, "mx8", [128, 8], F32)
    idx8 = sb(es_attn, "idx8", [128, 8], U32)
    eidf = sb(es_attn, "eidf", [128, 1], F32)
    oh_sb = sb(es_attn, "oh_sb", [128, 8], F32R)
    runoff = sb(es_attn, "runoff", [1, 16], F32)
    rowoff = sb(es_attn, "rowoff", [1, 8], F32R)
    scr8 = sb(es_attn, "scr8", [128, 8], F32)
    ranksel = sb(es_attn, "ranksel", [128, 1], F32)
    ovf_sb = sb(es_attn, "ovf_sb", [128, 1], F32)
    scr1 = sb(es_attn, "scr1", [128, 1], F32)
    scr2 = sb(es_attn, "scr2", [128, 1], F32)
    scr3 = sb(es_attn, "scr3", [128, 1], F32)
    slotf = sb(es_attn, "slotf", [128, 1], F32)
    slotf2 = sb(es_attn, "slotf2", [128, 1], F32)
    slotu = sb(es_attn, "slotu", [128, 8], U32)
    tokidf = sb(es_attn, "tokidf", [128, 4], F32)
    initrow = sb(es_attn, "initrow", [128, AUGW], F32)
    eps_sb = sb(es_attn, "eps_sb", [128, 1], F32)
    lnv_sb = sb(es_attn, "lnv_sb", [128, 1], F32)

    wqkv_sb = sb(es_qkv, "wqkv_sb", [128, 8 * 1536], BF16)
    ropecq_sb = sb(es_qkv, "ropecq_sb", [128, TOK_IN], BF16)
    ropesq_sb = sb(es_qkv, "ropesq_sb", [128, TOK_IN], BF16)
    ropeck_sb = sb(es_qkv, "ropeck_sb", [128, TOK_IN], BF16)
    ropesk_sb = sb(es_qkv, "ropesk_sb", [128, TOK_IN], BF16)
    xload = sb(es_qkv, "xload", [128, 5 * D], BF16)
    xt_sb = sb(es_qkv, "xt_sb", [128, 8 * 512], BF16)
    qkraw = sb(es_qkv, "qkraw", [128, 8 * 512], BF16)
    qswap = sb(es_qkv, "qswap", [128, 8 * 512], BF16)
    rt1 = sb(es_qkv, "rt1", [128, 512], BF16)
    rt2 = sb(es_qkv, "rt2", [128, 512], BF16)

    with nc.Block() as blk:
        @blk.sync
        def _(e):
            s_ld.inc(e.dma_start(wr_sb[:].rearrange("p (dc e) -> p dc e", dc=8),
                                 wr_d.rearrange("(dc p) e -> p dc e", p=128)), 16)
            s_ld.inc(e.dma_start(br_sb[:], br_d), 16)
            s_ld.inc(e.dma_start(ident_sb[:], ident_d), 16)
            s_ld.inc(e.dma_start(lstrict_sb[:], lstrict_d.bitcast(F32R)), 16)
            s_ld.inc(e.dma_start(iota8_sb[:], iota8_d), 16)
            s_ld.inc(e.dma_start(onesr_sb[:], onesr_d.bitcast(F32R)), 16)

        @blk.gpsimd
        def _(e):
            e.dma_start(wqkv_sb[:].rearrange("p (dc f) -> p dc f", dc=8),
                        wqkv.rearrange("(dc p) f -> p dc f", p=128)).then_inc(s_ldg.h, 16)
            s_ldg.n += 16
            e.dma_start(identb_sb[:], ident_d).then_inc(s_ldg.h, 16)
            s_ldg.n += 16
            for t in range(5):
                s_xl[t].inc(e.dma_start(xload[:, t * D:(t + 1) * D],
                                        x_in[t * 128:(t + 1) * 128, :]), 16)
            s_ldg.inc(e.dma_start(ropecq_sb[:], ropecq_d), 16)
            s_ldg.inc(e.dma_start(ropesq_sb[:], ropesq_d), 16)
            s_ldg.inc(e.dma_start(ropeck_sb[:], ropeck_d), 16)
            s_ldg.inc(e.dma_start(ropesk_sb[:], ropesk_d), 16)
            e.memset(runoff[:], 0.0)
            e.memset(eps_sb[:], 1.0e-5)
            e.memset(initrow[:, 0:1024], 0.0)
            g_ms.inc(e.memset(initrow[:, 1024:AUGW], 1.0e9))

    for gi, (t0, ntl) in enumerate(GRP):
        gw = ntl * 128
        gtok = t0 * 128
        nrounds = 2 * ntl
        tp0, ve0, ae0, qk0, tv0 = t_tp.n, v_e.n, a_e.n, t_qk.n, t_v.n
        swp_target = s_swp.n + 64

        with nc.Block() as blk:
            @blk.scalar
            def _(e, gi=gi, gw=gw, qk0=qk0, ae0=ae0):
                for fc in range(8):
                    e.wait_ge(t_qk.h, qk0 + fc + 1)
                    a_e.inc(e.copy(qkraw[:, fc * 512:fc * 512 + gw], bank(2 + (fc % 4), 0, gw)))
                e.wait_ge(a_e.h, ae0 + 8)
                for bi in range(4):
                    src_lo = [32, 0, 96, 64][bi]
                    s_swp.inc(e.dma_start(qswap[bi * 32:(bi + 1) * 32, 0:8 * 512],
                                          qkraw[src_lo:src_lo + 32, 0:8 * 512]), 16)

            @blk.tensor
            def _(e, gi=gi, t0=t0, ntl=ntl, gw=gw, nrounds=nrounds, tp0=tp0, ve0=ve0,
                  ae0=ae0):
                if gi == 0:
                    e.wait_ge(s_ld.h, 6 * 16)
                    e.wait_ge(s_ldg.h, 6 * 16)
                    e.wait_ge(g_ms.h, 1)
                for r in range(nrounds):
                    t = t0 + r // 2
                    half = r % 2
                    e.wait_ge(s_xl[t % 5].h, 16 * (t // 5 + 1))
                    if r >= 2:
                        e.wait_ge(v_e.h, ve0 + (r - 1))
                    for j in range(4):
                        dc = half * 4 + j
                        ii = e.transpose(bank(r % 2, j * 64, 64).bitcast(BF16),
                                         xload[:, (t % 5) * D + dc * 128:(t % 5) * D + (dc + 1) * 128],
                                         identb_sb[:])
                    t_tp.inc(ii)
                e.wait_ge(v_e.h, ve0 + nrounds)
                for fc in range(8):
                    bk = 2 + (fc % 4)
                    if fc >= 4:
                        e.wait_ge(a_e.h, ae0 + (fc - 4) + 1)
                    for dc in range(8):
                        ii = e.matmul(bank(bk, 0, gw),
                                      wqkv_sb[:, dc * 1536 + fc * 128:dc * 1536 + (fc + 1) * 128],
                                      xt_sb[:, dc * 512:dc * 512 + gw],
                                      start=(dc == 0), stop=(dc == 7))
                    t_qk.inc(ii)
                for k in range(ntl):
                    bk = 6 + (k % 2)
                    if k >= 2:
                        e.wait_ge(v_e.h, ve0 + nrounds + (k - 2) + 1)
                    for dc in range(8):
                        ii = e.matmul(bank(bk, 0, 512),
                                      xt_sb[:, dc * 512 + k * 128:dc * 512 + (k + 1) * 128],
                                      wqkv_sb[:, dc * 1536 + 1024:dc * 1536 + 1536],
                                      start=(dc == 0), stop=(dc == 7))
                    t_v.inc(ii)

            @blk.vector
            def _(e, t0=t0, ntl=ntl, gw=gw, gtok=gtok, nrounds=nrounds, tp0=tp0,
                  tv0=tv0, swp_target=swp_target):
                for r in range(nrounds):
                    half = r % 2
                    toff = (r // 2) * 128
                    e.wait_ge(t_tp.h, tp0 + r + 1)
                    dst = xt_sb[:].rearrange("p (dc t) -> p dc t", dc=8)[
                        :, 4 * half:4 * half + 4, toff:toff + 128]
                    v_e.inc(e.tensor_copy(dst, bank(r % 2, 0, 256).bitcast(BF16).rearrange("p (j t) -> p j t", j=4)))
                for k in range(ntl):
                    e.wait_ge(t_v.h, tv0 + k + 1)
                    v_e.inc(e.tensor_copy(v_sb[:, (t0 + k) * 512:(t0 + k + 1) * 512],
                                          bank(6 + (k % 2), 0, 512)))
                e.wait_ge(s_swp.h, swp_target)
                for c in range(8):
                    e.wait_ge(v_rope.h, v_rope.n)
                    src = qkraw[:, c * 512:c * 512 + gw]
                    swp = qswap[:, c * 512:c * 512 + gw]
                    cos = (ropecq_sb if c < 4 else ropeck_sb)[:, gtok:gtok + gw]
                    sin = (ropesq_sb if c < 4 else ropesk_sb)[:, gtok:gtok + gw]
                    cc = c % 4
                    dstt = qrot if c < 4 else krot
                    dst = dstt[:, cc * TOK_IN + gtok:cc * TOK_IN + gtok + gw]
                    e.tensor_tensor(rt1[:, 0:gw], src, cos, op=ALU.mult)
                    v_rope.inc(e.tensor_tensor(rt2[:, 0:gw], swp, sin, op=ALU.mult))
                    e.wait_ge(v_rope.h, v_rope.n)
                    v_rope.inc(e.tensor_tensor(dst, rt1[:, 0:gw], rt2[:, 0:gw], op=ALU.add))

            @blk.gpsimd
            def _(e, gi=gi):
                for t in range(4 * gi + 5, min(4 * gi + 9, NT)):
                    e.wait_ge(t_tp.h, 2 * (t - 5) + 2)
                    s_xl[t % 5].inc(e.dma_start(xload[:, (t % 5) * D:(t % 5 + 1) * D],
                                                x_in[t * 128:(t + 1) * 128, :]), 16)
                n = [5, 5, 5, 5, 4][gi]
                done = [0, 5, 10, 15, 20][gi]
                if gi == 0:
                    e.wait_ge(g_ms.h, 1)
                for k in range(n):
                    r0 = (done + k) * 128
                    s_init.inc(e.dma_start(xsort[r0:r0 + 128, :], initrow[:]), 16)

    es_qkv.close()

    masks_sb = sb(es_p2, "masks_sb", [128, 3 * 384], BF16)
    wout_sb = sb(es_p2, "wout_sb", [128, 8 * INNER], BF16)
    e_sb = sb(es_p2, "e_sb", [128, 2 * 1536], BF16)
    e2_sb = sb(es_p2, "e2_sb", [128, 2 * 1536], BF16)
    pt_sb = sb(es_p2, "pt_sb", [128, 2 * 1536], BF16)
    zacc_sb = sb(es_p2, "zacc_sb", [128, 8], F32)
    attnT = sb(es_p2, "attnT", [128, 512], BF16)
    invz_sb = sb(es_p2, "invz_sb", [128, 8], F32)
    xg_sb = sb(es_p2, "xg_sb", [128, 2 * D], F32)
    z1_sb = sb(es_p2, "z1_sb", [128, D], F32)
    x1aug = sb(es_p2, "x1aug", [128, 4 * AUGW], F32)
    x1T = sb(es_p2, "x1T", [128, 8 * 512], F32)
    scev = sb(es_p2, "scev", [8, 512], F32)

    with nc.Block() as blk:
        @blk.scalar
        def _(e):
            for q in range(2):
                s_xg[q].inc(e.dma_start(xg_sb[:, q * D:(q + 1) * D],
                                        x_in[(q + 2) * 128:(q + 3) * 128, :]), 16)

        @blk.gpsimd
        def _(e):
            s_ldg.inc(e.dma_start(masks_sb[:].rearrange("p (c k) -> p c k", c=3),
                                 masks_d.rearrange("c p k -> p c k")), 16)
            s_ldg.inc(e.dma_start(wout_sb[:].rearrange("p (c f) -> p c f", c=4),
                                  wout.rearrange("(c p) f -> p c f", p=128)), 16)
            for j in range(4):
                ii = e.memset(x1aug[:, j * AUGW + 1024:j * AUGW + 1028], 0.0)
            g_ms.inc(ii)
            for i in range(2):
                wq_load(e, i)

    def cls_of(t):
        return 1 if t == 2 else (2 if t == 3 else 0)

    def p2_S(e, q):
        t = q + 2
        cls = cls_of(t)
        if q == 0:
            e.wait_ge(s_ld.h, 6 * 16)
            e.wait_ge(s_ldg.h, 8 * 16)
        for w in range(2):
            if w == 1:
                e.wait_ge(a_ex.h, 8 * q + 4)
            elif q >= 1:
                e.wait_ge(a_ex.h, 8 * q)
                e.wait_ge(a_zb.h, 2 * q)
            for j in range(4):
                h = 4 * w + j
                pb = (h % 2) * 64
                ch = h // 2
                e.matmul(bank(j, 0, 384),
                         qrot[pb:pb + 64, ch * TOK_IN + t * 128:ch * TOK_IN + (t + 1) * 128],
                         krot[pb:pb + 64, ch * TOK_IN + (t - 2) * 128:ch * TOK_IN + (t + 1) * 128],
                         start=True, stop=False)
            for j in range(4):
                ii = e.matmul(bank(j, 0, 384), identb_sb[:],
                              masks_sb[:, cls * 384:(cls + 1) * 384],
                              start=False, stop=True)
            t_s.inc(ii)

    def p2_PTPV(e, q):
        t = q + 2
        P = q % 2
        for w in range(2):
            if q >= 2:
                e.wait_ge(a_xe.h, 2 * q - 2)
            for j in range(4):
                e.wait_ge(v_me.h, 8 * q + 4 * w + j + 1)
                for b in range(3):
                    c0b = (j % 2) * 384 + b * 128
                    ii = e.transpose(
                        bank(4 + 2 * P + j // 2, c0b // 2, 64).bitcast(BF16),
                        e2_sb[:, w * 1536 + j * 384 + b * 128:w * 1536 + j * 384 + (b + 1) * 128],
                        identb_sb[:])
                t_mt.inc(ii)
            e.wait_ge(a_ex.h, 8 * q + 8)
            e.wait_ge(v_at.h, 2 * q + w + 1)
            if w == 1:
                e.wait_ge(a_zb.h, 2 * q + 1)
            for j in range(4):
                h = 4 * w + j
                p0 = (j % 2) * 64
                for b in range(3):
                    ii = e.matmul(bank(j // 2, 384, 128, p0=p0, p1=p0 + 64),
                                  v_sb[:, (t - 2 + b) * 512 + h * 64:(t - 2 + b) * 512 + (h + 1) * 64],
                                  pt_sb[:, w * 1536 + j * 384 + b * 128:w * 1536 + j * 384 + (b + 1) * 128],
                                  start=(b == 0), stop=(b == 2))
            t_pv.inc(ii)

    def p2_Wout(e, q):
        P = q % 2
        e.wait_ge(a_zb.h, 2 * q + 2)
        for dh in range(2):
            for c in range(4):
                ii = e.matmul(bank(4 + 2 * P + dh, 0, 512),
                              attnT[:, c * 128:(c + 1) * 128],
                              wout_sb[:, c * 1024 + dh * 512:c * 1024 + (dh + 1) * 512],
                              start=(c == 0), stop=(c == 3))
            t_wo.inc(ii)

    def p2_x1T(e, q):
        P = q % 2
        e.wait_ge(v_x1.h, q + 1)
        for half in range(2):
            for j in range(4):
                dc = 4 * half + j
                ii = e.transpose(bank(4 + 2 * P + half, j * 128, 128),
                                 x1aug[:, (q % 4) * AUGW + dc * 128:(q % 4) * AUGW + (dc + 1) * 128],
                                 ident_sb[:])
            t_x1t.inc(ii)

    def p2_softmax_v(e, q):
        P = q % 2
        for w in range(2):
            e.wait_ge(a_ex.h, 8 * q + 4 * (w + 1))
            if q >= 1 and w == 0:
                e.wait_ge(v_me.h, 8 * q)
            v_rz.inc(e.reciprocal(invz_sb[:, 4 * w:4 * w + 4],
                                  zacc_sb[:, 4 * w:4 * w + 4]))
            e.wait_ge(v_rz.h, 2 * q + w + 1)
            for j in range(4):
                v_me.inc(e.tensor_scalar(
                    e2_sb[:, w * 1536 + j * 384:w * 1536 + (j + 1) * 384],
                    e_sb[:, w * 1536 + j * 384:w * 1536 + (j + 1) * 384],
                    invz_sb[:, 4 * w + j:4 * w + j + 1], None, op0=ALU.mult))
    def p2_evac_v(e, q):
        P = q % 2
        for w in range(2):
            e.wait_ge(t_mt.h, 8 * q + 4 * (w + 1))
            if w == 0:
                e.wait_ge(t_pv.h, 2 * q)
            srcv = psum[:, (4 + 2 * P) * 512:(6 + 2 * P) * 512].bitcast(BF16)\
                .rearrange("p (bk c) -> p bk c", bk=2)[:, :, 0:768]
            dstv = pt_sb[:, w * 1536:(w + 1) * 1536].rearrange("p (bk c) -> p bk c", bk=2)
            v_at.inc(e.tensor_copy(dstv, srcv))
        for w in range(2):
            e.wait_ge(t_pv.h, 2 * q + w + 1)
            if w == 0 and q >= 1:
                e.wait_ge(t_wo.h, 2 * q)
            srct = psum[:, 0:1024].rearrange("p (j c) -> p j c", j=2)[:, :, 384:512]
            a_zb.inc(e.tensor_copy(
                attnT[:, w * 256:(w + 1) * 256].rearrange("p (j c) -> p j c", j=2), srct))

    def p2_ln1_v(e, q):
        P = q % 2
        for dh in range(2):
            e.wait_ge(t_wo.h, 2 * q + dh + 1)
            e.wait_ge(s_xg[q % 2].h, 16 * (q // 2 + 1))
            if dh == 0 and q >= 1:
                e.wait_ge(v_x1.h, q)
            v_z1.inc(e.tensor_tensor(z1_sb[:, dh * 512:(dh + 1) * 512],
                                     xg_sb[:, (q % 2) * D + dh * 512:(q % 2) * D + (dh + 1) * 512],
                                     bank(4 + 2 * P + dh, 0, 512), op=ALU.add))
        e.wait_ge(v_z1.h, 4 * q + 2)
        e.bn_stats(bnst[:, 0:6], z1_sb[:, 0:512])
        v_z1.inc(e.bn_stats(bnst[:, 6:12], z1_sb[:, 512:1024]))
        e.wait_ge(v_z1.h, 4 * q + 3)
        if q >= 1:
            e.wait_ge(a_rs.h, 2 * q)
        v_z1.inc(e.bn_aggr(bnag[:], bnst[:]))
        e.wait_ge(a_rs.h, 2 * (q + 1))
        if q >= 4:
            e.wait_ge(s_scat.h, 16 * 4 * (q // 4))
        v_x1.inc(e.tensor_scalar(x1aug[:, (q % 4) * AUGW:(q % 4) * AUGW + 1024],
                                 z1_sb[:], bnag[:, 0:1], rstd_sb[:],
                                 op0=ALU.subtract, op1=ALU.mult))

    def p2_exps_a(e, q):
        for w in range(2):
            e.wait_ge(t_s.h, 2 * q + w + 1)
            if w == 0 and q >= 1:
                e.wait_ge(v_me.h, 8 * q)
            for j in range(4):
                a_ex.inc(e.activation(
                    e_sb[:, w * 1536 + j * 384:w * 1536 + (j + 1) * 384],
                    bank(j, 0, 384), AF.Exp,
                    accum_out=zacc_sb[:, 4 * w + j:4 * w + j + 1]))

    def p2_tail_a(e, q):
        e.wait_ge(v_z1.h, 4 * q + 4)
        a_rs.inc(e.activation(lnv_sb[:], bnag[:, 1:2], AF.Ln, bias=eps_sb[:]))
        e.wait_ge(a_rs.h, a_rs.n)
        if q >= 1:
            e.wait_ge(v_x1.h, q)
        a_rs.inc(e.activation(rstd_sb[:], lnv_sb[:], AF.Exp, scale=-0.5))

    def p2_x1e_a(e, q):
        P = q % 2
        for half in range(2):
            e.wait_ge(t_x1t.h, 2 * q + half + 1)
            dst = x1T[:].rearrange("p (dc x) -> p dc x", dc=8)[
                :, 4 * half:4 * half + 4, (q % 4) * 128:(q % 4) * 128 + 128]
            a_xe.inc(e.copy(dst, bank(4 + 2 * P + half, 0, 512).rearrange("p (j x) -> p j x", j=4)))
        if q + 2 < NQT:
            s_xg[(q + 2) % 2].inc(
                e.dma_start(xg_sb[:, ((q + 2) % 2) * D:((q + 2) % 2 + 1) * D],
                            x_in[(q + 4) * 128:(q + 5) * 128, :]), 16)

    for g in range(4):
        g0 = 4 * g
        with nc.Block() as blk:
            @blk.tensor
            def _(e, g0=g0):
                for i in range(4):
                    q = g0 + i
                    if i >= 1:
                        p2_PTPV(e, q - 1)
                    p2_S(e, q)
                    if i >= 1:
                        p2_Wout(e, q - 1)
                    if i >= 2:
                        p2_x1T(e, q - 2)
                p2_PTPV(e, g0 + 3)
                p2_Wout(e, g0 + 3)
                p2_x1T(e, g0 + 2)
                p2_x1T(e, g0 + 3)

            @blk.vector
            def _(e, g0=g0):
                for i in range(4):
                    q = g0 + i
                    p2_softmax_v(e, q)
                    if i >= 1:
                        p2_ln1_v(e, q - 1)
                    p2_evac_v(e, q)
                p2_ln1_v(e, g0 + 3)

            @blk.scalar
            def _(e, g0=g0):
                for i in range(4):
                    q = g0 + i
                    p2_exps_a(e, q)
                    if i >= 1:
                        p2_tail_a(e, q - 1)
                    if i >= 2:
                        p2_x1e_a(e, q - 2)
                p2_tail_a(e, g0 + 3)
                p2_x1e_a(e, g0 + 2)
                p2_x1e_a(e, g0 + 3)

            @blk.gpsimd
            def _(e, g0=g0):
                for i in range(4):
                    q = g0 + i
                    if q >= 4:
                        e.wait_ge(s_scat.h, 16 * 4 * (q // 4))
                    e.iota(x1aug[:, (q % 4) * AUGW + 1024:(q % 4) * AUGW + 1025], [[1, 1]],
                           base=q * 128, channel_multiplier=1,
                           allow_small_or_imprecise_dtypes=True)
                    g_io.inc(e.iota(tokidf[:, q % 4:q % 4 + 1], [[1, 1]], base=q * 128,
                                    channel_multiplier=1,
                                    allow_small_or_imprecise_dtypes=True))

        with nc.Block() as blk:
            @blk.tensor
            def _(e, g=g):
                e.wait_ge(a_xe.h, 8 * (g + 1) * 1)
                for dc in range(8):
                    ii = e.matmul(bank(0, 0, 512, p0=0, p1=8),
                                  wr_sb[:, dc * 8:(dc + 1) * 8],
                                  x1T[:, dc * 512:(dc + 1) * 512],
                                  start=(dc == 0), stop=(dc == 7))
                t_sc.inc(ii)
                e.wait_ge(a_sc.h, g + 1)
                for k in range(4):
                    ii = e.transpose(bank(1, k * 128, 8, p0=0, p1=128),
                                     scev[0:8, k * 128:(k + 1) * 128], ident_sb[0:8, 0:8])
                t_sct.inc(ii)
                for k in range(4):
                    qq = 4 * g + k
                    e.wait_ge(v_oh.h, 4 * qq + 4)
                    e.wait_ge(v_ro.h, qq + 1)
                    if k > 0:
                        e.wait_ge(v_sl.h, 7 * qq - 5)
                    e.matmul(bank(2, k * 32, 8, p0=0, p1=128), lstrict_sb[:],
                             oh_sb[:], start=True, stop=False)
                    e.matmul(bank(2, k * 32, 8, p0=0, p1=128), onesr_sb[0:1, 0:128],
                             rowoff[:], start=False, stop=True)
                    t_rk.inc(e.matmul(bank(2, k * 32 + 16, 8, p0=0, p1=1),
                                      onesr_sb[:, 0:1], oh_sb[:],
                                      start=True, stop=True))

            @blk.scalar
            def _(e, g=g):
                e.wait_ge(t_sc.h, g + 1)
                a_sc.inc(e.activation(scev[:], bank(0, 0, 512, p0=0, p1=8),
                                      AF.Identity, bias=br_sb[:]))

            @blk.vector
            def _(e, g=g):
                e.wait_ge(t_sct.h, g + 1)
                for k in range(4):
                    v_tm.inc(e.tensor_copy(stm[:, k * 8:(k + 1) * 8],
                                           bank(1, k * 128, 8, p0=0, p1=128)))
                for k in range(4):
                    qq = 4 * g + k
                    e.wait_ge(v_tm.h, 4 * g + 4)
                    if qq > 0:
                        e.wait_ge(t_rk.h, qq)
                    v_oh.inc(e.max(mx8[:], stm[:, k * 8:(k + 1) * 8]))
                    e.wait_ge(v_oh.h, 4 * qq + 1)
                    v_oh.inc(e.max_index(idx8[:], mx8[:], stm[:, k * 8:(k + 1) * 8]))
                    e.wait_ge(v_oh.h, 4 * qq + 2)
                    v_oh.inc(e.tensor_copy(eidf[:], idx8[:, 0:1]))
                    e.wait_ge(v_oh.h, 4 * qq + 3)
                    v_oh.inc(e.tensor_scalar(oh_sb[:], iota8_sb[:], eidf[:], None,
                                             op0=ALU.is_equal))
                    if qq > 0:
                        e.wait_ge(v_sl.h, 7 * qq - 5)
                    v_ro.inc(e.tensor_copy(rowoff[:], runoff[:, (qq % 2) * 8:(qq % 2) * 8 + 8]))
                    e.wait_ge(t_rk.h, qq + 1)
                    v_sl.inc(e.tensor_tensor(scr8[:], oh_sb[:].bitcast(F32),
                                             bank(2, k * 32, 8, p0=0, p1=128),
                                             op=ALU.mult))
                    e.wait_ge(v_sl.h, 7 * qq + 1)
                    e.reduce_sum(ranksel[:], scr8[:], axis=AX.X)
                    v_sl.inc(e.tensor_tensor(runoff[:, ((qq + 1) % 2) * 8:((qq + 1) % 2) * 8 + 8],
                                             runoff[:, (qq % 2) * 8:(qq % 2) * 8 + 8],
                                             bank(2, k * 32 + 16, 8, p0=0, p1=1),
                                             op=ALU.add))
                    e.wait_ge(v_sl.h, 7 * qq + 2)
                    e.tensor_scalar(ovf_sb[:], ranksel[:], float(CAP), None, op0=ALU.is_ge)
                    v_sl.inc(e.tensor_scalar(scr1[:], eidf[:], float(CAP), None, op0=ALU.mult))
                    e.wait_ge(v_sl.h, 7 * qq + 3)
                    e.wait_ge(g_io.h, qq + 1)
                    e.tensor_tensor(slotf[:], ranksel[:], scr1[:], op=ALU.add)
                    v_sl.inc(e.tensor_scalar(scr2[:], tokidf[:, qq % 4:qq % 4 + 1],
                                             float(TRASH), None, op0=ALU.add))
                    e.wait_ge(v_sl.h, 7 * qq + 4)
                    v_sl.inc(e.tensor_tensor(scr3[:], scr2[:], slotf[:], op=ALU.subtract))
                    e.wait_ge(v_sl.h, 7 * qq + 5)
                    v_sl.inc(e.tensor_scalar(slotf2[:], scr3[:], ovf_sb[:], slotf[:],
                                             op0=ALU.mult, op1=ALU.add))
                    e.wait_ge(v_sl.h, 7 * qq + 6)
                    if g >= 2:
                        e.wait_ge(s_scat.h, 16 * 4 * (g - 1))
                    v_sl.inc(e.tensor_copy(slotu[:, (g % 2) * 4 + k:(g % 2) * 4 + k + 1],
                                           slotf2[:]))

            @blk.gpsimd
            def _(e, g=g):
                for k in range(4):
                    qq = 4 * g + k
                    e.wait_ge(v_sl.h, 7 * qq + 7)
                    e.wait_ge(v_x1.h, qq + 1)
                    if qq == 0:
                        e.wait_ge(s_init.h, 24 * 16)
                        e.wait_ge(g_ms.h, 2)
                    s_scat.inc(e.indirect_dma_start(
                        xsort[:, :],
                        bass.IndirectOffsetOnAxis(ap=slotu[:, (g % 2) * 4 + k:(g % 2) * 4 + k + 1], axis=0),
                        x1aug[:, (qq % 4) * AUGW:(qq % 4) * AUGW + AUGW], None), 16)

    es_p2.close()
    es_attn.close()

    wstream23 = sb(es_ffn, "wstream23", [128, 2 * 8192], BF16)
    wstream45 = sb(es_ffn, "wstream45", [128, 2 * 8192], BF16)
    b1_sb = sb(es_ffn, "b1_sb", [128, NE * 32], F32)
    b2_sb = sb(es_ffn, "b2_sb", [128, NE * 8], F32)
    x_sel = sb(es_ffn, "x_sel", [128, 2 * NRT * AUGW], F32)
    xselT = sb(es_ffn, "xselT", [128, 8 * TOKW], BF16)
    hT = sb(es_ffn, "hT", [128, 32 * TOKW], BF16)
    yT = sb(es_ffn, "yT", [128, 8 * TOKW], BF16)
    z_sb = sb(es_ffn, "z_sb", [128, AUGW], F32)
    zn_sb = sb(es_ffn, "zn_sb", [128, 2 * 1024], F32)
    bnst2 = sb(es_ffn, "bnst2", [128, 12], F32)
    bnag2 = sb(es_ffn, "bnag2", [128, 4], F32)
    rstd2 = sb(es_ffn, "rstd2", [128, 2], F32)
    slotO = sb(es_ffn, "slotO", [128, 2], U32)
    ident2_sb = sb(es_ffn, "ident2_sb", [128, 128], F32)
    identb2_sb = sb(es_ffn, "identb2_sb", [128, 128], BF16)
    eps3_sb = sb(es_ffn, "eps3_sb", [128, 1], F32)
    rx_sb = sb(es_ffn, "rx_sb", [128, 1], F32)
    rf_sb = sb(es_ffn, "rf_sb", [128, 1], F32)
    ru_sb = sb(es_ffn, "ru_sb", [128, 1], F32)
    ry_sb = sb(es_ffn, "ry_sb", [128, 1], F32)
    rt1_sb = sb(es_ffn, "rt1_sb", [128, 1], F32)

    XS4 = NRT * AUGW

    with nc.Block() as blk:
        @blk.sync
        def _(e):
            e.wait_ge(s_scat.h, 16 * 16)
            s_b12.inc(e.dma_start(b1_sb[:], b1_d), 16)
            s_b12.inc(e.dma_start(b2_sb[:], b2_d), 16)
            s_ld.inc(e.dma_start(ident2_sb[:], ident_d), 16)

        @blk.scalar
        def _(e):
            e.wait_ge(s_scat.h, 16 * 16)
            for ee in range(2):
                for rt in range(NRT):
                    s_g[ee % 2].inc(e.dma_start(
                        x_sel[:, ee * XS4 + rt * AUGW:ee * XS4 + (rt + 1) * AUGW],
                        xsort[ee * CAP + rt * 128:ee * CAP + (rt + 1) * 128, :]), 16)

        @blk.gpsimd
        def _(e):
            e.wait_ge(s_scat.h, 16 * 16)
            s_ldg.inc(e.dma_start(identb2_sb[:], ident_d), 16)
            for i in range(2, 6):
                wq_load(e, i)
            g_ms.inc(e.memset(eps3_sb[:], 1.0e-5))

    def p3_gelu_a(e, ee, fcg):
        e.wait_ge(t_f1.h, 32 * ee + fcg + 1)
        if fcg == 0:
            if ee == 0:
                e.wait_ge(s_b12.h, 32)
                e.wait_ge(g_ms.h, 3)
            else:
                e.wait_ge(t_f2.h, 4 * ee)
        a_g.inc(e.activation(hT[:, fcg * TOKW:(fcg + 1) * TOKW],
                             bank(2 + (fcg % 2), 0, TOKW), AF.Gelu,
                             bias=b1_sb[:, ee * 32 + fcg:ee * 32 + fcg + 1]))

    with nc.Block() as blk:
        @blk.scalar
        def _(e):
            for fcg in range(32):
                p3_gelu_a(e, 0, fcg)
            for ee in range(NE):
                for dc in range(8):
                    e.wait_ge(t_f2.h, 4 * (ee + 1))
                    if dc == 0:
                        e.wait_ge(t_yt.h, NRT * ee)
                    a_y.inc(e.activation(yT[:, dc * TOKW:(dc + 1) * TOKW],
                                         bank(dc, 0, TOKW), AF.Identity,
                                         bias=b2_sb[:, ee * 8 + dc:ee * 8 + dc + 1]))
                if ee + 1 < NE:
                    for fcg in range(32):
                        p3_gelu_a(e, ee + 1, fcg)
                if ee + 2 < NE:
                    e.wait_ge(v_ln.h, NRT * (ee + 1))
                    par2 = (ee + 2) % 2
                    for rt in range(NRT):
                        s_g[par2].inc(e.dma_start(
                            x_sel[:, par2 * XS4 + rt * AUGW:par2 * XS4 + (rt + 1) * AUGW],
                            xsort[(ee + 2) * CAP + rt * 128:(ee + 2) * CAP + (rt + 1) * 128, :]), 16)

        @blk.tensor
        def _(e):
            for ee in range(NE):
                par = ee % 2
                if ee == 0:
                    e.wait_ge(s_ld.h, 7 * 16)
                    e.wait_ge(s_ldg.h, 9 * 16)
                e.wait_ge(s_g[ee % 2].h, 16 * NRT * (ee // 2 + 1))
                if ee >= 1:
                    e.wait_ge(a_y.h, 8 * ee - 4)
                for rt in range(NRT):
                    if rt >= 2:
                        e.wait_ge(v_e3.h, 2 * NRT * ee + 2 * (rt - 1))
                    for dc in range(8):
                        ii = e.transpose(bank(2 * (rt % 2) + dc // 4, (dc % 4) * 128, 128),
                                         x_sel[:, par * XS4 + rt * AUGW + dc * 128:par * XS4 + rt * AUGW + (dc + 1) * 128],
                                         ident2_sb[:])
                    t_tp3.inc(ii)
                e.wait_ge(v_e3.h, 2 * NRT * (ee + 1))
                for j in range(4):
                    i = 8 * ee + j
                    e.wait_ge(s_w[i % 6].h, 16 * (i // 6 + 1))
                    for fc in range(8):
                        fcg = 8 * j + fc
                        if fcg >= 2:
                            e.wait_ge(a_g.h, 32 * ee + fcg - 1)
                        for dc in range(8):
                            ii = e.matmul(bank(2 + (fcg % 2), 0, TOKW),
                                          wslot(i)[:, dc * 1024 + fc * 128:
                                                   dc * 1024 + (fc + 1) * 128],
                                          xselT[:, dc * TOKW:(dc + 1) * TOKW],
                                          start=(dc == 0), stop=(dc == 7))
                        t_f1.inc(ii)
                e.wait_ge(a_g.h, 32 * (ee + 1))
                if ee >= 1:
                    e.wait_ge(v_zz.h, 15 * (NRT * ee - 1) + 1)
                for j in range(4):
                    i = 8 * ee + 4 + j
                    e.wait_ge(s_w[i % 6].h, 16 * (i // 6 + 1))
                    for fc in range(8):
                        fcg = 8 * j + fc
                        for dc in range(8):
                            ii = e.matmul(bank(dc, 0, TOKW),
                                          wslot(i)[:, fc * 1024 + dc * 128:
                                                   fc * 1024 + (dc + 1) * 128],
                                          hT[:, fcg * TOKW:(fcg + 1) * TOKW],
                                          start=(fcg == 0), stop=(fcg == 31))
                    t_f2.inc(ii)
                e.wait_ge(a_y.h, 8 * (ee + 1))
                for rt in range(NRT):
                    if rt >= 2:
                        e.wait_ge(v_zz.h, 15 * (NRT * ee + rt - 2) + 1)
                    for dc in range(8):
                        ii = e.transpose(bank(4 + 2 * (rt % 2) + dc // 4, (dc % 4) * 64, 64).bitcast(BF16),
                                         yT[:, dc * TOKW + rt * 128:dc * TOKW + (rt + 1) * 128],
                                         identb2_sb[:])
                    t_yt.inc(ii)

        @blk.vector
        def _(e):
            for rt in range(NRT):
                e.wait_ge(t_tp3.h, rt + 1)
                for half in range(2):
                    dst = xselT[:].rearrange("p (dc x) -> p dc x", dc=8)[
                        :, 4 * half:4 * half + 4, rt * 128:(rt + 1) * 128]
                    v_e3.inc(e.tensor_copy(dst, bank(2 * (rt % 2) + half, 0, 512).rearrange("p (j x) -> p j x", j=4)))
            for ee in range(NE):
                par = ee % 2
                for rt in range(NRT):
                    R = NRT * ee + rt
                    Z0 = 15 * R
                    e.wait_ge(t_yt.h, NRT * ee + rt + 1)
                    if R >= 1:
                        e.wait_ge(v_ln.h, R)
                    for dh in range(2):
                        e.tensor_tensor(z_sb[:, dh * 512:(dh + 1) * 512],
                                        x_sel[:, par * XS4 + rt * AUGW + dh * 512:par * XS4 + rt * AUGW + (dh + 1) * 512],
                                        bank(4 + 2 * (rt % 2) + dh, 0, 256).bitcast(BF16), op=ALU.add)
                    v_zz.inc(e.tensor_copy(z_sb[:, 1024:1025],
                                           x_sel[:, par * XS4 + rt * AUGW + 1024:par * XS4 + rt * AUGW + 1025]))
                    e.wait_ge(v_zz.h, Z0 + 1)
                    e.bn_stats(bnst2[:, 0:6], z_sb[:, 0:512])
                    v_zz.inc(e.bn_stats(bnst2[:, 6:12], z_sb[:, 512:1024]))
                    e.wait_ge(v_zz.h, Z0 + 2)
                    v_zz.inc(e.bn_aggr(bnag2[:, 0:2], bnst2[:]))
                    e.wait_ge(v_zz.h, Z0 + 3)
                    v_zz.inc(e.tensor_scalar(rx_sb[:], bnag2[:, 1:2], 1.0e-5, None, op0=ALU.add))
                    e.wait_ge(v_zz.h, Z0 + 4)
                    v_zz.inc(e.tensor_copy(rf_sb[:], rx_sb[:].bitcast(U32)))
                    e.wait_ge(v_zz.h, Z0 + 5)
                    v_zz.inc(e.tensor_scalar(ru_sb[:], rf_sb[:], -0.5, 1597463007.0,
                                             op0=ALU.mult, op1=ALU.add))
                    e.wait_ge(v_zz.h, Z0 + 6)
                    v_zz.inc(e.tensor_copy(ry_sb[:].bitcast(U32), ru_sb[:]))
                    for _ in range(2):
                        e.wait_ge(v_zz.h, v_zz.n)
                        v_zz.inc(e.tensor_tensor(rt1_sb[:], ry_sb[:], ry_sb[:], op=ALU.mult))
                        e.wait_ge(v_zz.h, v_zz.n)
                        v_zz.inc(e.tensor_tensor(rt1_sb[:], rt1_sb[:], rx_sb[:], op=ALU.mult))
                        e.wait_ge(v_zz.h, v_zz.n)
                        v_zz.inc(e.tensor_scalar(rt1_sb[:], rt1_sb[:], -0.5, 1.5,
                                                 op0=ALU.mult, op1=ALU.add))
                        e.wait_ge(v_zz.h, v_zz.n)
                        v_zz.inc(e.tensor_tensor(ry_sb[:], ry_sb[:], rt1_sb[:], op=ALU.mult))
                    e.wait_ge(v_zz.h, Z0 + 15)
                    if R >= 2:
                        e.wait_ge(s_outd[R % 2].h, 16 * (R // 2))
                    e.tensor_scalar(zn_sb[:, (R % 2) * 1024:(R % 2) * 1024 + 1024],
                                    z_sb[:, 0:1024], bnag2[:, 0:1], ry_sb[:],
                                    op0=ALU.subtract, op1=ALU.mult)
                    v_ln.inc(e.tensor_copy(slotO[:, R % 2:R % 2 + 1], z_sb[:, 1024:1025]))
                if ee + 1 < NE:
                    for rt in range(NRT):
                        e.wait_ge(t_tp3.h, NRT * (ee + 1) + rt + 1)
                        for half in range(2):
                            dst = xselT[:].rearrange("p (dc x) -> p dc x", dc=8)[
                                :, 4 * half:4 * half + 4, rt * 128:(rt + 1) * 128]
                            v_e3.inc(e.tensor_copy(dst, bank(2 * (rt % 2) + half, 0, 512).rearrange("p (j x) -> p j x", j=4)))

        @blk.gpsimd
        def _(e):
            for ee in range(NE):
                for i in range(8 * ee + 6, min(8 * ee + 14, 64)):
                    ip = i - 6
                    eep, jp = ip // 8, ip % 8
                    if jp < 4:
                        e.wait_ge(t_f1.h, 32 * eep + 8 * (jp + 1))
                    else:
                        e.wait_ge(t_f2.h, 4 * eep + (jp - 4) + 1)
                    wq_load(e, i)
                for rt in range(NRT):
                    R = NRT * ee + rt
                    e.wait_ge(v_ln.h, NRT * ee + rt + 1)
                    s_outd[R % 2].inc(e.indirect_dma_start(
                        out_d[:, :],
                        bass.IndirectOffsetOnAxis(ap=slotO[:, R % 2:R % 2 + 1], axis=0),
                        zn_sb[:, (R % 2) * 1024:(R % 2) * 1024 + 1024], None,
                        bounds_check=TOK_OWN - 1, oob_is_err=False), 16)

    with nc.Block() as blk:
        @blk.gpsimd
        def _(e):
            e.wait_ge(s_outd[0].h, 16 * 12)
            e.wait_ge(s_outd[1].h, 16 * 12)

    es_ffn.close()
    es_top.close()
    return nc


def _host_consts(core):
    b, s = core // 2, core % 2
    r = np.arange(128)[:, None]
    m = np.arange(128)[None, :]
    blkA = np.where(m > r, 0.0, NEG).astype(np.float32)
    blkB = np.zeros((128, 128), np.float32)
    blkC = np.where(m <= r, 0.0, NEG).astype(np.float32)
    negb = np.full((128, 128), NEG, np.float32)
    generic = np.concatenate([blkA, blkB, blkC], axis=1)
    if s == 0:
        edge0 = np.concatenate([negb, negb, blkC], axis=1)
        edge1 = np.concatenate([negb, blkB, blkC], axis=1)
    else:
        edge0 = generic
        edge1 = generic
    masks = np.stack([generic, edge0, edge1])

    pos = (s * 2048 - 256 + np.arange(TOK_IN)).astype(np.float32)
    inv = (1.0 / 10000.0 ** (np.arange(0, 64, 2) / 64.0)).astype(np.float32)
    dmod = np.arange(128) % 64
    fidx = dmod % 32
    ang = pos[None, :] * inv[fidx][:, None]
    ropec = np.cos(ang).astype(np.float32)
    sign = np.where(dmod < 32, -1.0, 1.0).astype(np.float32)
    ropes = (sign[:, None] * np.sin(ang)).astype(np.float32)
    return masks, ropec * 0.125, ropes * 0.125, ropec, ropes


def kernel(L0_attn_x, L0_attn_Wqkv, L0_attn_Wout, L0_ffn_wr, L0_ffn_br,
           L0_ffn_w1, L0_ffn_b1, L0_ffn_w2, L0_ffn_b2):
    x = np.ascontiguousarray(L0_attn_x, dtype=np.float32)
    ident = np.eye(128, dtype=np.float32)
    lstrict = np.triu(np.ones((128, 128), np.float32), 1)
    iota8 = np.broadcast_to(np.arange(NE, dtype=np.float32), (128, NE)).copy()
    onesr = np.ones((128, 128), np.float32)
    shared = dict(
        wqkv=np.ascontiguousarray(L0_attn_Wqkv, np.float32),
        wout=np.ascontiguousarray(L0_attn_Wout, np.float32),
        wr=np.ascontiguousarray(L0_ffn_wr, np.float32),
        br=np.ascontiguousarray(L0_ffn_br, np.float32).reshape(NE, 1),
        w1=np.ascontiguousarray(L0_ffn_w1, np.float32),
        b1=np.ascontiguousarray(np.asarray(L0_ffn_b1, np.float32).reshape(NE, 32, 128).transpose(2, 0, 1).reshape(128, NE * 32)),
        w2=np.ascontiguousarray(L0_ffn_w2, np.float32),
        b2=np.ascontiguousarray(np.asarray(L0_ffn_b2, np.float32).reshape(NE, 8, 128).transpose(2, 0, 1).reshape(128, NE * 8)),
        ident=ident, lstrict=lstrict, iota8=iota8, onesr=onesr,
    )
    in_maps = []
    for c in range(8):
        b, s = c // 2, c % 2
        own = x[b, s * 2048:(s + 1) * 2048]
        halo = np.zeros((256, D), np.float32) if s == 0 else x[b, 1792:2048]
        x_c = np.concatenate([halo, own], axis=0)
        masks, rcq, rsq, rck, rsk = _host_consts(c)
        m = dict(shared)
        m.update(x_in=np.ascontiguousarray(x_c), masks=masks,
                 ropecq=rcq, ropesq=rsq, ropeck=rck, ropesk=rsk)
        in_maps.append(m)

    nc = _build_nc()
    res = run_bass_kernel_spmd(nc, in_maps, core_ids=list(range(8)))
    out = np.empty((4, 4096, D), np.float32)
    for c in range(8):
        out[c // 2, (c % 2) * 2048:(c % 2 + 1) * 2048] = res.results[c]["out"]
    return out


# version 20
# speedup vs baseline: 1.1671x; 1.0308x over previous
import numpy as np
import concourse.bass as bass
import concourse.mybir as mybir
from concourse.bass_utils import run_bass_kernel_spmd
from contextlib import ExitStack

F32 = mybir.dt.float32
F32R = mybir.dt.float32r
BF16 = mybir.dt.bfloat16
U32 = mybir.dt.uint32
AF = mybir.ActivationFunctionType
ALU = mybir.AluOpType
AX = mybir.AxisListType

D = 1024
INNER = 512
NT = 18
NQT = 16
TOK_IN = NT * 128
TOK_OWN = 2048
NE = 8
CAP = 384
TOKW = CAP
NRT = CAP // 128
FF = 4096
SORT_ROWS = NE * CAP
TRASH = SORT_ROWS
XS_ROWS = SORT_ROWS + TOK_OWN
AUGW = 1028
NEG = -1.0e30
GRP = [(0, 4), (4, 4), (8, 4), (12, 4), (16, 2)]


class _Sem:
    __slots__ = ("h", "n")

    def __init__(self, h):
        self.h = h
        self.n = 0

    def inc(self, instr, k=1):
        instr.then_inc(self.h, k)
        self.n += k
        return instr


def _build_nc():
    nc = bass.Bass("TRN2", target_bir_lowering=False)

    x_in = nc.dram_tensor("x_in", [TOK_IN, D], F32, kind="ExternalInput").ap()
    wqkv = nc.dram_tensor("wqkv", [D, 3 * INNER], F32, kind="ExternalInput").ap()
    wout = nc.dram_tensor("wout", [INNER, D], F32, kind="ExternalInput").ap()
    wr_d = nc.dram_tensor("wr", [D, NE], F32, kind="ExternalInput").ap()
    br_d = nc.dram_tensor("br", [NE, 1], F32, kind="ExternalInput").ap()
    w1_d = nc.dram_tensor("w1", [NE, D, FF], F32, kind="ExternalInput").ap()
    b1_d = nc.dram_tensor("b1", [128, NE * 32], F32, kind="ExternalInput").ap()
    w2_d = nc.dram_tensor("w2", [NE, FF, D], F32, kind="ExternalInput").ap()
    b2_d = nc.dram_tensor("b2", [128, NE * 8], F32, kind="ExternalInput").ap()
    ident_d = nc.dram_tensor("ident", [128, 128], F32, kind="ExternalInput").ap()
    lstrict_d = nc.dram_tensor("lstrict", [128, 128], F32, kind="ExternalInput").ap()
    masks_d = nc.dram_tensor("masks", [3, 128, 384], F32, kind="ExternalInput").ap()
    ropecq_d = nc.dram_tensor("ropecq", [128, TOK_IN], F32, kind="ExternalInput").ap()
    ropesq_d = nc.dram_tensor("ropesq", [128, TOK_IN], F32, kind="ExternalInput").ap()
    ropeck_d = nc.dram_tensor("ropeck", [128, TOK_IN], F32, kind="ExternalInput").ap()
    ropesk_d = nc.dram_tensor("ropesk", [128, TOK_IN], F32, kind="ExternalInput").ap()
    iota8_d = nc.dram_tensor("iota8", [128, NE], F32, kind="ExternalInput").ap()
    onesr_d = nc.dram_tensor("onesr", [128, 128], F32, kind="ExternalInput").ap()

    xsort = nc.dram_tensor("xsort", [XS_ROWS, AUGW], F32, kind="Internal").ap()
    out_d = nc.dram_tensor("out", [TOK_OWN, D], F32, kind="ExternalOutput").ap()

    es_top = ExitStack()
    es_qkv = ExitStack()
    es_attn = ExitStack()
    es_p2 = ExitStack()
    es_ffn = ExitStack()

    def sem(name):
        return _Sem(es_top.enter_context(nc.semaphore(name)))

    s_ld = sem("s_ld")
    s_ldg = sem("s_ldg")
    s_xl = [sem(f"s_xl{i}") for i in range(5)]
    s_swp = sem("s_swp")
    s_xg = [sem(f"s_xg{i}") for i in range(2)]
    s_w = [sem(f"s_w{i}") for i in range(6)]
    s_g = [sem(f"s_g{i}") for i in range(2)]
    s_scat = sem("s_scat")
    s_outd = [sem(f"s_outd{i}") for i in range(2)]
    s_init = sem("s_init")
    s_b12 = sem("s_b12")

    t_tp = sem("t_tp")
    t_qk = sem("t_qk")
    t_v = sem("t_v")
    v_e = sem("v_e")
    a_e = sem("a_e")
    g_ms = sem("g_ms")
    v_rope = sem("v_rope")
    t_s = sem("t_s")
    t_mt = sem("t_mt")
    v_mr = sem("v_mr")
    t_mb = sem("t_mb")
    v_me = sem("v_me")
    a_ex = sem("a_ex")
    t_pv = sem("t_pv")
    v_rz = sem("v_rz")
    t_zb = sem("t_zb")
    a_zb = sem("a_zb")
    v_at = sem("v_at")
    t_wo = sem("t_wo")
    v_z1 = sem("v_z1")
    a_rs = sem("a_rs")
    v_x1 = sem("v_x1")
    t_x1t = sem("t_x1t")
    a_xe = sem("a_xe")
    t_sc = sem("t_sc")
    a_sc = sem("a_sc")
    t_sct = sem("t_sct")
    v_tm = sem("v_tm")
    v_oh = sem("v_oh")
    v_ro = sem("v_ro")
    t_rk = sem("t_rk")
    v_sl = sem("v_sl")
    g_io = sem("g_io")
    t_wq = sem("t_wq")
    t_f1 = sem("t_f1")
    a_g = sem("a_g")
    t_f2 = sem("t_f2")
    a_y = sem("a_y")
    t_yt = sem("t_yt")
    v_zz = sem("v_zz")
    a_r2 = sem("a_r2")
    v_ln = sem("v_ln")
    t_tp3 = sem("t_tp3")
    v_e3 = sem("v_e3")

    psum = es_top.enter_context(nc.psum_tensor("ps", [128, 4096], F32))

    def bank(k, c0=0, w=512, p0=0, p1=128):
        return psum[p0:p1, k * 512 + c0:k * 512 + c0 + w]

    def sb(es, name, shape, dt=F32):
        return es.enter_context(nc.sbuf_tensor(name, shape, dt))

    wstream01 = sb(es_top, "wstream01", [128, 2 * 8192], BF16)

    def wslot(i):
        m = i % 6
        buf = wstream01 if m < 2 else (wstream23 if m < 4 else wstream45)
        s = m % 2
        return buf[:, s * 8192:(s + 1) * 8192]

    def wq_src(i):
        ee, j = i // 8, i % 8
        if j < 4:
            return w1_d.rearrange("E (dc p) (qq f) -> E qq p dc f", p=128, qq=4)[ee, j]
        return w2_d.rearrange("E (qq fs p) dd -> E qq p fs dd", qq=4, p=128)[ee, j - 4]

    def wq_load(e, i):
        s_w[i % 6].inc(e.dma_start(
            wslot(i).rearrange("p (a b) -> p a b", a=8), wq_src(i)), 16)

    qrot = sb(es_attn, "qrot", [128, 4 * TOK_IN], BF16)
    krot = sb(es_attn, "krot", [128, 4 * TOK_IN], BF16)
    v_sb = sb(es_attn, "v_sb", [128, NT * 512], BF16)
    ident_sb = sb(es_attn, "ident_sb", [128, 128], F32)
    identb_sb = sb(es_attn, "identb_sb", [128, 128], BF16)
    onesr_sb = sb(es_attn, "onesr_sb", [128, 128], F32R)
    wr_sb = sb(es_attn, "wr_sb", [128, 8 * NE], F32)
    br_sb = sb(es_attn, "br_sb", [NE, 1], F32)
    iota8_sb = sb(es_attn, "iota8_sb", [128, NE], F32)
    lstrict_sb = sb(es_attn, "lstrict_sb", [128, 128], F32R)
    bnst = sb(es_attn, "bnst", [128, 12], F32)
    bnag = sb(es_attn, "bnag", [128, 2], F32)
    rstd_sb = sb(es_attn, "rstd_sb", [128, 1], F32)
    stm = sb(es_attn, "stm", [128, 32], F32)
    mx8 = sb(es_attn, "mx8", [128, 8], F32)
    idx8 = sb(es_attn, "idx8", [128, 8], U32)
    eidf = sb(es_attn, "eidf", [128, 1], F32)
    oh_sb = sb(es_attn, "oh_sb", [128, 8], F32R)
    runoff = sb(es_attn, "runoff", [1, 16], F32)
    rowoff = sb(es_attn, "rowoff", [1, 8], F32R)
    scr8 = sb(es_attn, "scr8", [128, 8], F32)
    ranksel = sb(es_attn, "ranksel", [128, 1], F32)
    ovf_sb = sb(es_attn, "ovf_sb", [128, 1], F32)
    scr1 = sb(es_attn, "scr1", [128, 1], F32)
    scr2 = sb(es_attn, "scr2", [128, 1], F32)
    scr3 = sb(es_attn, "scr3", [128, 1], F32)
    slotf = sb(es_attn, "slotf", [128, 1], F32)
    slotf2 = sb(es_attn, "slotf2", [128, 1], F32)
    slotu = sb(es_attn, "slotu", [128, 8], U32)
    tokidf = sb(es_attn, "tokidf", [128, 8], F32)
    initrow = sb(es_attn, "initrow", [128, AUGW], F32)
    eps_sb = sb(es_attn, "eps_sb", [128, 1], F32)
    lnv_sb = sb(es_attn, "lnv_sb", [128, 1], F32)

    wqkv_sb = sb(es_qkv, "wqkv_sb", [128, 8 * 1536], BF16)
    ropecq_sb = sb(es_qkv, "ropecq_sb", [128, TOK_IN], BF16)
    ropesq_sb = sb(es_qkv, "ropesq_sb", [128, TOK_IN], BF16)
    ropeck_sb = sb(es_qkv, "ropeck_sb", [128, TOK_IN], BF16)
    ropesk_sb = sb(es_qkv, "ropesk_sb", [128, TOK_IN], BF16)
    xload = sb(es_qkv, "xload", [128, 5 * D], BF16)
    xt_sb = sb(es_qkv, "xt_sb", [128, 8 * 512], BF16)
    qkraw = sb(es_qkv, "qkraw", [128, 8 * 512], BF16)
    qswap = sb(es_qkv, "qswap", [128, 8 * 512], BF16)
    rt1 = sb(es_qkv, "rt1", [128, 512], BF16)
    rt2 = sb(es_qkv, "rt2", [128, 512], BF16)

    with nc.Block() as blk:
        @blk.sync
        def _(e):
            s_ld.inc(e.dma_start(wr_sb[:].rearrange("p (dc e) -> p dc e", dc=8),
                                 wr_d.rearrange("(dc p) e -> p dc e", p=128)), 16)
            s_ld.inc(e.dma_start(br_sb[:], br_d), 16)
            s_ld.inc(e.dma_start(ident_sb[:], ident_d), 16)
            s_ld.inc(e.dma_start(lstrict_sb[:], lstrict_d.bitcast(F32R)), 16)
            s_ld.inc(e.dma_start(iota8_sb[:], iota8_d), 16)
            s_ld.inc(e.dma_start(onesr_sb[:], onesr_d.bitcast(F32R)), 16)

        @blk.gpsimd
        def _(e):
            e.dma_start(wqkv_sb[:].rearrange("p (dc f) -> p dc f", dc=8),
                        wqkv.rearrange("(dc p) f -> p dc f", p=128)).then_inc(s_ldg.h, 16)
            s_ldg.n += 16
            e.dma_start(identb_sb[:], ident_d).then_inc(s_ldg.h, 16)
            s_ldg.n += 16
            for t in range(5):
                s_xl[t].inc(e.dma_start(xload[:, t * D:(t + 1) * D],
                                        x_in[t * 128:(t + 1) * 128, :]), 16)
            s_ldg.inc(e.dma_start(ropecq_sb[:], ropecq_d), 16)
            s_ldg.inc(e.dma_start(ropesq_sb[:], ropesq_d), 16)
            s_ldg.inc(e.dma_start(ropeck_sb[:], ropeck_d), 16)
            s_ldg.inc(e.dma_start(ropesk_sb[:], ropesk_d), 16)
            e.memset(runoff[:], 0.0)
            e.memset(eps_sb[:], 1.0e-5)
            e.memset(initrow[:, 0:1024], 0.0)
            g_ms.inc(e.memset(initrow[:, 1024:AUGW], 1.0e9))

    for gi, (t0, ntl) in enumerate(GRP):
        gw = ntl * 128
        gtok = t0 * 128
        nrounds = 2 * ntl
        tp0, ve0, ae0, qk0, tv0 = t_tp.n, v_e.n, a_e.n, t_qk.n, t_v.n
        swp_target = s_swp.n + 64

        with nc.Block() as blk:
            @blk.scalar
            def _(e, gi=gi, gw=gw, qk0=qk0, ae0=ae0):
                for fc in range(8):
                    e.wait_ge(t_qk.h, qk0 + fc + 1)
                    a_e.inc(e.copy(qkraw[:, fc * 512:fc * 512 + gw], bank(2 + (fc % 4), 0, gw)))
                e.wait_ge(a_e.h, ae0 + 8)
                for bi in range(4):
                    src_lo = [32, 0, 96, 64][bi]
                    s_swp.inc(e.dma_start(qswap[bi * 32:(bi + 1) * 32, 0:8 * 512],
                                          qkraw[src_lo:src_lo + 32, 0:8 * 512]), 16)

            @blk.tensor
            def _(e, gi=gi, t0=t0, ntl=ntl, gw=gw, nrounds=nrounds, tp0=tp0, ve0=ve0,
                  ae0=ae0):
                if gi == 0:
                    e.wait_ge(s_ld.h, 6 * 16)
                    e.wait_ge(s_ldg.h, 6 * 16)
                    e.wait_ge(g_ms.h, 1)
                for r in range(nrounds):
                    t = t0 + r // 2
                    half = r % 2
                    e.wait_ge(s_xl[t % 5].h, 16 * (t // 5 + 1))
                    if r >= 2:
                        e.wait_ge(v_e.h, ve0 + (r - 1))
                    for j in range(4):
                        dc = half * 4 + j
                        ii = e.transpose(bank(r % 2, j * 64, 64).bitcast(BF16),
                                         xload[:, (t % 5) * D + dc * 128:(t % 5) * D + (dc + 1) * 128],
                                         identb_sb[:])
                    t_tp.inc(ii)
                e.wait_ge(v_e.h, ve0 + nrounds)
                for fc in range(8):
                    bk = 2 + (fc % 4)
                    if fc >= 4:
                        e.wait_ge(a_e.h, ae0 + (fc - 4) + 1)
                    for dc in range(8):
                        ii = e.matmul(bank(bk, 0, gw),
                                      wqkv_sb[:, dc * 1536 + fc * 128:dc * 1536 + (fc + 1) * 128],
                                      xt_sb[:, dc * 512:dc * 512 + gw],
                                      start=(dc == 0), stop=(dc == 7))
                    t_qk.inc(ii)
                for k in range(ntl):
                    bk = 6 + (k % 2)
                    if k >= 2:
                        e.wait_ge(v_e.h, ve0 + nrounds + (k - 2) + 1)
                    for dc in range(8):
                        ii = e.matmul(bank(bk, 0, 512),
                                      xt_sb[:, dc * 512 + k * 128:dc * 512 + (k + 1) * 128],
                                      wqkv_sb[:, dc * 1536 + 1024:dc * 1536 + 1536],
                                      start=(dc == 0), stop=(dc == 7))
                    t_v.inc(ii)

            @blk.vector
            def _(e, t0=t0, ntl=ntl, gw=gw, gtok=gtok, nrounds=nrounds, tp0=tp0,
                  tv0=tv0, swp_target=swp_target):
                for r in range(nrounds):
                    half = r % 2
                    toff = (r // 2) * 128
                    e.wait_ge(t_tp.h, tp0 + r + 1)
                    dst = xt_sb[:].rearrange("p (dc t) -> p dc t", dc=8)[
                        :, 4 * half:4 * half + 4, toff:toff + 128]
                    v_e.inc(e.tensor_copy(dst, bank(r % 2, 0, 256).bitcast(BF16).rearrange("p (j t) -> p j t", j=4)))
                for k in range(ntl):
                    e.wait_ge(t_v.h, tv0 + k + 1)
                    v_e.inc(e.tensor_copy(v_sb[:, (t0 + k) * 512:(t0 + k + 1) * 512],
                                          bank(6 + (k % 2), 0, 512)))
                e.wait_ge(s_swp.h, swp_target)
                for c in range(8):
                    e.wait_ge(v_rope.h, v_rope.n)
                    src = qkraw[:, c * 512:c * 512 + gw]
                    swp = qswap[:, c * 512:c * 512 + gw]
                    cos = (ropecq_sb if c < 4 else ropeck_sb)[:, gtok:gtok + gw]
                    sin = (ropesq_sb if c < 4 else ropesk_sb)[:, gtok:gtok + gw]
                    cc = c % 4
                    dstt = qrot if c < 4 else krot
                    dst = dstt[:, cc * TOK_IN + gtok:cc * TOK_IN + gtok + gw]
                    e.tensor_tensor(rt1[:, 0:gw], src, cos, op=ALU.mult)
                    v_rope.inc(e.tensor_tensor(rt2[:, 0:gw], swp, sin, op=ALU.mult))
                    e.wait_ge(v_rope.h, v_rope.n)
                    v_rope.inc(e.tensor_tensor(dst, rt1[:, 0:gw], rt2[:, 0:gw], op=ALU.add))

            @blk.gpsimd
            def _(e, gi=gi):
                for t in range(4 * gi + 5, min(4 * gi + 9, NT)):
                    e.wait_ge(t_tp.h, 2 * (t - 5) + 2)
                    s_xl[t % 5].inc(e.dma_start(xload[:, (t % 5) * D:(t % 5 + 1) * D],
                                                x_in[t * 128:(t + 1) * 128, :]), 16)
                n = [5, 5, 5, 5, 4][gi]
                done = [0, 5, 10, 15, 20][gi]
                if gi == 0:
                    e.wait_ge(g_ms.h, 1)
                for k in range(n):
                    r0 = (done + k) * 128
                    s_init.inc(e.dma_start(xsort[r0:r0 + 128, :], initrow[:]), 16)

    es_qkv.close()

    masks_sb = sb(es_p2, "masks_sb", [128, 3 * 384], BF16)
    wout_sb = sb(es_p2, "wout_sb", [128, 8 * INNER], BF16)
    e_sb = sb(es_p2, "e_sb", [128, 2 * 1536], BF16)
    e2_sb = sb(es_p2, "e2_sb", [128, 2 * 1536], BF16)
    pt_sb = sb(es_p2, "pt_sb", [128, 2 * 1536], BF16)
    zacc_sb = sb(es_p2, "zacc_sb", [128, 8], F32)
    attnT = sb(es_p2, "attnT", [128, 512], BF16)
    invz_sb = sb(es_p2, "invz_sb", [128, 8], F32)
    xg_sb = sb(es_p2, "xg_sb", [128, 2 * D], F32)
    z1_sb = sb(es_p2, "z1_sb", [128, D], F32)
    x1aug = sb(es_p2, "x1aug", [128, 6 * AUGW], F32)
    x1T = sb(es_p2, "x1T", [128, 8 * 512], F32)
    scev = sb(es_p2, "scev", [8, 512], F32)

    with nc.Block() as blk:
        @blk.scalar
        def _(e):
            for q in range(2):
                s_xg[q].inc(e.dma_start(xg_sb[:, q * D:(q + 1) * D],
                                        x_in[(q + 2) * 128:(q + 3) * 128, :]), 16)

        @blk.gpsimd
        def _(e):
            s_ldg.inc(e.dma_start(masks_sb[:].rearrange("p (c k) -> p c k", c=3),
                                 masks_d.rearrange("c p k -> p c k")), 16)
            s_ldg.inc(e.dma_start(wout_sb[:].rearrange("p (c f) -> p c f", c=4),
                                  wout.rearrange("(c p) f -> p c f", p=128)), 16)
            for j in range(6):
                ii = e.memset(x1aug[:, j * AUGW + 1024:j * AUGW + 1028], 0.0)
            g_ms.inc(ii)
            for i in range(2):
                wq_load(e, i)

    def cls_of(t):
        return 1 if t == 2 else (2 if t == 3 else 0)

    def p2_S(e, q):
        t = q + 2
        cls = cls_of(t)
        if q == 0:
            e.wait_ge(s_ld.h, 6 * 16)
            e.wait_ge(s_ldg.h, 8 * 16)
        for w in range(2):
            if w == 1:
                e.wait_ge(a_ex.h, 8 * q + 4)
            elif q >= 1:
                e.wait_ge(a_ex.h, 8 * q)
                e.wait_ge(a_zb.h, 2 * q)
            for j in range(4):
                h = 4 * w + j
                pb = (h % 2) * 64
                ch = h // 2
                e.matmul(bank(j, 0, 384),
                         qrot[pb:pb + 64, ch * TOK_IN + t * 128:ch * TOK_IN + (t + 1) * 128],
                         krot[pb:pb + 64, ch * TOK_IN + (t - 2) * 128:ch * TOK_IN + (t + 1) * 128],
                         start=True, stop=False)
            for j in range(4):
                ii = e.matmul(bank(j, 0, 384), identb_sb[:],
                              masks_sb[:, cls * 384:(cls + 1) * 384],
                              start=False, stop=True)
            t_s.inc(ii)

    def p2_PT(e, q):
        P = q % 2
        for w in range(2):
            if q >= 2:
                e.wait_ge(a_xe.h, 2 * q - 2)
            if w == 1:
                e.wait_ge(v_at.h, 2 * q + 1)
            for j in range(4):
                e.wait_ge(v_me.h, 8 * q + 4 * w + j + 1)
                for b in range(3):
                    c0b = (j % 2) * 384 + b * 128
                    ii = e.transpose(
                        bank(4 + 2 * P + j // 2, c0b // 2, 64).bitcast(BF16),
                        e2_sb[:, w * 1536 + j * 384 + b * 128:w * 1536 + j * 384 + (b + 1) * 128],
                        identb_sb[:])
                t_mt.inc(ii)

    def p2_PTPV(e, q, skip_pt=False):
        t = q + 2
        P = q % 2
        if not skip_pt:
            p2_PT(e, q)
        for w in range(2):
            e.wait_ge(a_ex.h, 8 * q + 8)
            e.wait_ge(v_at.h, 2 * q + w + 1)
            if w == 0 and q % 4 == 1 and q >= 5:
                e.wait_ge(a_sc.h, q // 4)
                e.wait_ge(v_tm.h, 4 * (q // 4))
            if w == 1:
                e.wait_ge(a_zb.h, 2 * q + 1)
            for j in range(4):
                h = 4 * w + j
                p0 = (j % 2) * 64
                for b in range(3):
                    ii = e.matmul(bank(j // 2, 384, 128, p0=p0, p1=p0 + 64),
                                  v_sb[:, (t - 2 + b) * 512 + h * 64:(t - 2 + b) * 512 + (h + 1) * 64],
                                  pt_sb[:, w * 1536 + j * 384 + b * 128:w * 1536 + j * 384 + (b + 1) * 128],
                                  start=(b == 0), stop=(b == 2))
            t_pv.inc(ii)

    def p2_Wout(e, q):
        P = q % 2
        e.wait_ge(a_zb.h, 2 * q + 2)
        for dh in range(2):
            for c in range(4):
                ii = e.matmul(bank(4 + 2 * P + dh, 0, 512),
                              attnT[:, c * 128:(c + 1) * 128],
                              wout_sb[:, c * 1024 + dh * 512:c * 1024 + (dh + 1) * 512],
                              start=(c == 0), stop=(c == 3))
            t_wo.inc(ii)

    def p2_x1T(e, q):
        P = q % 2
        e.wait_ge(v_x1.h, q + 1)
        for half in range(2):
            for j in range(4):
                dc = 4 * half + j
                ii = e.transpose(bank(4 + 2 * P + half, j * 128, 128),
                                 x1aug[:, (q % 6) * AUGW + dc * 128:(q % 6) * AUGW + (dc + 1) * 128],
                                 ident_sb[:])
            t_x1t.inc(ii)

    def p2_softmax_v(e, q):
        P = q % 2
        for w in range(2):
            e.wait_ge(a_ex.h, 8 * q + 4 * (w + 1))
            if q >= 1 and w == 0:
                e.wait_ge(v_me.h, 8 * q)
            v_rz.inc(e.reciprocal(invz_sb[:, 4 * w:4 * w + 4],
                                  zacc_sb[:, 4 * w:4 * w + 4]))
            e.wait_ge(v_rz.h, 2 * q + w + 1)
            for j in range(4):
                v_me.inc(e.tensor_scalar(
                    e2_sb[:, w * 1536 + j * 384:w * 1536 + (j + 1) * 384],
                    e_sb[:, w * 1536 + j * 384:w * 1536 + (j + 1) * 384],
                    invz_sb[:, 4 * w + j:4 * w + j + 1], None, op0=ALU.mult))
    def p2_evacpt_v(e, q):
        P = q % 2
        for w in range(2):
            e.wait_ge(t_mt.h, 8 * q + 4 * (w + 1))
            if w == 0:
                e.wait_ge(t_pv.h, 2 * q)
            srcv = psum[:, (4 + 2 * P) * 512:(6 + 2 * P) * 512].bitcast(BF16)\
                .rearrange("p (bk c) -> p bk c", bk=2)[:, :, 0:768]
            dstv = pt_sb[:, w * 1536:(w + 1) * 1536].rearrange("p (bk c) -> p bk c", bk=2)
            v_at.inc(e.tensor_copy(dstv, srcv))

    def p2_evacat_v(e, q):
        for w in range(2):
            e.wait_ge(t_pv.h, 2 * q + w + 1)
            if w == 0 and q >= 1:
                e.wait_ge(t_wo.h, 2 * q)
            srct = psum[:, 0:1024].rearrange("p (j c) -> p j c", j=2)[:, :, 384:512]
            a_zb.inc(e.tensor_copy(
                attnT[:, w * 256:(w + 1) * 256].rearrange("p (j c) -> p j c", j=2), srct))

    def p2_ln1_v(e, q):
        P = q % 2
        for dh in range(2):
            e.wait_ge(t_wo.h, 2 * q + dh + 1)
            e.wait_ge(s_xg[q % 2].h, 16 * (q // 2 + 1))
            if dh == 0 and q >= 1:
                e.wait_ge(v_x1.h, q)
            v_z1.inc(e.tensor_tensor(z1_sb[:, dh * 512:(dh + 1) * 512],
                                     xg_sb[:, (q % 2) * D + dh * 512:(q % 2) * D + (dh + 1) * 512],
                                     bank(4 + 2 * P + dh, 0, 512), op=ALU.add))
        e.wait_ge(v_z1.h, 4 * q + 2)
        e.bn_stats(bnst[:, 0:6], z1_sb[:, 0:512])
        v_z1.inc(e.bn_stats(bnst[:, 6:12], z1_sb[:, 512:1024]))
        e.wait_ge(v_z1.h, 4 * q + 3)
        if q >= 1:
            e.wait_ge(a_rs.h, 2 * q)
        v_z1.inc(e.bn_aggr(bnag[:], bnst[:]))
        e.wait_ge(a_rs.h, 2 * (q + 1))
        if q >= 6:
            e.wait_ge(s_scat.h, 16 * (q - 5))
        v_x1.inc(e.tensor_scalar(x1aug[:, (q % 6) * AUGW:(q % 6) * AUGW + 1024],
                                 z1_sb[:], bnag[:, 0:1], rstd_sb[:],
                                 op0=ALU.subtract, op1=ALU.mult))

    def p2_exps_a(e, q):
        for w in range(2):
            e.wait_ge(t_s.h, 2 * q + w + 1)
            if w == 0 and q >= 1:
                e.wait_ge(v_me.h, 8 * q)
            for j in range(4):
                a_ex.inc(e.activation(
                    e_sb[:, w * 1536 + j * 384:w * 1536 + (j + 1) * 384],
                    bank(j, 0, 384), AF.Exp,
                    accum_out=zacc_sb[:, 4 * w + j:4 * w + j + 1]))

    def p2_tail_a(e, q):
        e.wait_ge(v_z1.h, 4 * q + 4)
        a_rs.inc(e.activation(lnv_sb[:], bnag[:, 1:2], AF.Ln, bias=eps_sb[:]))
        e.wait_ge(a_rs.h, a_rs.n)
        if q >= 1:
            e.wait_ge(v_x1.h, q)
        a_rs.inc(e.activation(rstd_sb[:], lnv_sb[:], AF.Exp, scale=-0.5))

    def p2_x1e_a(e, q):
        P = q % 2
        if q >= 4 and q % 4 == 0:
            e.wait_ge(t_sc.h, q // 4)
        for half in range(2):
            e.wait_ge(t_x1t.h, 2 * q + half + 1)
            dst = x1T[:].rearrange("p (dc x) -> p dc x", dc=8)[
                :, 4 * half:4 * half + 4, (q % 4) * 128:(q % 4) * 128 + 128]
            a_xe.inc(e.copy(dst, bank(4 + 2 * P + half, 0, 512).rearrange("p (j x) -> p j x", j=4)))
        if q + 2 < NQT:
            s_xg[(q + 2) % 2].inc(
                e.dma_start(xg_sb[:, ((q + 2) % 2) * D:((q + 2) % 2 + 1) * D],
                            x_in[(q + 4) * 128:(q + 5) * 128, :]), 16)

    def router_pe(e, g):
        g0 = 4 * g
        e.wait_ge(a_xe.h, 8 * (g + 1))
        e.wait_ge(a_ex.h, min(8 * (g0 + 5) + 8, 8 * 16))
        for dc in range(8):
            ii = e.matmul(bank(0, 0, 512, p0=0, p1=8),
                          wr_sb[:, dc * 8:(dc + 1) * 8],
                          x1T[:, dc * 512:(dc + 1) * 512],
                          start=(dc == 0), stop=(dc == 7))
        t_sc.inc(ii)
        e.wait_ge(a_sc.h, g + 1)
        for k in range(4):
            ii = e.transpose(bank(1, k * 128, 8, p0=0, p1=128),
                             scev[0:8, k * 128:(k + 1) * 128], ident_sb[0:8, 0:8])
        t_sct.inc(ii)
        for k in range(4):
            qq = g0 + k
            e.wait_ge(v_oh.h, 4 * qq + 4)
            e.wait_ge(v_ro.h, qq + 1)
            if k == 0 and g >= 1:
                e.wait_ge(v_sl.h, 7 * (g0 - 1) + 2)
            e.matmul(bank(2, 384 + k * 8, 8, p0=0, p1=128), lstrict_sb[:],
                     oh_sb[:], start=True, stop=False)
            e.matmul(bank(2, 384 + k * 8, 8, p0=0, p1=128), onesr_sb[0:1, 0:128],
                     rowoff[:], start=False, stop=True)
            t_rk.inc(e.matmul(bank(2, 416 + k * 8, 8, p0=0, p1=1),
                              onesr_sb[:, 0:1], oh_sb[:],
                              start=True, stop=True))

    def router_sc(e, g):
        e.wait_ge(t_sc.h, g + 1)
        a_sc.inc(e.activation(scev[:], bank(0, 0, 512, p0=0, p1=8),
                              AF.Identity, bias=br_sb[:]))

    def router_vec(e, g):
        g0 = 4 * g
        e.wait_ge(t_sct.h, g + 1)
        for k in range(4):
            v_tm.inc(e.tensor_copy(stm[:, k * 8:(k + 1) * 8],
                                   bank(1, k * 128, 8, p0=0, p1=128)))
        for k in range(4):
            qq = g0 + k
            e.wait_ge(v_tm.h, 4 * g + 4)
            if qq > 0:
                e.wait_ge(t_rk.h, qq)
            v_oh.inc(e.max(mx8[:], stm[:, k * 8:(k + 1) * 8]))
            e.wait_ge(v_oh.h, 4 * qq + 1)
            v_oh.inc(e.max_index(idx8[:], mx8[:], stm[:, k * 8:(k + 1) * 8]))
            e.wait_ge(v_oh.h, 4 * qq + 2)
            v_oh.inc(e.tensor_copy(eidf[:], idx8[:, 0:1]))
            e.wait_ge(v_oh.h, 4 * qq + 3)
            v_oh.inc(e.tensor_scalar(oh_sb[:], iota8_sb[:], eidf[:], None,
                                     op0=ALU.is_equal))
            if qq > 0:
                e.wait_ge(v_sl.h, 7 * qq - 5)
            v_ro.inc(e.tensor_copy(rowoff[:], runoff[:, (qq % 2) * 8:(qq % 2) * 8 + 8]))
            e.wait_ge(t_rk.h, qq + 1)
            v_sl.inc(e.tensor_tensor(scr8[:], oh_sb[:].bitcast(F32),
                                     bank(2, 384 + k * 8, 8, p0=0, p1=128),
                                     op=ALU.mult))
            e.wait_ge(v_sl.h, 7 * qq + 1)
            e.reduce_sum(ranksel[:], scr8[:], axis=AX.X)
            v_sl.inc(e.tensor_tensor(runoff[:, ((qq + 1) % 2) * 8:((qq + 1) % 2) * 8 + 8],
                                     runoff[:, (qq % 2) * 8:(qq % 2) * 8 + 8],
                                     bank(2, 416 + k * 8, 8, p0=0, p1=1),
                                     op=ALU.add))
            e.wait_ge(v_sl.h, 7 * qq + 2)
            e.tensor_scalar(ovf_sb[:], ranksel[:], float(CAP), None, op0=ALU.is_ge)
            v_sl.inc(e.tensor_scalar(scr1[:], eidf[:], float(CAP), None, op0=ALU.mult))
            e.wait_ge(v_sl.h, 7 * qq + 3)
            e.wait_ge(g_io.h, qq + 1)
            e.tensor_tensor(slotf[:], ranksel[:], scr1[:], op=ALU.add)
            v_sl.inc(e.tensor_scalar(scr2[:], tokidf[:, qq % 8:qq % 8 + 1],
                                     float(TRASH), None, op0=ALU.add))
            e.wait_ge(v_sl.h, 7 * qq + 4)
            v_sl.inc(e.tensor_tensor(scr3[:], scr2[:], slotf[:], op=ALU.subtract))
            e.wait_ge(v_sl.h, 7 * qq + 5)
            v_sl.inc(e.tensor_scalar(slotf2[:], scr3[:], ovf_sb[:], slotf[:],
                                     op0=ALU.mult, op1=ALU.add))
            e.wait_ge(v_sl.h, 7 * qq + 6)
            if g >= 2:
                e.wait_ge(s_scat.h, 16 * 4 * (g - 1))
            v_sl.inc(e.tensor_copy(slotu[:, (g % 2) * 4 + k:(g % 2) * 4 + k + 1],
                                   slotf2[:]))

    def router_gps(e, g):
        for k in range(4):
            qq = 4 * g + k
            e.wait_ge(v_sl.h, 7 * qq + 7)
            e.wait_ge(v_x1.h, qq + 1)
            if qq == 0:
                e.wait_ge(s_init.h, 24 * 16)
                e.wait_ge(g_ms.h, 2)
            s_scat.inc(e.indirect_dma_start(
                xsort[:, :],
                bass.IndirectOffsetOnAxis(ap=slotu[:, (g % 2) * 4 + k:(g % 2) * 4 + k + 1], axis=0),
                x1aug[:, (qq % 6) * AUGW:(qq % 6) * AUGW + AUGW], None), 16)

    def rtr(q):
        return q >= 5 and q % 4 == 1

    with nc.Block() as blk:
        @blk.tensor
        def _(e):
            for q in range(18):
                if 1 <= q <= 16:
                    p2_PTPV(e, q - 1, skip_pt=rtr(q - 1))
                if q <= 15:
                    p2_S(e, q)
                if 1 <= q <= 16:
                    p2_Wout(e, q - 1)
                if 2 <= q <= 17:
                    p2_x1T(e, q - 2)
                if rtr(q):
                    if q <= 15:
                        p2_PT(e, q)
                    router_pe(e, (q - 5) // 4)

        @blk.vector
        def _(e):
            for q in range(18):
                if q <= 15:
                    p2_softmax_v(e, q)
                if 1 <= q <= 16:
                    p2_ln1_v(e, q - 1)
                if q <= 15:
                    p2_evacpt_v(e, q)
                if rtr(q):
                    router_vec(e, (q - 5) // 4)
                if q <= 15:
                    p2_evacat_v(e, q)

        @blk.scalar
        def _(e):
            for q in range(18):
                if q <= 15:
                    p2_exps_a(e, q)
                if 1 <= q <= 16:
                    p2_tail_a(e, q - 1)
                if 2 <= q <= 17:
                    p2_x1e_a(e, q - 2)
                if rtr(q):
                    router_sc(e, (q - 5) // 4)

        @blk.gpsimd
        def _(e):
            for q in range(18):
                if q <= 15:
                    if q >= 6:
                        e.wait_ge(s_scat.h, 16 * (q - 5))
                    e.iota(x1aug[:, (q % 6) * AUGW + 1024:(q % 6) * AUGW + 1025], [[1, 1]],
                           base=q * 128, channel_multiplier=1,
                           allow_small_or_imprecise_dtypes=True)
                    g_io.inc(e.iota(tokidf[:, q % 8:q % 8 + 1], [[1, 1]], base=q * 128,
                                    channel_multiplier=1,
                                    allow_small_or_imprecise_dtypes=True))
                if rtr(q):
                    router_gps(e, (q - 5) // 4)

    es_p2.close()
    es_attn.close()

    wstream23 = sb(es_ffn, "wstream23", [128, 2 * 8192], BF16)
    wstream45 = sb(es_ffn, "wstream45", [128, 2 * 8192], BF16)
    b1_sb = sb(es_ffn, "b1_sb", [128, NE * 32], F32)
    b2_sb = sb(es_ffn, "b2_sb", [128, NE * 8], F32)
    x_sel = sb(es_ffn, "x_sel", [128, 2 * NRT * AUGW], F32)
    xselT = sb(es_ffn, "xselT", [128, 8 * TOKW], BF16)
    hT = sb(es_ffn, "hT", [128, 32 * TOKW], BF16)
    yT = sb(es_ffn, "yT", [128, 8 * TOKW], BF16)
    z_sb = sb(es_ffn, "z_sb", [128, AUGW], F32)
    zn_sb = sb(es_ffn, "zn_sb", [128, 2 * 1024], F32)
    bnst2 = sb(es_ffn, "bnst2", [128, 12], F32)
    bnag2 = sb(es_ffn, "bnag2", [128, 4], F32)
    rstd2 = sb(es_ffn, "rstd2", [128, 2], F32)
    slotO = sb(es_ffn, "slotO", [128, 2], U32)
    ident2_sb = sb(es_ffn, "ident2_sb", [128, 128], F32)
    identb2_sb = sb(es_ffn, "identb2_sb", [128, 128], BF16)
    eps3_sb = sb(es_ffn, "eps3_sb", [128, 1], F32)
    rx_sb = sb(es_ffn, "rx_sb", [128, 1], F32)
    rf_sb = sb(es_ffn, "rf_sb", [128, 1], F32)
    ru_sb = sb(es_ffn, "ru_sb", [128, 1], F32)
    ry_sb = sb(es_ffn, "ry_sb", [128, 1], F32)
    rt1_sb = sb(es_ffn, "rt1_sb", [128, 1], F32)

    XS4 = NRT * AUGW

    with nc.Block() as blk:
        @blk.sync
        def _(e):
            e.wait_ge(s_scat.h, 16 * 16)
            s_b12.inc(e.dma_start(b1_sb[:], b1_d), 16)
            s_b12.inc(e.dma_start(b2_sb[:], b2_d), 16)
            s_ld.inc(e.dma_start(ident2_sb[:], ident_d), 16)

        @blk.scalar
        def _(e):
            e.wait_ge(s_scat.h, 16 * 16)
            for ee in range(2):
                for rt in range(NRT):
                    s_g[ee % 2].inc(e.dma_start(
                        x_sel[:, ee * XS4 + rt * AUGW:ee * XS4 + (rt + 1) * AUGW],
                        xsort[ee * CAP + rt * 128:ee * CAP + (rt + 1) * 128, :]), 16)

        @blk.gpsimd
        def _(e):
            e.wait_ge(s_scat.h, 16 * 16)
            s_ldg.inc(e.dma_start(identb2_sb[:], ident_d), 16)
            for i in range(2, 6):
                wq_load(e, i)
            g_ms.inc(e.memset(eps3_sb[:], 1.0e-5))

    def p3_gelu_a(e, ee, fcg):
        e.wait_ge(t_f1.h, 32 * ee + fcg + 1)
        if fcg == 0:
            if ee == 0:
                e.wait_ge(s_b12.h, 32)
                e.wait_ge(g_ms.h, 3)
            else:
                e.wait_ge(t_f2.h, 4 * ee)
        a_g.inc(e.activation(hT[:, fcg * TOKW:(fcg + 1) * TOKW],
                             bank(2 + (fcg % 2), 0, TOKW), AF.Gelu,
                             bias=b1_sb[:, ee * 32 + fcg:ee * 32 + fcg + 1]))

    with nc.Block() as blk:
        @blk.scalar
        def _(e):
            for fcg in range(32):
                p3_gelu_a(e, 0, fcg)
            for ee in range(NE):
                for dc in range(8):
                    e.wait_ge(t_f2.h, 4 * (ee + 1))
                    if dc == 0:
                        e.wait_ge(t_yt.h, NRT * ee)
                    a_y.inc(e.activation(yT[:, dc * TOKW:(dc + 1) * TOKW],
                                         bank(dc, 0, TOKW), AF.Identity,
                                         bias=b2_sb[:, ee * 8 + dc:ee * 8 + dc + 1]))
                if ee + 1 < NE:
                    for fcg in range(32):
                        p3_gelu_a(e, ee + 1, fcg)
                if ee + 2 < NE:
                    e.wait_ge(v_ln.h, NRT * (ee + 1))
                    par2 = (ee + 2) % 2
                    for rt in range(NRT):
                        s_g[par2].inc(e.dma_start(
                            x_sel[:, par2 * XS4 + rt * AUGW:par2 * XS4 + (rt + 1) * AUGW],
                            xsort[(ee + 2) * CAP + rt * 128:(ee + 2) * CAP + (rt + 1) * 128, :]), 16)

        @blk.tensor
        def _(e):
            for ee in range(NE):
                par = ee % 2
                if ee == 0:
                    e.wait_ge(s_ld.h, 7 * 16)
                    e.wait_ge(s_ldg.h, 9 * 16)
                e.wait_ge(s_g[ee % 2].h, 16 * NRT * (ee // 2 + 1))
                if ee >= 1:
                    e.wait_ge(a_y.h, 8 * ee - 4)
                for rt in range(NRT):
                    if rt >= 2:
                        e.wait_ge(v_e3.h, 2 * NRT * ee + 2 * (rt - 1))
                    for dc in range(8):
                        ii = e.transpose(bank(2 * (rt % 2) + dc // 4, (dc % 4) * 128, 128),
                                         x_sel[:, par * XS4 + rt * AUGW + dc * 128:par * XS4 + rt * AUGW + (dc + 1) * 128],
                                         ident2_sb[:])
                    t_tp3.inc(ii)
                e.wait_ge(v_e3.h, 2 * NRT * (ee + 1))
                for j in range(4):
                    i = 8 * ee + j
                    e.wait_ge(s_w[i % 6].h, 16 * (i // 6 + 1))
                    for fc in range(8):
                        fcg = 8 * j + fc
                        if fcg >= 2:
                            e.wait_ge(a_g.h, 32 * ee + fcg - 1)
                        for dc in range(8):
                            ii = e.matmul(bank(2 + (fcg % 2), 0, TOKW),
                                          wslot(i)[:, dc * 1024 + fc * 128:
                                                   dc * 1024 + (fc + 1) * 128],
                                          xselT[:, dc * TOKW:(dc + 1) * TOKW],
                                          start=(dc == 0), stop=(dc == 7))
                        t_f1.inc(ii)
                e.wait_ge(a_g.h, 32 * (ee + 1))
                if ee >= 1:
                    e.wait_ge(v_zz.h, 15 * (NRT * ee - 1) + 1)
                for j in range(4):
                    i = 8 * ee + 4 + j
                    e.wait_ge(s_w[i % 6].h, 16 * (i // 6 + 1))
                    for fc in range(8):
                        fcg = 8 * j + fc
                        for dc in range(8):
                            ii = e.matmul(bank(dc, 0, TOKW),
                                          wslot(i)[:, fc * 1024 + dc * 128:
                                                   fc * 1024 + (dc + 1) * 128],
                                          hT[:, fcg * TOKW:(fcg + 1) * TOKW],
                                          start=(fcg == 0), stop=(fcg == 31))
                    t_f2.inc(ii)
                e.wait_ge(a_y.h, 8 * (ee + 1))
                for rt in range(NRT):
                    if rt >= 2:
                        e.wait_ge(v_zz.h, 15 * (NRT * ee + rt - 2) + 1)
                    for dc in range(8):
                        ii = e.transpose(bank(4 + 2 * (rt % 2) + dc // 4, (dc % 4) * 64, 64).bitcast(BF16),
                                         yT[:, dc * TOKW + rt * 128:dc * TOKW + (rt + 1) * 128],
                                         identb2_sb[:])
                    t_yt.inc(ii)

        @blk.vector
        def _(e):
            for rt in range(NRT):
                e.wait_ge(t_tp3.h, rt + 1)
                for half in range(2):
                    dst = xselT[:].rearrange("p (dc x) -> p dc x", dc=8)[
                        :, 4 * half:4 * half + 4, rt * 128:(rt + 1) * 128]
                    v_e3.inc(e.tensor_copy(dst, bank(2 * (rt % 2) + half, 0, 512).rearrange("p (j x) -> p j x", j=4)))
            for ee in range(NE):
                par = ee % 2
                for rt in range(NRT):
                    R = NRT * ee + rt
                    Z0 = 15 * R
                    e.wait_ge(t_yt.h, NRT * ee + rt + 1)
                    if R >= 1:
                        e.wait_ge(v_ln.h, R)
                    for dh in range(2):
                        e.tensor_tensor(z_sb[:, dh * 512:(dh + 1) * 512],
                                        x_sel[:, par * XS4 + rt * AUGW + dh * 512:par * XS4 + rt * AUGW + (dh + 1) * 512],
                                        bank(4 + 2 * (rt % 2) + dh, 0, 256).bitcast(BF16), op=ALU.add)
                    v_zz.inc(e.tensor_copy(z_sb[:, 1024:1025],
                                           x_sel[:, par * XS4 + rt * AUGW + 1024:par * XS4 + rt * AUGW + 1025]))
                    e.wait_ge(v_zz.h, Z0 + 1)
                    e.bn_stats(bnst2[:, 0:6], z_sb[:, 0:512])
                    v_zz.inc(e.bn_stats(bnst2[:, 6:12], z_sb[:, 512:1024]))
                    e.wait_ge(v_zz.h, Z0 + 2)
                    v_zz.inc(e.bn_aggr(bnag2[:, 0:2], bnst2[:]))
                    e.wait_ge(v_zz.h, Z0 + 3)
                    v_zz.inc(e.tensor_scalar(rx_sb[:], bnag2[:, 1:2], 1.0e-5, None, op0=ALU.add))
                    e.wait_ge(v_zz.h, Z0 + 4)
                    v_zz.inc(e.tensor_copy(rf_sb[:], rx_sb[:].bitcast(U32)))
                    e.wait_ge(v_zz.h, Z0 + 5)
                    v_zz.inc(e.tensor_scalar(ru_sb[:], rf_sb[:], -0.5, 1597463007.0,
                                             op0=ALU.mult, op1=ALU.add))
                    e.wait_ge(v_zz.h, Z0 + 6)
                    v_zz.inc(e.tensor_copy(ry_sb[:].bitcast(U32), ru_sb[:]))
                    for _ in range(2):
                        e.wait_ge(v_zz.h, v_zz.n)
                        v_zz.inc(e.tensor_tensor(rt1_sb[:], ry_sb[:], ry_sb[:], op=ALU.mult))
                        e.wait_ge(v_zz.h, v_zz.n)
                        v_zz.inc(e.tensor_tensor(rt1_sb[:], rt1_sb[:], rx_sb[:], op=ALU.mult))
                        e.wait_ge(v_zz.h, v_zz.n)
                        v_zz.inc(e.tensor_scalar(rt1_sb[:], rt1_sb[:], -0.5, 1.5,
                                                 op0=ALU.mult, op1=ALU.add))
                        e.wait_ge(v_zz.h, v_zz.n)
                        v_zz.inc(e.tensor_tensor(ry_sb[:], ry_sb[:], rt1_sb[:], op=ALU.mult))
                    e.wait_ge(v_zz.h, Z0 + 15)
                    if R >= 2:
                        e.wait_ge(s_outd[R % 2].h, 16 * (R // 2))
                    e.tensor_scalar(zn_sb[:, (R % 2) * 1024:(R % 2) * 1024 + 1024],
                                    z_sb[:, 0:1024], bnag2[:, 0:1], ry_sb[:],
                                    op0=ALU.subtract, op1=ALU.mult)
                    v_ln.inc(e.tensor_copy(slotO[:, R % 2:R % 2 + 1], z_sb[:, 1024:1025]))
                if ee + 1 < NE:
                    for rt in range(NRT):
                        e.wait_ge(t_tp3.h, NRT * (ee + 1) + rt + 1)
                        for half in range(2):
                            dst = xselT[:].rearrange("p (dc x) -> p dc x", dc=8)[
                                :, 4 * half:4 * half + 4, rt * 128:(rt + 1) * 128]
                            v_e3.inc(e.tensor_copy(dst, bank(2 * (rt % 2) + half, 0, 512).rearrange("p (j x) -> p j x", j=4)))

        @blk.gpsimd
        def _(e):
            for ee in range(NE):
                for i in range(8 * ee + 6, min(8 * ee + 14, 64)):
                    ip = i - 6
                    eep, jp = ip // 8, ip % 8
                    if jp < 4:
                        e.wait_ge(t_f1.h, 32 * eep + 8 * (jp + 1))
                    else:
                        e.wait_ge(t_f2.h, 4 * eep + (jp - 4) + 1)
                    wq_load(e, i)
                for rt in range(NRT):
                    R = NRT * ee + rt
                    e.wait_ge(v_ln.h, NRT * ee + rt + 1)
                    s_outd[R % 2].inc(e.indirect_dma_start(
                        out_d[:, :],
                        bass.IndirectOffsetOnAxis(ap=slotO[:, R % 2:R % 2 + 1], axis=0),
                        zn_sb[:, (R % 2) * 1024:(R % 2) * 1024 + 1024], None,
                        bounds_check=TOK_OWN - 1, oob_is_err=False), 16)

    with nc.Block() as blk:
        @blk.gpsimd
        def _(e):
            e.wait_ge(s_outd[0].h, 16 * 12)
            e.wait_ge(s_outd[1].h, 16 * 12)

    es_ffn.close()
    es_top.close()
    return nc


def _host_consts(core):
    b, s = core // 2, core % 2
    r = np.arange(128)[:, None]
    m = np.arange(128)[None, :]
    blkA = np.where(m > r, 0.0, NEG).astype(np.float32)
    blkB = np.zeros((128, 128), np.float32)
    blkC = np.where(m <= r, 0.0, NEG).astype(np.float32)
    negb = np.full((128, 128), NEG, np.float32)
    generic = np.concatenate([blkA, blkB, blkC], axis=1)
    if s == 0:
        edge0 = np.concatenate([negb, negb, blkC], axis=1)
        edge1 = np.concatenate([negb, blkB, blkC], axis=1)
    else:
        edge0 = generic
        edge1 = generic
    masks = np.stack([generic, edge0, edge1])

    pos = (s * 2048 - 256 + np.arange(TOK_IN)).astype(np.float32)
    inv = (1.0 / 10000.0 ** (np.arange(0, 64, 2) / 64.0)).astype(np.float32)
    dmod = np.arange(128) % 64
    fidx = dmod % 32
    ang = pos[None, :] * inv[fidx][:, None]
    ropec = np.cos(ang).astype(np.float32)
    sign = np.where(dmod < 32, -1.0, 1.0).astype(np.float32)
    ropes = (sign[:, None] * np.sin(ang)).astype(np.float32)
    return masks, ropec * 0.125, ropes * 0.125, ropec, ropes


def kernel(L0_attn_x, L0_attn_Wqkv, L0_attn_Wout, L0_ffn_wr, L0_ffn_br,
           L0_ffn_w1, L0_ffn_b1, L0_ffn_w2, L0_ffn_b2):
    x = np.ascontiguousarray(L0_attn_x, dtype=np.float32)
    ident = np.eye(128, dtype=np.float32)
    lstrict = np.triu(np.ones((128, 128), np.float32), 1)
    iota8 = np.broadcast_to(np.arange(NE, dtype=np.float32), (128, NE)).copy()
    onesr = np.ones((128, 128), np.float32)
    shared = dict(
        wqkv=np.ascontiguousarray(L0_attn_Wqkv, np.float32),
        wout=np.ascontiguousarray(L0_attn_Wout, np.float32),
        wr=np.ascontiguousarray(L0_ffn_wr, np.float32),
        br=np.ascontiguousarray(L0_ffn_br, np.float32).reshape(NE, 1),
        w1=np.ascontiguousarray(L0_ffn_w1, np.float32),
        b1=np.ascontiguousarray(np.asarray(L0_ffn_b1, np.float32).reshape(NE, 32, 128).transpose(2, 0, 1).reshape(128, NE * 32)),
        w2=np.ascontiguousarray(L0_ffn_w2, np.float32),
        b2=np.ascontiguousarray(np.asarray(L0_ffn_b2, np.float32).reshape(NE, 8, 128).transpose(2, 0, 1).reshape(128, NE * 8)),
        ident=ident, lstrict=lstrict, iota8=iota8, onesr=onesr,
    )
    in_maps = []
    for c in range(8):
        b, s = c // 2, c % 2
        own = x[b, s * 2048:(s + 1) * 2048]
        halo = np.zeros((256, D), np.float32) if s == 0 else x[b, 1792:2048]
        x_c = np.concatenate([halo, own], axis=0)
        masks, rcq, rsq, rck, rsk = _host_consts(c)
        m = dict(shared)
        m.update(x_in=np.ascontiguousarray(x_c), masks=masks,
                 ropecq=rcq, ropesq=rsq, ropeck=rck, ropesk=rsk)
        in_maps.append(m)

    nc = _build_nc()
    res = run_bass_kernel_spmd(nc, in_maps, core_ids=list(range(8)))
    out = np.empty((4, 4096, D), np.float32)
    for c in range(8):
        out[c // 2, (c % 2) * 2048:(c % 2 + 1) * 2048] = res.results[c]["out"]
    return out


# version 21
# speedup vs baseline: 1.1985x; 1.0269x over previous
import numpy as np
import concourse.bass as bass
import concourse.mybir as mybir
from concourse.bass_utils import run_bass_kernel_spmd
from contextlib import ExitStack

F32 = mybir.dt.float32
F32R = mybir.dt.float32r
BF16 = mybir.dt.bfloat16
U32 = mybir.dt.uint32
AF = mybir.ActivationFunctionType
ALU = mybir.AluOpType
AX = mybir.AxisListType

D = 1024
INNER = 512
NT = 18
NQT = 16
TOK_IN = NT * 128
TOK_OWN = 2048
NE = 8
CAP = 384
TOKW = CAP
NRT = CAP // 128
FF = 4096
SORT_ROWS = NE * CAP
TRASH = SORT_ROWS
XS_ROWS = SORT_ROWS + TOK_OWN
AUGW = 1028
NEG = -1.0e30
GRP = [(0, 4), (4, 4), (8, 4), (12, 4), (16, 2)]


class _Sem:
    __slots__ = ("h", "n")

    def __init__(self, h):
        self.h = h
        self.n = 0

    def inc(self, instr, k=1):
        instr.then_inc(self.h, k)
        self.n += k
        return instr


def _build_nc():
    nc = bass.Bass("TRN2", target_bir_lowering=False)

    x_in = nc.dram_tensor("x_in", [TOK_IN, D], F32, kind="ExternalInput").ap()
    wqkv = nc.dram_tensor("wqkv", [D, 3 * INNER], F32, kind="ExternalInput").ap()
    wout = nc.dram_tensor("wout", [INNER, D], F32, kind="ExternalInput").ap()
    wr_d = nc.dram_tensor("wr", [D, NE], F32, kind="ExternalInput").ap()
    br_d = nc.dram_tensor("br", [NE, 1], F32, kind="ExternalInput").ap()
    w1_d = nc.dram_tensor("w1", [NE, D, FF], F32, kind="ExternalInput").ap()
    b1_d = nc.dram_tensor("b1", [128, NE * 32], F32, kind="ExternalInput").ap()
    w2_d = nc.dram_tensor("w2", [NE, FF, D], F32, kind="ExternalInput").ap()
    b2_d = nc.dram_tensor("b2", [128, NE * 8], F32, kind="ExternalInput").ap()
    ident_d = nc.dram_tensor("ident", [128, 128], F32, kind="ExternalInput").ap()
    lstrict_d = nc.dram_tensor("lstrict", [128, 128], F32, kind="ExternalInput").ap()
    masks_d = nc.dram_tensor("masks", [3, 128, 384], F32, kind="ExternalInput").ap()
    ropecq_d = nc.dram_tensor("ropecq", [128, TOK_IN], F32, kind="ExternalInput").ap()
    ropesq_d = nc.dram_tensor("ropesq", [128, TOK_IN], F32, kind="ExternalInput").ap()
    ropeck_d = nc.dram_tensor("ropeck", [128, TOK_IN], F32, kind="ExternalInput").ap()
    ropesk_d = nc.dram_tensor("ropesk", [128, TOK_IN], F32, kind="ExternalInput").ap()
    iota8_d = nc.dram_tensor("iota8", [128, NE], F32, kind="ExternalInput").ap()
    onesr_d = nc.dram_tensor("onesr", [128, 128], F32, kind="ExternalInput").ap()

    xsort = nc.dram_tensor("xsort", [XS_ROWS, AUGW], F32, kind="Internal").ap()
    out_d = nc.dram_tensor("out", [TOK_OWN, D], F32, kind="ExternalOutput").ap()

    es_top = ExitStack()
    es_qkv = ExitStack()
    es_attn = ExitStack()
    es_p2 = ExitStack()
    es_ffn = ExitStack()

    def sem(name):
        return _Sem(es_top.enter_context(nc.semaphore(name)))

    s_ld = sem("s_ld")
    s_ldg = sem("s_ldg")
    s_xl = [sem(f"s_xl{i}") for i in range(5)]
    s_swp = sem("s_swp")
    s_xg = [sem(f"s_xg{i}") for i in range(2)]
    s_w = [sem(f"s_w{i}") for i in range(6)]
    s_g = [sem(f"s_g{i}") for i in range(2)]
    s_scat = sem("s_scat")
    s_outd = [sem(f"s_outd{i}") for i in range(2)]
    s_init = sem("s_init")
    s_b12 = sem("s_b12")

    t_tp = sem("t_tp")
    t_qk = sem("t_qk")
    t_v = sem("t_v")
    v_e = sem("v_e")
    a_e = sem("a_e")
    g_ms = sem("g_ms")
    v_rope = sem("v_rope")
    t_s = sem("t_s")
    t_mt = sem("t_mt")
    v_mr = sem("v_mr")
    t_mb = sem("t_mb")
    v_me = sem("v_me")
    a_ex = sem("a_ex")
    t_pv = sem("t_pv")
    v_rz = sem("v_rz")
    t_zb = sem("t_zb")
    a_zb = sem("a_zb")
    v_at = sem("v_at")
    t_wo = sem("t_wo")
    v_z1 = sem("v_z1")
    a_rs = sem("a_rs")
    v_x1 = sem("v_x1")
    t_x1t = sem("t_x1t")
    a_xe = sem("a_xe")
    t_sc = sem("t_sc")
    a_sc = sem("a_sc")
    t_sct = sem("t_sct")
    v_tm = sem("v_tm")
    v_oh = sem("v_oh")
    v_ro = sem("v_ro")
    t_rk = sem("t_rk")
    v_sl = sem("v_sl")
    g_io = sem("g_io")
    t_wq = sem("t_wq")
    t_f1 = sem("t_f1")
    a_g = sem("a_g")
    t_f2 = sem("t_f2")
    a_y = sem("a_y")
    t_yt = sem("t_yt")
    v_zz = sem("v_zz")
    a_r2 = sem("a_r2")
    v_ln = sem("v_ln")
    t_tp3 = sem("t_tp3")
    v_e3 = sem("v_e3")

    psum = es_top.enter_context(nc.psum_tensor("ps", [128, 4096], F32))

    def bank(k, c0=0, w=512, p0=0, p1=128):
        return psum[p0:p1, k * 512 + c0:k * 512 + c0 + w]

    def sb(es, name, shape, dt=F32):
        return es.enter_context(nc.sbuf_tensor(name, shape, dt))

    wstream01 = sb(es_top, "wstream01", [128, 2 * 8192], BF16)

    def wslot(i):
        m = i % 6
        buf = wstream01 if m < 2 else (wstream23 if m < 4 else wstream45)
        s = m % 2
        return buf[:, s * 8192:(s + 1) * 8192]

    def wq_src(i):
        ee, j = i // 8, i % 8
        if j < 4:
            return w1_d.rearrange("E (dc p) (qq f) -> E qq p dc f", p=128, qq=4)[ee, j]
        return w2_d.rearrange("E (qq fs p) dd -> E qq p fs dd", qq=4, p=128)[ee, j - 4]

    def wq_load(e, i):
        s_w[i % 6].inc(e.dma_start(
            wslot(i).rearrange("p (a b) -> p a b", a=8), wq_src(i)), 16)

    qrot = sb(es_attn, "qrot", [128, 4 * TOK_IN], BF16)
    krot = sb(es_attn, "krot", [128, 4 * TOK_IN], BF16)
    v_sb = sb(es_attn, "v_sb", [128, NT * 512], BF16)
    ident_sb = sb(es_attn, "ident_sb", [128, 128], F32)
    identb_sb = sb(es_attn, "identb_sb", [128, 128], BF16)
    onesr_sb = sb(es_attn, "onesr_sb", [128, 128], F32R)
    wr_sb = sb(es_attn, "wr_sb", [128, 8 * NE], F32)
    br_sb = sb(es_attn, "br_sb", [NE, 1], F32)
    iota8_sb = sb(es_attn, "iota8_sb", [128, NE], F32)
    lstrict_sb = sb(es_attn, "lstrict_sb", [128, 128], F32R)
    bnst = sb(es_attn, "bnst", [128, 12], F32)
    bnag = sb(es_attn, "bnag", [128, 2], F32)
    rstd_sb = sb(es_attn, "rstd_sb", [128, 1], F32)
    stm = sb(es_attn, "stm", [128, 32], F32)
    mx8 = sb(es_attn, "mx8", [128, 8], F32)
    idx8 = sb(es_attn, "idx8", [128, 8], U32)
    eidf = sb(es_attn, "eidf", [128, 1], F32)
    oh_sb = sb(es_attn, "oh_sb", [128, 8], F32R)
    runoff = sb(es_attn, "runoff", [1, 16], F32)
    rowoff = sb(es_attn, "rowoff", [1, 8], F32R)
    scr8 = sb(es_attn, "scr8", [128, 8], F32)
    ranksel = sb(es_attn, "ranksel", [128, 1], F32)
    ovf_sb = sb(es_attn, "ovf_sb", [128, 1], F32)
    scr1 = sb(es_attn, "scr1", [128, 1], F32)
    scr2 = sb(es_attn, "scr2", [128, 1], F32)
    scr3 = sb(es_attn, "scr3", [128, 1], F32)
    slotf = sb(es_attn, "slotf", [128, 1], F32)
    slotf2 = sb(es_attn, "slotf2", [128, 1], F32)
    slotu = sb(es_attn, "slotu", [128, 8], U32)
    tokidf = sb(es_attn, "tokidf", [128, 8], F32)
    initrow = sb(es_attn, "initrow", [128, AUGW], F32)
    eps_sb = sb(es_attn, "eps_sb", [128, 1], F32)
    lnv_sb = sb(es_attn, "lnv_sb", [128, 1], F32)

    wqkv_sb = sb(es_qkv, "wqkv_sb", [128, 8 * 1536], BF16)
    ropecq_sb = sb(es_qkv, "ropecq_sb", [128, TOK_IN], BF16)
    ropesq_sb = sb(es_qkv, "ropesq_sb", [128, TOK_IN], BF16)
    ropeck_sb = sb(es_qkv, "ropeck_sb", [128, TOK_IN], BF16)
    ropesk_sb = sb(es_qkv, "ropesk_sb", [128, TOK_IN], BF16)
    xload = sb(es_qkv, "xload", [128, 5 * D], BF16)
    xt_sb = sb(es_qkv, "xt_sb", [128, 8 * 512], BF16)
    qkraw = sb(es_qkv, "qkraw", [128, 8 * 512], BF16)
    qswap = sb(es_qkv, "qswap", [128, 8 * 512], BF16)
    rt1 = sb(es_qkv, "rt1", [128, 512], BF16)
    rt2 = sb(es_qkv, "rt2", [128, 512], BF16)

    with nc.Block() as blk:
        @blk.sync
        def _(e):
            s_ld.inc(e.dma_start(wr_sb[:].rearrange("p (dc e) -> p dc e", dc=8),
                                 wr_d.rearrange("(dc p) e -> p dc e", p=128)), 16)
            s_ld.inc(e.dma_start(br_sb[:], br_d), 16)
            s_ld.inc(e.dma_start(ident_sb[:], ident_d), 16)
            s_ld.inc(e.dma_start(lstrict_sb[:], lstrict_d.bitcast(F32R)), 16)
            s_ld.inc(e.dma_start(iota8_sb[:], iota8_d), 16)
            s_ld.inc(e.dma_start(onesr_sb[:], onesr_d.bitcast(F32R)), 16)

        @blk.gpsimd
        def _(e):
            e.dma_start(wqkv_sb[:].rearrange("p (dc f) -> p dc f", dc=8),
                        wqkv.rearrange("(dc p) f -> p dc f", p=128)).then_inc(s_ldg.h, 16)
            s_ldg.n += 16
            e.dma_start(identb_sb[:], ident_d).then_inc(s_ldg.h, 16)
            s_ldg.n += 16
            for t in range(5):
                s_xl[t].inc(e.dma_start(xload[:, t * D:(t + 1) * D],
                                        x_in[t * 128:(t + 1) * 128, :]), 16)
            s_ldg.inc(e.dma_start(ropecq_sb[:], ropecq_d), 16)
            s_ldg.inc(e.dma_start(ropesq_sb[:], ropesq_d), 16)
            s_ldg.inc(e.dma_start(ropeck_sb[:], ropeck_d), 16)
            s_ldg.inc(e.dma_start(ropesk_sb[:], ropesk_d), 16)
            e.memset(runoff[:], 0.0)
            e.memset(eps_sb[:], 1.0e-5)
            e.memset(initrow[:, 0:1024], 0.0)
            g_ms.inc(e.memset(initrow[:, 1024:AUGW], 1.0e9))

    for gi, (t0, ntl) in enumerate(GRP):
        gw = ntl * 128
        gtok = t0 * 128
        nrounds = 2 * ntl
        tp0, ve0, ae0, qk0, tv0 = t_tp.n, v_e.n, a_e.n, t_qk.n, t_v.n
        swp_target = s_swp.n + 64

        with nc.Block() as blk:
            @blk.scalar
            def _(e, gi=gi, gw=gw, qk0=qk0, ae0=ae0):
                for fc in range(8):
                    e.wait_ge(t_qk.h, qk0 + fc + 1)
                    a_e.inc(e.copy(qkraw[:, fc * 512:fc * 512 + gw], bank(2 + (fc % 4), 0, gw)))
                e.wait_ge(a_e.h, ae0 + 8)
                for bi in range(4):
                    src_lo = [32, 0, 96, 64][bi]
                    s_swp.inc(e.dma_start(qswap[bi * 32:(bi + 1) * 32, 0:8 * 512],
                                          qkraw[src_lo:src_lo + 32, 0:8 * 512]), 16)

            @blk.tensor
            def _(e, gi=gi, t0=t0, ntl=ntl, gw=gw, nrounds=nrounds, tp0=tp0, ve0=ve0,
                  ae0=ae0):
                if gi == 0:
                    e.wait_ge(s_ld.h, 6 * 16)
                    e.wait_ge(s_ldg.h, 6 * 16)
                    e.wait_ge(g_ms.h, 1)
                for r in range(nrounds):
                    t = t0 + r // 2
                    half = r % 2
                    e.wait_ge(s_xl[t % 5].h, 16 * (t // 5 + 1))
                    if r >= 2:
                        e.wait_ge(v_e.h, ve0 + (r - 1))
                    for j in range(4):
                        dc = half * 4 + j
                        ii = e.transpose(bank(r % 2, j * 64, 64).bitcast(BF16),
                                         xload[:, (t % 5) * D + dc * 128:(t % 5) * D + (dc + 1) * 128],
                                         identb_sb[:])
                    t_tp.inc(ii)
                e.wait_ge(v_e.h, ve0 + nrounds)
                for fc in range(8):
                    bk = 2 + (fc % 4)
                    if fc >= 4:
                        e.wait_ge(a_e.h, ae0 + (fc - 4) + 1)
                    for dc in range(8):
                        ii = e.matmul(bank(bk, 0, gw),
                                      wqkv_sb[:, dc * 1536 + fc * 128:dc * 1536 + (fc + 1) * 128],
                                      xt_sb[:, dc * 512:dc * 512 + gw],
                                      start=(dc == 0), stop=(dc == 7))
                    t_qk.inc(ii)
                for k in range(ntl):
                    bk = 6 + (k % 2)
                    if k >= 2:
                        e.wait_ge(v_e.h, ve0 + nrounds + (k - 2) + 1)
                    for dc in range(8):
                        ii = e.matmul(bank(bk, 0, 512),
                                      xt_sb[:, dc * 512 + k * 128:dc * 512 + (k + 1) * 128],
                                      wqkv_sb[:, dc * 1536 + 1024:dc * 1536 + 1536],
                                      start=(dc == 0), stop=(dc == 7))
                    t_v.inc(ii)

            @blk.vector
            def _(e, t0=t0, ntl=ntl, gw=gw, gtok=gtok, nrounds=nrounds, tp0=tp0,
                  tv0=tv0, swp_target=swp_target):
                for r in range(nrounds):
                    half = r % 2
                    toff = (r // 2) * 128
                    e.wait_ge(t_tp.h, tp0 + r + 1)
                    dst = xt_sb[:].rearrange("p (dc t) -> p dc t", dc=8)[
                        :, 4 * half:4 * half + 4, toff:toff + 128]
                    v_e.inc(e.tensor_copy(dst, bank(r % 2, 0, 256).bitcast(BF16).rearrange("p (j t) -> p j t", j=4)))
                for k in range(ntl):
                    e.wait_ge(t_v.h, tv0 + k + 1)
                    v_e.inc(e.tensor_copy(v_sb[:, (t0 + k) * 512:(t0 + k + 1) * 512],
                                          bank(6 + (k % 2), 0, 512)))
                e.wait_ge(s_swp.h, swp_target)
                for c in range(8):
                    e.wait_ge(v_rope.h, v_rope.n)
                    src = qkraw[:, c * 512:c * 512 + gw]
                    swp = qswap[:, c * 512:c * 512 + gw]
                    cos = (ropecq_sb if c < 4 else ropeck_sb)[:, gtok:gtok + gw]
                    sin = (ropesq_sb if c < 4 else ropesk_sb)[:, gtok:gtok + gw]
                    cc = c % 4
                    dstt = qrot if c < 4 else krot
                    dst = dstt[:, cc * TOK_IN + gtok:cc * TOK_IN + gtok + gw]
                    e.tensor_tensor(rt1[:, 0:gw], src, cos, op=ALU.mult)
                    v_rope.inc(e.tensor_tensor(rt2[:, 0:gw], swp, sin, op=ALU.mult))
                    e.wait_ge(v_rope.h, v_rope.n)
                    v_rope.inc(e.tensor_tensor(dst, rt1[:, 0:gw], rt2[:, 0:gw], op=ALU.add))

            @blk.gpsimd
            def _(e, gi=gi):
                for t in range(4 * gi + 5, min(4 * gi + 9, NT)):
                    e.wait_ge(t_tp.h, 2 * (t - 5) + 2)
                    s_xl[t % 5].inc(e.dma_start(xload[:, (t % 5) * D:(t % 5 + 1) * D],
                                                x_in[t * 128:(t + 1) * 128, :]), 16)
                n = [5, 5, 5, 5, 4][gi]
                done = [0, 5, 10, 15, 20][gi]
                if gi == 0:
                    e.wait_ge(g_ms.h, 1)
                for k in range(n):
                    r0 = (done + k) * 128
                    s_init.inc(e.dma_start(xsort[r0:r0 + 128, :], initrow[:]), 16)

    es_qkv.close()

    masks_sb = sb(es_p2, "masks_sb", [128, 3 * 384], BF16)
    wout_sb = sb(es_p2, "wout_sb", [128, 8 * INNER], BF16)
    e_sb = sb(es_p2, "e_sb", [128, 2 * 1536], BF16)
    e2_sb = sb(es_p2, "e2_sb", [128, 2 * 1536], BF16)
    pt_sb = sb(es_p2, "pt_sb", [128, 2 * 1536], BF16)
    zacc_sb = sb(es_p2, "zacc_sb", [128, 8], F32)
    attnT = sb(es_p2, "attnT", [128, 512], BF16)
    invz_sb = sb(es_p2, "invz_sb", [128, 8], F32)
    xg_sb = sb(es_p2, "xg_sb", [128, 2 * D], F32)
    z1_sb = sb(es_p2, "z1_sb", [128, D], F32)
    x1aug = sb(es_p2, "x1aug", [128, 6 * AUGW], F32)
    x1T = sb(es_p2, "x1T", [128, 8 * 512], F32)
    scev = sb(es_p2, "scev", [8, 512], F32)

    with nc.Block() as blk:
        @blk.scalar
        def _(e):
            for q in range(2):
                s_xg[q].inc(e.dma_start(xg_sb[:, q * D:(q + 1) * D],
                                        x_in[(q + 2) * 128:(q + 3) * 128, :]), 16)

        @blk.gpsimd
        def _(e):
            s_ldg.inc(e.dma_start(masks_sb[:].rearrange("p (c k) -> p c k", c=3),
                                 masks_d.rearrange("c p k -> p c k")), 16)
            s_ldg.inc(e.dma_start(wout_sb[:].rearrange("p (c f) -> p c f", c=4),
                                  wout.rearrange("(c p) f -> p c f", p=128)), 16)
            for j in range(6):
                ii = e.memset(x1aug[:, j * AUGW + 1024:j * AUGW + 1028], 0.0)
            g_ms.inc(ii)
            for i in range(2):
                wq_load(e, i)

    def cls_of(t):
        return 1 if t == 2 else (2 if t == 3 else 0)

    def p2_S(e, q):
        t = q + 2
        cls = cls_of(t)
        if q == 0:
            e.wait_ge(s_ld.h, 6 * 16)
            e.wait_ge(s_ldg.h, 8 * 16)
        for w in range(2):
            if w == 1:
                e.wait_ge(a_ex.h, 8 * q + 4)
            elif q >= 1:
                e.wait_ge(a_ex.h, 8 * q)
                e.wait_ge(a_zb.h, 2 * q)
            for j in range(4):
                h = 4 * w + j
                pb = (h % 2) * 64
                ch = h // 2
                e.matmul(bank(j, 0, 384),
                         qrot[pb:pb + 64, ch * TOK_IN + t * 128:ch * TOK_IN + (t + 1) * 128],
                         krot[pb:pb + 64, ch * TOK_IN + (t - 2) * 128:ch * TOK_IN + (t + 1) * 128],
                         start=True, stop=False)
            for j in range(4):
                ii = e.matmul(bank(j, 0, 384), identb_sb[:],
                              masks_sb[:, cls * 384:(cls + 1) * 384],
                              start=False, stop=True)
            t_s.inc(ii)

    def p2_PT(e, q):
        P = q % 2
        for w in range(2):
            if q >= 2:
                e.wait_ge(a_xe.h, 2 * q - 2)
            if w == 1:
                e.wait_ge(v_at.h, 2 * q + 1)
            for j in range(4):
                e.wait_ge(v_me.h, 8 * q + 4 * w + j + 1)
                for b in range(3):
                    c0b = (j % 2) * 384 + b * 128
                    ii = e.transpose(
                        bank(4 + 2 * P + j // 2, c0b // 2, 64).bitcast(BF16),
                        e2_sb[:, w * 1536 + j * 384 + b * 128:w * 1536 + j * 384 + (b + 1) * 128],
                        identb_sb[:])
                t_mt.inc(ii)

    def p2_PTPV(e, q, skip_pt=False):
        t = q + 2
        P = q % 2
        if not skip_pt:
            p2_PT(e, q)
        for w in range(2):
            e.wait_ge(a_ex.h, 8 * q + 8)
            e.wait_ge(v_at.h, 2 * q + w + 1)
            if w == 0 and q % 4 == 1 and q >= 5:
                e.wait_ge(a_sc.h, q // 4)
                e.wait_ge(v_tm.h, 4 * (q // 4))
            if w == 1:
                e.wait_ge(a_zb.h, 2 * q + 1)
            for j in range(4):
                h = 4 * w + j
                p0 = (j % 2) * 64
                for b in range(3):
                    ii = e.matmul(bank(j // 2, 384, 128, p0=p0, p1=p0 + 64),
                                  v_sb[:, (t - 2 + b) * 512 + h * 64:(t - 2 + b) * 512 + (h + 1) * 64],
                                  pt_sb[:, w * 1536 + j * 384 + b * 128:w * 1536 + j * 384 + (b + 1) * 128],
                                  start=(b == 0), stop=(b == 2))
            t_pv.inc(ii)

    def p2_Wout(e, q):
        P = q % 2
        e.wait_ge(a_zb.h, 2 * q + 2)
        for dh in range(2):
            for c in range(4):
                ii = e.matmul(bank(4 + 2 * P + dh, 0, 512),
                              attnT[:, c * 128:(c + 1) * 128],
                              wout_sb[:, c * 1024 + dh * 512:c * 1024 + (dh + 1) * 512],
                              start=(c == 0), stop=(c == 3))
            t_wo.inc(ii)

    def p2_x1T(e, q):
        P = q % 2
        e.wait_ge(v_x1.h, q + 1)
        for half in range(2):
            for j in range(4):
                dc = 4 * half + j
                ii = e.transpose(bank(4 + 2 * P + half, j * 128, 128),
                                 x1aug[:, (q % 6) * AUGW + dc * 128:(q % 6) * AUGW + (dc + 1) * 128],
                                 ident_sb[:])
            t_x1t.inc(ii)

    def p2_softmax_v(e, q):
        P = q % 2
        for w in range(2):
            e.wait_ge(a_ex.h, 8 * q + 4 * (w + 1))
            if q >= 1 and w == 0:
                e.wait_ge(v_me.h, 8 * q)
            v_rz.inc(e.reciprocal(invz_sb[:, 4 * w:4 * w + 4],
                                  zacc_sb[:, 4 * w:4 * w + 4]))
            e.wait_ge(v_rz.h, 2 * q + w + 1)
            for j in range(4):
                v_me.inc(e.tensor_scalar(
                    e2_sb[:, w * 1536 + j * 384:w * 1536 + (j + 1) * 384],
                    e_sb[:, w * 1536 + j * 384:w * 1536 + (j + 1) * 384],
                    invz_sb[:, 4 * w + j:4 * w + j + 1], None, op0=ALU.mult))
    def p2_evacpt_v(e, q):
        P = q % 2
        for w in range(2):
            e.wait_ge(t_mt.h, 8 * q + 4 * (w + 1))
            if w == 0:
                e.wait_ge(t_pv.h, 2 * q)
            srcv = psum[:, (4 + 2 * P) * 512:(6 + 2 * P) * 512].bitcast(BF16)\
                .rearrange("p (bk c) -> p bk c", bk=2)[:, :, 0:768]
            dstv = pt_sb[:, w * 1536:(w + 1) * 1536].rearrange("p (bk c) -> p bk c", bk=2)
            v_at.inc(e.tensor_copy(dstv, srcv))

    def p2_evacat_v(e, q):
        for w in range(2):
            e.wait_ge(t_pv.h, 2 * q + w + 1)
            if w == 0 and q >= 1:
                e.wait_ge(t_wo.h, 2 * q)
            srct = psum[:, 0:1024].rearrange("p (j c) -> p j c", j=2)[:, :, 384:512]
            a_zb.inc(e.tensor_copy(
                attnT[:, w * 256:(w + 1) * 256].rearrange("p (j c) -> p j c", j=2), srct))

    def p2_ln1_v(e, q):
        P = q % 2
        for dh in range(2):
            e.wait_ge(t_wo.h, 2 * q + dh + 1)
            e.wait_ge(s_xg[q % 2].h, 16 * (q // 2 + 1))
            if dh == 0 and q >= 1:
                e.wait_ge(v_x1.h, q)
            v_z1.inc(e.tensor_tensor(z1_sb[:, dh * 512:(dh + 1) * 512],
                                     xg_sb[:, (q % 2) * D + dh * 512:(q % 2) * D + (dh + 1) * 512],
                                     bank(4 + 2 * P + dh, 0, 512), op=ALU.add))
        e.wait_ge(v_z1.h, 4 * q + 2)
        e.bn_stats(bnst[:, 0:6], z1_sb[:, 0:512])
        v_z1.inc(e.bn_stats(bnst[:, 6:12], z1_sb[:, 512:1024]))
        e.wait_ge(v_z1.h, 4 * q + 3)
        if q >= 1:
            e.wait_ge(a_rs.h, 2 * q)
        v_z1.inc(e.bn_aggr(bnag[:], bnst[:]))
        e.wait_ge(a_rs.h, 2 * (q + 1))
        if q >= 6:
            e.wait_ge(s_scat.h, 64 * ((q - 6) // 4 + 1))
        v_x1.inc(e.tensor_scalar(x1aug[:, (q % 6) * AUGW:(q % 6) * AUGW + 1024],
                                 z1_sb[:], bnag[:, 0:1], rstd_sb[:],
                                 op0=ALU.subtract, op1=ALU.mult))

    def p2_exps_a(e, q):
        for w in range(2):
            e.wait_ge(t_s.h, 2 * q + w + 1)
            if w == 0 and q >= 1:
                e.wait_ge(v_me.h, 8 * q)
            for j in range(4):
                a_ex.inc(e.activation(
                    e_sb[:, w * 1536 + j * 384:w * 1536 + (j + 1) * 384],
                    bank(j, 0, 384), AF.Exp,
                    accum_out=zacc_sb[:, 4 * w + j:4 * w + j + 1]))

    def p2_tail_a(e, q):
        e.wait_ge(v_z1.h, 4 * q + 4)
        a_rs.inc(e.activation(lnv_sb[:], bnag[:, 1:2], AF.Ln, bias=eps_sb[:]))
        e.wait_ge(a_rs.h, a_rs.n)
        if q >= 1:
            e.wait_ge(v_x1.h, q)
        a_rs.inc(e.activation(rstd_sb[:], lnv_sb[:], AF.Exp, scale=-0.5))

    def p2_x1e_a(e, q):
        P = q % 2
        if q >= 4 and q % 4 == 0:
            e.wait_ge(t_sc.h, q // 4)
        for half in range(2):
            e.wait_ge(t_x1t.h, 2 * q + half + 1)
            dst = x1T[:].rearrange("p (dc x) -> p dc x", dc=8)[
                :, 4 * half:4 * half + 4, (q % 4) * 128:(q % 4) * 128 + 128]
            a_xe.inc(e.copy(dst, bank(4 + 2 * P + half, 0, 512).rearrange("p (j x) -> p j x", j=4)))
        if q + 2 < NQT:
            s_xg[(q + 2) % 2].inc(
                e.dma_start(xg_sb[:, ((q + 2) % 2) * D:((q + 2) % 2 + 1) * D],
                            x_in[(q + 4) * 128:(q + 5) * 128, :]), 16)

    def router_pe(e, g):
        g0 = 4 * g
        e.wait_ge(a_xe.h, 8 * (g + 1))
        e.wait_ge(a_ex.h, min(8 * (g0 + 5) + 8, 8 * 16))
        for dc in range(8):
            ii = e.matmul(bank(0, 0, 512, p0=0, p1=8),
                          wr_sb[:, dc * 8:(dc + 1) * 8],
                          x1T[:, dc * 512:(dc + 1) * 512],
                          start=(dc == 0), stop=(dc == 7))
        t_sc.inc(ii)
        e.wait_ge(a_sc.h, g + 1)
        for k in range(4):
            ii = e.transpose(bank(1, k * 128, 8, p0=0, p1=128),
                             scev[0:8, k * 128:(k + 1) * 128], ident_sb[0:8, 0:8])
        t_sct.inc(ii)
        for k in range(4):
            qq = g0 + k
            e.wait_ge(v_oh.h, 4 * qq + 4)
            e.wait_ge(v_ro.h, qq + 1)
            if k == 0 and g >= 1:
                e.wait_ge(v_sl.h, 7 * (g0 - 1) + 2)
            e.matmul(bank(2, 384 + k * 8, 8, p0=0, p1=128), lstrict_sb[:],
                     oh_sb[:], start=True, stop=False)
            e.matmul(bank(2, 384 + k * 8, 8, p0=0, p1=128), onesr_sb[0:1, 0:128],
                     rowoff[:], start=False, stop=True)
            t_rk.inc(e.matmul(bank(2, 416 + k * 8, 8, p0=0, p1=1),
                              onesr_sb[:, 0:1], oh_sb[:],
                              start=True, stop=True))

    def router_sc(e, g):
        e.wait_ge(t_sc.h, g + 1)
        a_sc.inc(e.activation(scev[:], bank(0, 0, 512, p0=0, p1=8),
                              AF.Identity, bias=br_sb[:]))

    def router_vec(e, g):
        g0 = 4 * g
        e.wait_ge(t_sct.h, g + 1)
        for k in range(4):
            v_tm.inc(e.tensor_copy(stm[:, k * 8:(k + 1) * 8],
                                   bank(1, k * 128, 8, p0=0, p1=128)))
        for k in range(4):
            qq = g0 + k
            e.wait_ge(v_tm.h, 4 * g + 4)
            if qq > 0:
                e.wait_ge(t_rk.h, qq)
            v_oh.inc(e.max(mx8[:], stm[:, k * 8:(k + 1) * 8]))
            e.wait_ge(v_oh.h, 4 * qq + 1)
            v_oh.inc(e.max_index(idx8[:], mx8[:], stm[:, k * 8:(k + 1) * 8]))
            e.wait_ge(v_oh.h, 4 * qq + 2)
            v_oh.inc(e.tensor_copy(eidf[:], idx8[:, 0:1]))
            e.wait_ge(v_oh.h, 4 * qq + 3)
            v_oh.inc(e.tensor_scalar(oh_sb[:], iota8_sb[:], eidf[:], None,
                                     op0=ALU.is_equal))
            if qq > 0:
                e.wait_ge(v_sl.h, 7 * qq - 5)
            v_ro.inc(e.tensor_copy(rowoff[:], runoff[:, (qq % 2) * 8:(qq % 2) * 8 + 8]))
            e.wait_ge(t_rk.h, qq + 1)
            v_sl.inc(e.tensor_tensor(scr8[:], oh_sb[:].bitcast(F32),
                                     bank(2, 384 + k * 8, 8, p0=0, p1=128),
                                     op=ALU.mult))
            e.wait_ge(v_sl.h, 7 * qq + 1)
            e.reduce_sum(ranksel[:], scr8[:], axis=AX.X)
            v_sl.inc(e.tensor_tensor(runoff[:, ((qq + 1) % 2) * 8:((qq + 1) % 2) * 8 + 8],
                                     runoff[:, (qq % 2) * 8:(qq % 2) * 8 + 8],
                                     bank(2, 416 + k * 8, 8, p0=0, p1=1),
                                     op=ALU.add))
            e.wait_ge(v_sl.h, 7 * qq + 2)
            e.tensor_scalar(ovf_sb[:], ranksel[:], float(CAP), None, op0=ALU.is_ge)
            v_sl.inc(e.tensor_scalar(scr1[:], eidf[:], float(CAP), None, op0=ALU.mult))
            e.wait_ge(v_sl.h, 7 * qq + 3)
            e.wait_ge(g_io.h, qq + 1)
            e.tensor_tensor(slotf[:], ranksel[:], scr1[:], op=ALU.add)
            v_sl.inc(e.tensor_scalar(scr2[:], tokidf[:, qq % 8:qq % 8 + 1],
                                     float(TRASH), None, op0=ALU.add))
            e.wait_ge(v_sl.h, 7 * qq + 4)
            v_sl.inc(e.tensor_tensor(scr3[:], scr2[:], slotf[:], op=ALU.subtract))
            e.wait_ge(v_sl.h, 7 * qq + 5)
            v_sl.inc(e.tensor_scalar(slotf2[:], scr3[:], ovf_sb[:], slotf[:],
                                     op0=ALU.mult, op1=ALU.add))
            e.wait_ge(v_sl.h, 7 * qq + 6)
            if g >= 2:
                e.wait_ge(s_scat.h, 16 * 4 * (g - 1))
            v_sl.inc(e.tensor_copy(slotu[:, (g % 2) * 4 + k:(g % 2) * 4 + k + 1],
                                   slotf2[:]))

    def router_gps(e, g):
        for k in range(4):
            qq = 4 * g + k
            e.wait_ge(v_sl.h, 7 * qq + 7)
            e.wait_ge(v_x1.h, qq + 1)
            if qq == 0:
                e.wait_ge(s_init.h, 24 * 16)
                e.wait_ge(g_ms.h, 2)
            s_scat.inc(e.indirect_dma_start(
                xsort[:, :],
                bass.IndirectOffsetOnAxis(ap=slotu[:, (g % 2) * 4 + k:(g % 2) * 4 + k + 1], axis=0),
                x1aug[:, (qq % 6) * AUGW:(qq % 6) * AUGW + AUGW], None), 16)

    def rtr(q):
        return q >= 5 and q % 4 == 1

    with nc.Block() as blk:
        @blk.tensor
        def _(e):
            for q in range(18):
                if 1 <= q <= 16:
                    p2_PTPV(e, q - 1, skip_pt=rtr(q - 1))
                if q <= 15:
                    p2_S(e, q)
                if 1 <= q <= 16:
                    p2_Wout(e, q - 1)
                if 2 <= q <= 17:
                    p2_x1T(e, q - 2)
                if rtr(q):
                    if q <= 15:
                        p2_PT(e, q)
                    router_pe(e, (q - 5) // 4)

        @blk.vector
        def _(e):
            for q in range(18):
                if q <= 15:
                    p2_softmax_v(e, q)
                if 1 <= q <= 16:
                    p2_ln1_v(e, q - 1)
                if q <= 15:
                    p2_evacpt_v(e, q)
                if rtr(q):
                    router_vec(e, (q - 5) // 4)
                if q <= 15:
                    p2_evacat_v(e, q)

        @blk.scalar
        def _(e):
            for q in range(18):
                if q <= 15:
                    p2_exps_a(e, q)
                if 1 <= q <= 16:
                    p2_tail_a(e, q - 1)
                if 2 <= q <= 17:
                    p2_x1e_a(e, q - 2)
                if rtr(q):
                    router_sc(e, (q - 5) // 4)

        @blk.gpsimd
        def _(e):
            for q in range(18):
                if q <= 15:
                    if q >= 6:
                        e.wait_ge(s_scat.h, 64 * ((q - 6) // 4 + 1))
                    e.iota(x1aug[:, (q % 6) * AUGW + 1024:(q % 6) * AUGW + 1025], [[1, 1]],
                           base=q * 128, channel_multiplier=1,
                           allow_small_or_imprecise_dtypes=True)
                    g_io.inc(e.iota(tokidf[:, q % 8:q % 8 + 1], [[1, 1]], base=q * 128,
                                    channel_multiplier=1,
                                    allow_small_or_imprecise_dtypes=True))
                if rtr(q):
                    router_gps(e, (q - 5) // 4)

    es_p2.close()
    es_attn.close()

    wstream23 = sb(es_ffn, "wstream23", [128, 2 * 8192], BF16)
    wstream45 = sb(es_ffn, "wstream45", [128, 2 * 8192], BF16)
    b1_sb = sb(es_ffn, "b1_sb", [128, NE * 32], F32)
    b2_sb = sb(es_ffn, "b2_sb", [128, NE * 8], F32)
    x_sel = sb(es_ffn, "x_sel", [128, 2 * NRT * AUGW], F32)
    xselT = sb(es_ffn, "xselT", [128, 8 * TOKW], BF16)
    hT = sb(es_ffn, "hT", [128, 32 * TOKW], BF16)
    yT = sb(es_ffn, "yT", [128, 8 * TOKW], BF16)
    z_sb = sb(es_ffn, "z_sb", [128, AUGW], F32)
    zn_sb = sb(es_ffn, "zn_sb", [128, 2 * 1024], F32)
    bnst2 = sb(es_ffn, "bnst2", [128, 12], F32)
    bnag2 = sb(es_ffn, "bnag2", [128, 4], F32)
    rstd2 = sb(es_ffn, "rstd2", [128, 2], F32)
    slotO = sb(es_ffn, "slotO", [128, 2], U32)
    ident2_sb = sb(es_ffn, "ident2_sb", [128, 128], F32)
    identb2_sb = sb(es_ffn, "identb2_sb", [128, 128], BF16)
    eps3_sb = sb(es_ffn, "eps3_sb", [128, 1], F32)
    rx_sb = sb(es_ffn, "rx_sb", [128, 1], F32)
    rf_sb = sb(es_ffn, "rf_sb", [128, 1], F32)
    ru_sb = sb(es_ffn, "ru_sb", [128, 1], F32)
    ry_sb = sb(es_ffn, "ry_sb", [128, 1], F32)
    rt1_sb = sb(es_ffn, "rt1_sb", [128, 1], F32)

    XS4 = NRT * AUGW

    with nc.Block() as blk:
        @blk.sync
        def _(e):
            e.wait_ge(s_scat.h, 16 * 16)
            s_b12.inc(e.dma_start(b1_sb[:], b1_d), 16)
            s_b12.inc(e.dma_start(b2_sb[:], b2_d), 16)
            s_ld.inc(e.dma_start(ident2_sb[:], ident_d), 16)

        @blk.scalar
        def _(e):
            e.wait_ge(s_scat.h, 16 * 16)
            for ee in range(2):
                for rt in range(NRT):
                    s_g[ee % 2].inc(e.dma_start(
                        x_sel[:, ee * XS4 + rt * AUGW:ee * XS4 + (rt + 1) * AUGW],
                        xsort[ee * CAP + rt * 128:ee * CAP + (rt + 1) * 128, :]), 16)

        @blk.gpsimd
        def _(e):
            e.wait_ge(s_scat.h, 16 * 16)
            s_ldg.inc(e.dma_start(identb2_sb[:], ident_d), 16)
            for i in range(2, 6):
                wq_load(e, i)
            g_ms.inc(e.memset(eps3_sb[:], 1.0e-5))

    def p3_gelu_a(e, ee, fcg):
        e.wait_ge(t_f1.h, 32 * ee + fcg + 1)
        if fcg == 0:
            if ee == 0:
                e.wait_ge(s_b12.h, 32)
                e.wait_ge(g_ms.h, 3)
            else:
                e.wait_ge(t_f2.h, 4 * ee)
        a_g.inc(e.activation(hT[:, fcg * TOKW:(fcg + 1) * TOKW],
                             bank(2 + (fcg % 2), 0, TOKW), AF.Gelu,
                             bias=b1_sb[:, ee * 32 + fcg:ee * 32 + fcg + 1]))

    with nc.Block() as blk:
        @blk.scalar
        def _(e):
            for fcg in range(32):
                p3_gelu_a(e, 0, fcg)
            for ee in range(NE):
                for dc in range(8):
                    e.wait_ge(t_f2.h, 4 * (ee + 1))
                    if dc == 0:
                        e.wait_ge(t_yt.h, NRT * ee)
                    a_y.inc(e.activation(yT[:, dc * TOKW:(dc + 1) * TOKW],
                                         bank(dc, 0, TOKW), AF.Identity,
                                         bias=b2_sb[:, ee * 8 + dc:ee * 8 + dc + 1]))
                if ee + 1 < NE:
                    for fcg in range(32):
                        p3_gelu_a(e, ee + 1, fcg)
                if ee + 2 < NE:
                    e.wait_ge(v_ln.h, NRT * (ee + 1))
                    par2 = (ee + 2) % 2
                    for rt in range(NRT):
                        s_g[par2].inc(e.dma_start(
                            x_sel[:, par2 * XS4 + rt * AUGW:par2 * XS4 + (rt + 1) * AUGW],
                            xsort[(ee + 2) * CAP + rt * 128:(ee + 2) * CAP + (rt + 1) * 128, :]), 16)

        @blk.tensor
        def _(e):
            for ee in range(NE):
                par = ee % 2
                if ee == 0:
                    e.wait_ge(s_ld.h, 7 * 16)
                    e.wait_ge(s_ldg.h, 9 * 16)
                e.wait_ge(s_g[ee % 2].h, 16 * NRT * (ee // 2 + 1))
                if ee >= 1:
                    e.wait_ge(a_y.h, 8 * ee - 4)
                for rt in range(NRT):
                    if rt >= 2:
                        e.wait_ge(v_e3.h, 2 * NRT * ee + 2 * (rt - 1))
                    for dc in range(8):
                        ii = e.transpose(bank(2 * (rt % 2) + dc // 4, (dc % 4) * 128, 128),
                                         x_sel[:, par * XS4 + rt * AUGW + dc * 128:par * XS4 + rt * AUGW + (dc + 1) * 128],
                                         ident2_sb[:])
                    t_tp3.inc(ii)
                e.wait_ge(v_e3.h, 2 * NRT * (ee + 1))
                for j in range(4):
                    i = 8 * ee + j
                    e.wait_ge(s_w[i % 6].h, 16 * (i // 6 + 1))
                    for fc in range(8):
                        fcg = 8 * j + fc
                        if fcg >= 2:
                            e.wait_ge(a_g.h, 32 * ee + fcg - 1)
                        for dc in range(8):
                            ii = e.matmul(bank(2 + (fcg % 2), 0, TOKW),
                                          wslot(i)[:, dc * 1024 + fc * 128:
                                                   dc * 1024 + (fc + 1) * 128],
                                          xselT[:, dc * TOKW:(dc + 1) * TOKW],
                                          start=(dc == 0), stop=(dc == 7))
                        t_f1.inc(ii)
                e.wait_ge(a_g.h, 32 * (ee + 1))
                if ee >= 1:
                    e.wait_ge(v_zz.h, 15 * (NRT * ee - 1) + 1)
                for j in range(4):
                    i = 8 * ee + 4 + j
                    e.wait_ge(s_w[i % 6].h, 16 * (i // 6 + 1))
                    for fc in range(8):
                        fcg = 8 * j + fc
                        for dc in range(8):
                            ii = e.matmul(bank(dc, 0, TOKW),
                                          wslot(i)[:, fc * 1024 + dc * 128:
                                                   fc * 1024 + (dc + 1) * 128],
                                          hT[:, fcg * TOKW:(fcg + 1) * TOKW],
                                          start=(fcg == 0), stop=(fcg == 31))
                    t_f2.inc(ii)
                e.wait_ge(a_y.h, 8 * (ee + 1))
                for rt in range(NRT):
                    if rt >= 2:
                        e.wait_ge(v_zz.h, 15 * (NRT * ee + rt - 2) + 1)
                    for dc in range(8):
                        ii = e.transpose(bank(4 + 2 * (rt % 2) + dc // 4, (dc % 4) * 64, 64).bitcast(BF16),
                                         yT[:, dc * TOKW + rt * 128:dc * TOKW + (rt + 1) * 128],
                                         identb2_sb[:])
                    t_yt.inc(ii)

        @blk.vector
        def _(e):
            for rt in range(NRT):
                e.wait_ge(t_tp3.h, rt + 1)
                for half in range(2):
                    dst = xselT[:].rearrange("p (dc x) -> p dc x", dc=8)[
                        :, 4 * half:4 * half + 4, rt * 128:(rt + 1) * 128]
                    v_e3.inc(e.tensor_copy(dst, bank(2 * (rt % 2) + half, 0, 512).rearrange("p (j x) -> p j x", j=4)))
            for ee in range(NE):
                par = ee % 2
                for rt in range(NRT):
                    R = NRT * ee + rt
                    Z0 = 15 * R
                    e.wait_ge(t_yt.h, NRT * ee + rt + 1)
                    if R >= 1:
                        e.wait_ge(v_ln.h, R)
                    for dh in range(2):
                        e.tensor_tensor(z_sb[:, dh * 512:(dh + 1) * 512],
                                        x_sel[:, par * XS4 + rt * AUGW + dh * 512:par * XS4 + rt * AUGW + (dh + 1) * 512],
                                        bank(4 + 2 * (rt % 2) + dh, 0, 256).bitcast(BF16), op=ALU.add)
                    v_zz.inc(e.tensor_copy(z_sb[:, 1024:1025],
                                           x_sel[:, par * XS4 + rt * AUGW + 1024:par * XS4 + rt * AUGW + 1025]))
                    e.wait_ge(v_zz.h, Z0 + 1)
                    e.bn_stats(bnst2[:, 0:6], z_sb[:, 0:512])
                    v_zz.inc(e.bn_stats(bnst2[:, 6:12], z_sb[:, 512:1024]))
                    e.wait_ge(v_zz.h, Z0 + 2)
                    v_zz.inc(e.bn_aggr(bnag2[:, 0:2], bnst2[:]))
                    e.wait_ge(v_zz.h, Z0 + 3)
                    v_zz.inc(e.tensor_scalar(rx_sb[:], bnag2[:, 1:2], 1.0e-5, None, op0=ALU.add))
                    e.wait_ge(v_zz.h, Z0 + 4)
                    v_zz.inc(e.tensor_copy(rf_sb[:], rx_sb[:].bitcast(U32)))
                    e.wait_ge(v_zz.h, Z0 + 5)
                    v_zz.inc(e.tensor_scalar(ru_sb[:], rf_sb[:], -0.5, 1597463007.0,
                                             op0=ALU.mult, op1=ALU.add))
                    e.wait_ge(v_zz.h, Z0 + 6)
                    v_zz.inc(e.tensor_copy(ry_sb[:].bitcast(U32), ru_sb[:]))
                    for _ in range(2):
                        e.wait_ge(v_zz.h, v_zz.n)
                        v_zz.inc(e.tensor_tensor(rt1_sb[:], ry_sb[:], ry_sb[:], op=ALU.mult))
                        e.wait_ge(v_zz.h, v_zz.n)
                        v_zz.inc(e.tensor_tensor(rt1_sb[:], rt1_sb[:], rx_sb[:], op=ALU.mult))
                        e.wait_ge(v_zz.h, v_zz.n)
                        v_zz.inc(e.tensor_scalar(rt1_sb[:], rt1_sb[:], -0.5, 1.5,
                                                 op0=ALU.mult, op1=ALU.add))
                        e.wait_ge(v_zz.h, v_zz.n)
                        v_zz.inc(e.tensor_tensor(ry_sb[:], ry_sb[:], rt1_sb[:], op=ALU.mult))
                    e.wait_ge(v_zz.h, Z0 + 15)
                    if R >= 2:
                        e.wait_ge(s_outd[R % 2].h, 16 * (R // 2))
                    e.tensor_scalar(zn_sb[:, (R % 2) * 1024:(R % 2) * 1024 + 1024],
                                    z_sb[:, 0:1024], bnag2[:, 0:1], ry_sb[:],
                                    op0=ALU.subtract, op1=ALU.mult)
                    v_ln.inc(e.tensor_copy(slotO[:, R % 2:R % 2 + 1], z_sb[:, 1024:1025]))
                if ee + 1 < NE:
                    for rt in range(NRT):
                        e.wait_ge(t_tp3.h, NRT * (ee + 1) + rt + 1)
                        for half in range(2):
                            dst = xselT[:].rearrange("p (dc x) -> p dc x", dc=8)[
                                :, 4 * half:4 * half + 4, rt * 128:(rt + 1) * 128]
                            v_e3.inc(e.tensor_copy(dst, bank(2 * (rt % 2) + half, 0, 512).rearrange("p (j x) -> p j x", j=4)))

        @blk.gpsimd
        def _(e):
            for ee in range(NE):
                for i in range(8 * ee + 6, min(8 * ee + 14, 64)):
                    ip = i - 6
                    eep, jp = ip // 8, ip % 8
                    if jp < 4:
                        e.wait_ge(t_f1.h, 32 * eep + 8 * (jp + 1))
                    else:
                        e.wait_ge(t_f2.h, 4 * eep + (jp - 4) + 1)
                    wq_load(e, i)
                for rt in range(NRT):
                    R = NRT * ee + rt
                    e.wait_ge(v_ln.h, NRT * ee + rt + 1)
                    s_outd[R % 2].inc(e.indirect_dma_start(
                        out_d[:, :],
                        bass.IndirectOffsetOnAxis(ap=slotO[:, R % 2:R % 2 + 1], axis=0),
                        zn_sb[:, (R % 2) * 1024:(R % 2) * 1024 + 1024], None,
                        bounds_check=TOK_OWN - 1, oob_is_err=False), 16)

    with nc.Block() as blk:
        @blk.gpsimd
        def _(e):
            e.wait_ge(s_outd[0].h, 16 * 12)
            e.wait_ge(s_outd[1].h, 16 * 12)

    es_ffn.close()
    es_top.close()
    return nc


def _host_consts(core):
    b, s = core // 2, core % 2
    r = np.arange(128)[:, None]
    m = np.arange(128)[None, :]
    blkA = np.where(m > r, 0.0, NEG).astype(np.float32)
    blkB = np.zeros((128, 128), np.float32)
    blkC = np.where(m <= r, 0.0, NEG).astype(np.float32)
    negb = np.full((128, 128), NEG, np.float32)
    generic = np.concatenate([blkA, blkB, blkC], axis=1)
    if s == 0:
        edge0 = np.concatenate([negb, negb, blkC], axis=1)
        edge1 = np.concatenate([negb, blkB, blkC], axis=1)
    else:
        edge0 = generic
        edge1 = generic
    masks = np.stack([generic, edge0, edge1])

    pos = (s * 2048 - 256 + np.arange(TOK_IN)).astype(np.float32)
    inv = (1.0 / 10000.0 ** (np.arange(0, 64, 2) / 64.0)).astype(np.float32)
    dmod = np.arange(128) % 64
    fidx = dmod % 32
    ang = pos[None, :] * inv[fidx][:, None]
    ropec = np.cos(ang).astype(np.float32)
    sign = np.where(dmod < 32, -1.0, 1.0).astype(np.float32)
    ropes = (sign[:, None] * np.sin(ang)).astype(np.float32)
    return masks, ropec * 0.125, ropes * 0.125, ropec, ropes


def kernel(L0_attn_x, L0_attn_Wqkv, L0_attn_Wout, L0_ffn_wr, L0_ffn_br,
           L0_ffn_w1, L0_ffn_b1, L0_ffn_w2, L0_ffn_b2):
    x = np.ascontiguousarray(L0_attn_x, dtype=np.float32)
    ident = np.eye(128, dtype=np.float32)
    lstrict = np.triu(np.ones((128, 128), np.float32), 1)
    iota8 = np.broadcast_to(np.arange(NE, dtype=np.float32), (128, NE)).copy()
    onesr = np.ones((128, 128), np.float32)
    shared = dict(
        wqkv=np.ascontiguousarray(L0_attn_Wqkv, np.float32),
        wout=np.ascontiguousarray(L0_attn_Wout, np.float32),
        wr=np.ascontiguousarray(L0_ffn_wr, np.float32),
        br=np.ascontiguousarray(L0_ffn_br, np.float32).reshape(NE, 1),
        w1=np.ascontiguousarray(L0_ffn_w1, np.float32),
        b1=np.ascontiguousarray(np.asarray(L0_ffn_b1, np.float32).reshape(NE, 32, 128).transpose(2, 0, 1).reshape(128, NE * 32)),
        w2=np.ascontiguousarray(L0_ffn_w2, np.float32),
        b2=np.ascontiguousarray(np.asarray(L0_ffn_b2, np.float32).reshape(NE, 8, 128).transpose(2, 0, 1).reshape(128, NE * 8)),
        ident=ident, lstrict=lstrict, iota8=iota8, onesr=onesr,
    )
    in_maps = []
    for c in range(8):
        b, s = c // 2, c % 2
        own = x[b, s * 2048:(s + 1) * 2048]
        halo = np.zeros((256, D), np.float32) if s == 0 else x[b, 1792:2048]
        x_c = np.concatenate([halo, own], axis=0)
        masks, rcq, rsq, rck, rsk = _host_consts(c)
        m = dict(shared)
        m.update(x_in=np.ascontiguousarray(x_c), masks=masks,
                 ropecq=rcq, ropesq=rsq, ropeck=rck, ropesk=rsk)
        in_maps.append(m)

    nc = _build_nc()
    res = run_bass_kernel_spmd(nc, in_maps, core_ids=list(range(8)))
    out = np.empty((4, 4096, D), np.float32)
    for c in range(8):
        out[c // 2, (c % 2) * 2048:(c % 2 + 1) * 2048] = res.results[c]["out"]
    return out


# version 22
# speedup vs baseline: 1.2257x; 1.0227x over previous
import numpy as np
import concourse.bass as bass
import concourse.mybir as mybir
from concourse.bass_utils import run_bass_kernel_spmd
from contextlib import ExitStack

F32 = mybir.dt.float32
F32R = mybir.dt.float32r
BF16 = mybir.dt.bfloat16
U32 = mybir.dt.uint32
AF = mybir.ActivationFunctionType
ALU = mybir.AluOpType
AX = mybir.AxisListType

D = 1024
INNER = 512
NT = 18
NQT = 16
TOK_IN = NT * 128
TOK_OWN = 2048
NE = 8
CAP = 352
TOKW = CAP
NRT = 3
RT_R = [128, 128, 96]
RT_O = [0, 128, 256]
FF = 4096
SORT_ROWS = NE * CAP
TRASH = SORT_ROWS
XS_ROWS = SORT_ROWS + TOK_OWN
AUGW = 1028
NEG = -1.0e30
GRP = [(0, 4), (4, 4), (8, 4), (12, 4), (16, 2)]


class _Sem:
    __slots__ = ("h", "n")

    def __init__(self, h):
        self.h = h
        self.n = 0

    def inc(self, instr, k=1):
        instr.then_inc(self.h, k)
        self.n += k
        return instr


def _build_nc():
    nc = bass.Bass("TRN2", target_bir_lowering=False)

    x_in = nc.dram_tensor("x_in", [TOK_IN, D], F32, kind="ExternalInput").ap()
    wqkv = nc.dram_tensor("wqkv", [D, 3 * INNER], F32, kind="ExternalInput").ap()
    wout = nc.dram_tensor("wout", [INNER, D], F32, kind="ExternalInput").ap()
    wr_d = nc.dram_tensor("wr", [D, NE], F32, kind="ExternalInput").ap()
    br_d = nc.dram_tensor("br", [NE, 1], F32, kind="ExternalInput").ap()
    w1_d = nc.dram_tensor("w1", [NE, D, FF], F32, kind="ExternalInput").ap()
    b1_d = nc.dram_tensor("b1", [128, NE * 32], F32, kind="ExternalInput").ap()
    w2_d = nc.dram_tensor("w2", [NE, FF, D], F32, kind="ExternalInput").ap()
    b2_d = nc.dram_tensor("b2", [128, NE * 8], F32, kind="ExternalInput").ap()
    ident_d = nc.dram_tensor("ident", [128, 128], F32, kind="ExternalInput").ap()
    lstrict_d = nc.dram_tensor("lstrict", [128, 128], F32, kind="ExternalInput").ap()
    masks_d = nc.dram_tensor("masks", [3, 128, 384], F32, kind="ExternalInput").ap()
    ropecq_d = nc.dram_tensor("ropecq", [128, TOK_IN], F32, kind="ExternalInput").ap()
    ropesq_d = nc.dram_tensor("ropesq", [128, TOK_IN], F32, kind="ExternalInput").ap()
    ropeck_d = nc.dram_tensor("ropeck", [128, TOK_IN], F32, kind="ExternalInput").ap()
    ropesk_d = nc.dram_tensor("ropesk", [128, TOK_IN], F32, kind="ExternalInput").ap()
    iota8_d = nc.dram_tensor("iota8", [128, NE], F32, kind="ExternalInput").ap()
    onesr_d = nc.dram_tensor("onesr", [128, 128], F32, kind="ExternalInput").ap()

    xsort = nc.dram_tensor("xsort", [XS_ROWS, AUGW], F32, kind="Internal").ap()
    out_d = nc.dram_tensor("out", [TOK_OWN, D], F32, kind="ExternalOutput").ap()

    es_top = ExitStack()
    es_qkv = ExitStack()
    es_attn = ExitStack()
    es_p2 = ExitStack()
    es_ffn = ExitStack()

    def sem(name):
        return _Sem(es_top.enter_context(nc.semaphore(name)))

    s_ld = sem("s_ld")
    s_ldg = sem("s_ldg")
    s_xl = [sem(f"s_xl{i}") for i in range(5)]
    s_swp = sem("s_swp")
    s_xg = [sem(f"s_xg{i}") for i in range(2)]
    s_w = [sem(f"s_w{i}") for i in range(6)]
    s_g = [sem(f"s_g{i}") for i in range(2)]
    s_scat = sem("s_scat")
    s_outd = [sem(f"s_outd{i}") for i in range(2)]
    s_init = sem("s_init")
    s_b12 = sem("s_b12")

    t_tp = sem("t_tp")
    t_qk = sem("t_qk")
    t_v = sem("t_v")
    v_e = sem("v_e")
    a_e = sem("a_e")
    g_ms = sem("g_ms")
    v_rope = sem("v_rope")
    t_s = sem("t_s")
    t_mt = sem("t_mt")
    v_mr = sem("v_mr")
    t_mb = sem("t_mb")
    v_me = sem("v_me")
    a_ex = sem("a_ex")
    t_pv = sem("t_pv")
    v_rz = sem("v_rz")
    t_zb = sem("t_zb")
    a_zb = sem("a_zb")
    v_at = sem("v_at")
    t_wo = sem("t_wo")
    v_z1 = sem("v_z1")
    a_rs = sem("a_rs")
    v_x1 = sem("v_x1")
    t_x1t = sem("t_x1t")
    a_xe = sem("a_xe")
    t_sc = sem("t_sc")
    a_sc = sem("a_sc")
    t_sct = sem("t_sct")
    v_tm = sem("v_tm")
    v_oh = sem("v_oh")
    v_ro = sem("v_ro")
    t_rk = sem("t_rk")
    v_sl = sem("v_sl")
    g_io = sem("g_io")
    t_wq = sem("t_wq")
    t_f1 = sem("t_f1")
    a_g = sem("a_g")
    t_f2 = sem("t_f2")
    a_y = sem("a_y")
    t_yt = sem("t_yt")
    v_zz = sem("v_zz")
    a_r2 = sem("a_r2")
    v_ln = sem("v_ln")
    t_tp3 = sem("t_tp3")
    v_e3 = sem("v_e3")

    psum = es_top.enter_context(nc.psum_tensor("ps", [128, 4096], F32))

    def bank(k, c0=0, w=512, p0=0, p1=128):
        return psum[p0:p1, k * 512 + c0:k * 512 + c0 + w]

    def sb(es, name, shape, dt=F32):
        return es.enter_context(nc.sbuf_tensor(name, shape, dt))

    wstream01 = sb(es_top, "wstream01", [128, 2 * 8192], BF16)

    def wslot(i):
        m = i % 6
        buf = wstream01 if m < 2 else (wstream23 if m < 4 else wstream45)
        s = m % 2
        return buf[:, s * 8192:(s + 1) * 8192]

    def wq_src(i):
        ee, j = i // 8, i % 8
        if j < 4:
            return w1_d.rearrange("E (dc p) (qq f) -> E qq p dc f", p=128, qq=4)[ee, j]
        return w2_d.rearrange("E (qq fs p) dd -> E qq p fs dd", qq=4, p=128)[ee, j - 4]

    def wq_load(e, i):
        s_w[i % 6].inc(e.dma_start(
            wslot(i).rearrange("p (a b) -> p a b", a=8), wq_src(i)), 16)

    qrot = sb(es_attn, "qrot", [128, 4 * TOK_IN], BF16)
    krot = sb(es_attn, "krot", [128, 4 * TOK_IN], BF16)
    v_sb = sb(es_attn, "v_sb", [128, NT * 512], BF16)
    ident_sb = sb(es_attn, "ident_sb", [128, 128], F32)
    identb_sb = sb(es_attn, "identb_sb", [128, 128], BF16)
    onesr_sb = sb(es_attn, "onesr_sb", [128, 128], F32R)
    wr_sb = sb(es_attn, "wr_sb", [128, 8 * NE], F32)
    br_sb = sb(es_attn, "br_sb", [NE, 1], F32)
    iota8_sb = sb(es_attn, "iota8_sb", [128, NE], F32)
    lstrict_sb = sb(es_attn, "lstrict_sb", [128, 128], F32R)
    bnst = sb(es_attn, "bnst", [128, 12], F32)
    bnag = sb(es_attn, "bnag", [128, 2], F32)
    rstd_sb = sb(es_attn, "rstd_sb", [128, 1], F32)
    stm = sb(es_attn, "stm", [128, 32], F32)
    mx8 = sb(es_attn, "mx8", [128, 8], F32)
    idx8 = sb(es_attn, "idx8", [128, 8], U32)
    eidf = sb(es_attn, "eidf", [128, 1], F32)
    oh_sb = sb(es_attn, "oh_sb", [128, 8], F32R)
    runoff = sb(es_attn, "runoff", [1, 16], F32)
    rowoff = sb(es_attn, "rowoff", [1, 8], F32R)
    scr8 = sb(es_attn, "scr8", [128, 8], F32)
    ranksel = sb(es_attn, "ranksel", [128, 1], F32)
    ovf_sb = sb(es_attn, "ovf_sb", [128, 1], F32)
    scr1 = sb(es_attn, "scr1", [128, 1], F32)
    scr2 = sb(es_attn, "scr2", [128, 1], F32)
    scr3 = sb(es_attn, "scr3", [128, 1], F32)
    slotf = sb(es_attn, "slotf", [128, 1], F32)
    slotf2 = sb(es_attn, "slotf2", [128, 1], F32)
    slotu = sb(es_attn, "slotu", [128, 8], U32)
    tokidf = sb(es_attn, "tokidf", [128, 8], F32)
    initrow = sb(es_attn, "initrow", [128, AUGW], F32)
    eps_sb = sb(es_attn, "eps_sb", [128, 1], F32)
    lnv_sb = sb(es_attn, "lnv_sb", [128, 1], F32)

    wqkv_sb = sb(es_qkv, "wqkv_sb", [128, 8 * 1536], BF16)
    ropecq_sb = sb(es_qkv, "ropecq_sb", [128, TOK_IN], BF16)
    ropesq_sb = sb(es_qkv, "ropesq_sb", [128, TOK_IN], BF16)
    ropeck_sb = sb(es_qkv, "ropeck_sb", [128, TOK_IN], BF16)
    ropesk_sb = sb(es_qkv, "ropesk_sb", [128, TOK_IN], BF16)
    xload = sb(es_qkv, "xload", [128, 5 * D], BF16)
    xt_sb = sb(es_qkv, "xt_sb", [128, 8 * 512], BF16)
    qkraw = sb(es_qkv, "qkraw", [128, 8 * 512], BF16)
    qswap = sb(es_qkv, "qswap", [128, 8 * 512], BF16)
    rt1 = sb(es_qkv, "rt1", [128, 512], BF16)
    rt2 = sb(es_qkv, "rt2", [128, 512], BF16)

    with nc.Block() as blk:
        @blk.sync
        def _(e):
            s_ld.inc(e.dma_start(wr_sb[:].rearrange("p (dc e) -> p dc e", dc=8),
                                 wr_d.rearrange("(dc p) e -> p dc e", p=128)), 16)
            s_ld.inc(e.dma_start(br_sb[:], br_d), 16)
            s_ld.inc(e.dma_start(ident_sb[:], ident_d), 16)
            s_ld.inc(e.dma_start(lstrict_sb[:], lstrict_d.bitcast(F32R)), 16)
            s_ld.inc(e.dma_start(iota8_sb[:], iota8_d), 16)
            s_ld.inc(e.dma_start(onesr_sb[:], onesr_d.bitcast(F32R)), 16)

        @blk.gpsimd
        def _(e):
            e.dma_start(wqkv_sb[:].rearrange("p (dc f) -> p dc f", dc=8),
                        wqkv.rearrange("(dc p) f -> p dc f", p=128)).then_inc(s_ldg.h, 16)
            s_ldg.n += 16
            e.dma_start(identb_sb[:], ident_d).then_inc(s_ldg.h, 16)
            s_ldg.n += 16
            for t in range(5):
                s_xl[t].inc(e.dma_start(xload[:, t * D:(t + 1) * D],
                                        x_in[t * 128:(t + 1) * 128, :]), 16)
            s_ldg.inc(e.dma_start(ropecq_sb[:], ropecq_d), 16)
            s_ldg.inc(e.dma_start(ropesq_sb[:], ropesq_d), 16)
            s_ldg.inc(e.dma_start(ropeck_sb[:], ropeck_d), 16)
            s_ldg.inc(e.dma_start(ropesk_sb[:], ropesk_d), 16)
            e.memset(runoff[:], 0.0)
            e.memset(eps_sb[:], 1.0e-5)
            e.memset(initrow[:, 0:1024], 0.0)
            g_ms.inc(e.memset(initrow[:, 1024:AUGW], 1.0e9))

    for gi, (t0, ntl) in enumerate(GRP):
        gw = ntl * 128
        gtok = t0 * 128
        nrounds = 2 * ntl
        tp0, ve0, ae0, qk0, tv0 = t_tp.n, v_e.n, a_e.n, t_qk.n, t_v.n
        swp_target = s_swp.n + 64

        with nc.Block() as blk:
            @blk.scalar
            def _(e, gi=gi, gw=gw, qk0=qk0, ae0=ae0):
                for fc in range(8):
                    e.wait_ge(t_qk.h, qk0 + fc + 1)
                    a_e.inc(e.copy(qkraw[:, fc * 512:fc * 512 + gw], bank(2 + (fc % 4), 0, gw)))
                e.wait_ge(a_e.h, ae0 + 8)
                for bi in range(4):
                    src_lo = [32, 0, 96, 64][bi]
                    s_swp.inc(e.dma_start(qswap[bi * 32:(bi + 1) * 32, 0:8 * 512],
                                          qkraw[src_lo:src_lo + 32, 0:8 * 512]), 16)

            @blk.tensor
            def _(e, gi=gi, t0=t0, ntl=ntl, gw=gw, nrounds=nrounds, tp0=tp0, ve0=ve0,
                  ae0=ae0):
                if gi == 0:
                    e.wait_ge(s_ld.h, 6 * 16)
                    e.wait_ge(s_ldg.h, 6 * 16)
                    e.wait_ge(g_ms.h, 1)
                for r in range(nrounds):
                    t = t0 + r // 2
                    half = r % 2
                    e.wait_ge(s_xl[t % 5].h, 16 * (t // 5 + 1))
                    if r >= 2:
                        e.wait_ge(v_e.h, ve0 + (r - 1))
                    for j in range(4):
                        dc = half * 4 + j
                        ii = e.transpose(bank(r % 2, j * 64, 64).bitcast(BF16),
                                         xload[:, (t % 5) * D + dc * 128:(t % 5) * D + (dc + 1) * 128],
                                         identb_sb[:])
                    t_tp.inc(ii)
                e.wait_ge(v_e.h, ve0 + nrounds)
                for fc in range(8):
                    bk = 2 + (fc % 4)
                    if fc >= 4:
                        e.wait_ge(a_e.h, ae0 + (fc - 4) + 1)
                    for dc in range(8):
                        ii = e.matmul(bank(bk, 0, gw),
                                      wqkv_sb[:, dc * 1536 + fc * 128:dc * 1536 + (fc + 1) * 128],
                                      xt_sb[:, dc * 512:dc * 512 + gw],
                                      start=(dc == 0), stop=(dc == 7))
                    t_qk.inc(ii)
                for k in range(ntl):
                    bk = 6 + (k % 2)
                    if k >= 2:
                        e.wait_ge(v_e.h, ve0 + nrounds + (k - 2) + 1)
                    for dc in range(8):
                        ii = e.matmul(bank(bk, 0, 512),
                                      xt_sb[:, dc * 512 + k * 128:dc * 512 + (k + 1) * 128],
                                      wqkv_sb[:, dc * 1536 + 1024:dc * 1536 + 1536],
                                      start=(dc == 0), stop=(dc == 7))
                    t_v.inc(ii)

            @blk.vector
            def _(e, t0=t0, ntl=ntl, gw=gw, gtok=gtok, nrounds=nrounds, tp0=tp0,
                  tv0=tv0, swp_target=swp_target):
                for r in range(nrounds):
                    half = r % 2
                    toff = (r // 2) * 128
                    e.wait_ge(t_tp.h, tp0 + r + 1)
                    dst = xt_sb[:].rearrange("p (dc t) -> p dc t", dc=8)[
                        :, 4 * half:4 * half + 4, toff:toff + 128]
                    v_e.inc(e.tensor_copy(dst, bank(r % 2, 0, 256).bitcast(BF16).rearrange("p (j t) -> p j t", j=4)))
                for k in range(ntl):
                    e.wait_ge(t_v.h, tv0 + k + 1)
                    v_e.inc(e.tensor_copy(v_sb[:, (t0 + k) * 512:(t0 + k + 1) * 512],
                                          bank(6 + (k % 2), 0, 512)))
                e.wait_ge(s_swp.h, swp_target)
                for c in range(8):
                    e.wait_ge(v_rope.h, v_rope.n)
                    src = qkraw[:, c * 512:c * 512 + gw]
                    swp = qswap[:, c * 512:c * 512 + gw]
                    cos = (ropecq_sb if c < 4 else ropeck_sb)[:, gtok:gtok + gw]
                    sin = (ropesq_sb if c < 4 else ropesk_sb)[:, gtok:gtok + gw]
                    cc = c % 4
                    dstt = qrot if c < 4 else krot
                    dst = dstt[:, cc * TOK_IN + gtok:cc * TOK_IN + gtok + gw]
                    e.tensor_tensor(rt1[:, 0:gw], src, cos, op=ALU.mult)
                    v_rope.inc(e.tensor_tensor(rt2[:, 0:gw], swp, sin, op=ALU.mult))
                    e.wait_ge(v_rope.h, v_rope.n)
                    v_rope.inc(e.tensor_tensor(dst, rt1[:, 0:gw], rt2[:, 0:gw], op=ALU.add))

            @blk.gpsimd
            def _(e, gi=gi):
                for t in range(4 * gi + 5, min(4 * gi + 9, NT)):
                    e.wait_ge(t_tp.h, 2 * (t - 5) + 2)
                    s_xl[t % 5].inc(e.dma_start(xload[:, (t % 5) * D:(t % 5 + 1) * D],
                                                x_in[t * 128:(t + 1) * 128, :]), 16)
                n = [5, 5, 4, 4, 4][gi]
                done = [0, 5, 10, 14, 18][gi]
                if gi == 0:
                    e.wait_ge(g_ms.h, 1)
                for k in range(n):
                    r0 = (done + k) * 128
                    s_init.inc(e.dma_start(xsort[r0:r0 + 128, :], initrow[:]), 16)

    es_qkv.close()

    masks_sb = sb(es_p2, "masks_sb", [128, 3 * 384], BF16)
    wout_sb = sb(es_p2, "wout_sb", [128, 8 * INNER], BF16)
    e_sb = sb(es_p2, "e_sb", [128, 2 * 1536], BF16)
    e2_sb = sb(es_p2, "e2_sb", [128, 2 * 1536], BF16)
    pt_sb = sb(es_p2, "pt_sb", [128, 2 * 1536], BF16)
    zacc_sb = sb(es_p2, "zacc_sb", [128, 8], F32)
    attnT = sb(es_p2, "attnT", [128, 512], BF16)
    invz_sb = sb(es_p2, "invz_sb", [128, 8], F32)
    xg_sb = sb(es_p2, "xg_sb", [128, 2 * D], F32)
    z1_sb = sb(es_p2, "z1_sb", [128, D], F32)
    x1aug = sb(es_p2, "x1aug", [128, 6 * AUGW], F32)
    x1T = sb(es_p2, "x1T", [128, 8 * 512], F32)
    scev = sb(es_p2, "scev", [8, 512], F32)

    with nc.Block() as blk:
        @blk.scalar
        def _(e):
            for q in range(2):
                s_xg[q].inc(e.dma_start(xg_sb[:, q * D:(q + 1) * D],
                                        x_in[(q + 2) * 128:(q + 3) * 128, :]), 16)

        @blk.gpsimd
        def _(e):
            s_ldg.inc(e.dma_start(masks_sb[:].rearrange("p (c k) -> p c k", c=3),
                                 masks_d.rearrange("c p k -> p c k")), 16)
            s_ldg.inc(e.dma_start(wout_sb[:].rearrange("p (c f) -> p c f", c=4),
                                  wout.rearrange("(c p) f -> p c f", p=128)), 16)
            for j in range(6):
                ii = e.memset(x1aug[:, j * AUGW + 1024:j * AUGW + 1028], 0.0)
            g_ms.inc(ii)
            for i in range(2):
                wq_load(e, i)

    def cls_of(t):
        return 1 if t == 2 else (2 if t == 3 else 0)

    def p2_S(e, q):
        t = q + 2
        cls = cls_of(t)
        if q == 0:
            e.wait_ge(s_ld.h, 6 * 16)
            e.wait_ge(s_ldg.h, 8 * 16)
        for w in range(2):
            if w == 1:
                e.wait_ge(a_ex.h, 8 * q + 4)
            elif q >= 1:
                e.wait_ge(a_ex.h, 8 * q)
                e.wait_ge(a_zb.h, 2 * q)
            for j in range(4):
                h = 4 * w + j
                pb = (h % 2) * 64
                ch = h // 2
                e.matmul(bank(j, 0, 384),
                         qrot[pb:pb + 64, ch * TOK_IN + t * 128:ch * TOK_IN + (t + 1) * 128],
                         krot[pb:pb + 64, ch * TOK_IN + (t - 2) * 128:ch * TOK_IN + (t + 1) * 128],
                         start=True, stop=False)
            for j in range(4):
                ii = e.matmul(bank(j, 0, 384), identb_sb[:],
                              masks_sb[:, cls * 384:(cls + 1) * 384],
                              start=False, stop=True)
            t_s.inc(ii)

    def p2_PT(e, q):
        P = q % 2
        for w in range(2):
            if q >= 2:
                e.wait_ge(a_xe.h, 2 * q - 2)
            if w == 1:
                e.wait_ge(v_at.h, 2 * q + 1)
            for j in range(4):
                e.wait_ge(v_me.h, 8 * q + 4 * w + j + 1)
                for b in range(3):
                    c0b = (j % 2) * 384 + b * 128
                    ii = e.transpose(
                        bank(4 + 2 * P + j // 2, c0b // 2, 64).bitcast(BF16),
                        e2_sb[:, w * 1536 + j * 384 + b * 128:w * 1536 + j * 384 + (b + 1) * 128],
                        identb_sb[:])
                t_mt.inc(ii)

    def p2_PTPV(e, q, skip_pt=False):
        t = q + 2
        P = q % 2
        if not skip_pt:
            p2_PT(e, q)
        for w in range(2):
            e.wait_ge(a_ex.h, 8 * q + 8)
            e.wait_ge(v_at.h, 2 * q + w + 1)
            if w == 0 and q % 4 == 1 and q >= 5:
                e.wait_ge(a_sc.h, q // 4)
                e.wait_ge(v_tm.h, 4 * (q // 4))
            if w == 1:
                e.wait_ge(a_zb.h, 2 * q + 1)
            for j in range(4):
                h = 4 * w + j
                p0 = (j % 2) * 64
                for b in range(3):
                    ii = e.matmul(bank(j // 2, 384, 128, p0=p0, p1=p0 + 64),
                                  v_sb[:, (t - 2 + b) * 512 + h * 64:(t - 2 + b) * 512 + (h + 1) * 64],
                                  pt_sb[:, w * 1536 + j * 384 + b * 128:w * 1536 + j * 384 + (b + 1) * 128],
                                  start=(b == 0), stop=(b == 2))
            t_pv.inc(ii)

    def p2_Wout(e, q):
        P = q % 2
        e.wait_ge(a_zb.h, 2 * q + 2)
        for dh in range(2):
            for c in range(4):
                ii = e.matmul(bank(4 + 2 * P + dh, 0, 512),
                              attnT[:, c * 128:(c + 1) * 128],
                              wout_sb[:, c * 1024 + dh * 512:c * 1024 + (dh + 1) * 512],
                              start=(c == 0), stop=(c == 3))
            t_wo.inc(ii)

    def p2_x1T(e, q):
        P = q % 2
        e.wait_ge(v_x1.h, q + 1)
        for half in range(2):
            for j in range(4):
                dc = 4 * half + j
                ii = e.transpose(bank(4 + 2 * P + half, j * 128, 128),
                                 x1aug[:, (q % 6) * AUGW + dc * 128:(q % 6) * AUGW + (dc + 1) * 128],
                                 ident_sb[:])
            t_x1t.inc(ii)

    def p2_softmax_v(e, q):
        P = q % 2
        for w in range(2):
            e.wait_ge(a_ex.h, 8 * q + 4 * (w + 1))
            if q >= 1 and w == 0:
                e.wait_ge(v_me.h, 8 * q)
            v_rz.inc(e.reciprocal(invz_sb[:, 4 * w:4 * w + 4],
                                  zacc_sb[:, 4 * w:4 * w + 4]))
            e.wait_ge(v_rz.h, 2 * q + w + 1)
            for j in range(4):
                v_me.inc(e.tensor_scalar(
                    e2_sb[:, w * 1536 + j * 384:w * 1536 + (j + 1) * 384],
                    e_sb[:, w * 1536 + j * 384:w * 1536 + (j + 1) * 384],
                    invz_sb[:, 4 * w + j:4 * w + j + 1], None, op0=ALU.mult))
    def p2_evacpt_v(e, q):
        P = q % 2
        for w in range(2):
            e.wait_ge(t_mt.h, 8 * q + 4 * (w + 1))
            if w == 0:
                e.wait_ge(t_pv.h, 2 * q)
            srcv = psum[:, (4 + 2 * P) * 512:(6 + 2 * P) * 512].bitcast(BF16)\
                .rearrange("p (bk c) -> p bk c", bk=2)[:, :, 0:768]
            dstv = pt_sb[:, w * 1536:(w + 1) * 1536].rearrange("p (bk c) -> p bk c", bk=2)
            v_at.inc(e.tensor_copy(dstv, srcv))

    def p2_evacat_v(e, q):
        for w in range(2):
            e.wait_ge(t_pv.h, 2 * q + w + 1)
            if w == 0 and q >= 1:
                e.wait_ge(t_wo.h, 2 * q)
            srct = psum[:, 0:1024].rearrange("p (j c) -> p j c", j=2)[:, :, 384:512]
            a_zb.inc(e.tensor_copy(
                attnT[:, w * 256:(w + 1) * 256].rearrange("p (j c) -> p j c", j=2), srct))

    def p2_ln1_v(e, q):
        P = q % 2
        for dh in range(2):
            e.wait_ge(t_wo.h, 2 * q + dh + 1)
            e.wait_ge(s_xg[q % 2].h, 16 * (q // 2 + 1))
            if dh == 0 and q >= 1:
                e.wait_ge(v_x1.h, q)
            v_z1.inc(e.tensor_tensor(z1_sb[:, dh * 512:(dh + 1) * 512],
                                     xg_sb[:, (q % 2) * D + dh * 512:(q % 2) * D + (dh + 1) * 512],
                                     bank(4 + 2 * P + dh, 0, 512), op=ALU.add))
        e.wait_ge(v_z1.h, 4 * q + 2)
        e.bn_stats(bnst[:, 0:6], z1_sb[:, 0:512])
        v_z1.inc(e.bn_stats(bnst[:, 6:12], z1_sb[:, 512:1024]))
        e.wait_ge(v_z1.h, 4 * q + 3)
        if q >= 1:
            e.wait_ge(a_rs.h, 2 * q)
        v_z1.inc(e.bn_aggr(bnag[:], bnst[:]))
        e.wait_ge(a_rs.h, 2 * (q + 1))
        if q >= 6:
            e.wait_ge(s_scat.h, 64 * ((q - 6) // 4 + 1))
        v_x1.inc(e.tensor_scalar(x1aug[:, (q % 6) * AUGW:(q % 6) * AUGW + 1024],
                                 z1_sb[:], bnag[:, 0:1], rstd_sb[:],
                                 op0=ALU.subtract, op1=ALU.mult))

    def p2_exps_a(e, q):
        for w in range(2):
            e.wait_ge(t_s.h, 2 * q + w + 1)
            if w == 0 and q >= 1:
                e.wait_ge(v_me.h, 8 * q)
            for j in range(4):
                a_ex.inc(e.activation(
                    e_sb[:, w * 1536 + j * 384:w * 1536 + (j + 1) * 384],
                    bank(j, 0, 384), AF.Exp,
                    accum_out=zacc_sb[:, 4 * w + j:4 * w + j + 1]))

    def p2_tail_a(e, q):
        e.wait_ge(v_z1.h, 4 * q + 4)
        a_rs.inc(e.activation(lnv_sb[:], bnag[:, 1:2], AF.Ln, bias=eps_sb[:]))
        e.wait_ge(a_rs.h, a_rs.n)
        if q >= 1:
            e.wait_ge(v_x1.h, q)
        a_rs.inc(e.activation(rstd_sb[:], lnv_sb[:], AF.Exp, scale=-0.5))

    def p2_x1e_a(e, q):
        P = q % 2
        if q >= 4 and q % 4 == 0:
            e.wait_ge(t_sc.h, q // 4)
        for half in range(2):
            e.wait_ge(t_x1t.h, 2 * q + half + 1)
            dst = x1T[:].rearrange("p (dc x) -> p dc x", dc=8)[
                :, 4 * half:4 * half + 4, (q % 4) * 128:(q % 4) * 128 + 128]
            a_xe.inc(e.copy(dst, bank(4 + 2 * P + half, 0, 512).rearrange("p (j x) -> p j x", j=4)))
        if q + 2 < NQT:
            s_xg[(q + 2) % 2].inc(
                e.dma_start(xg_sb[:, ((q + 2) % 2) * D:((q + 2) % 2 + 1) * D],
                            x_in[(q + 4) * 128:(q + 5) * 128, :]), 16)

    def router_pe(e, g):
        g0 = 4 * g
        e.wait_ge(a_xe.h, 8 * (g + 1))
        e.wait_ge(a_ex.h, min(8 * (g0 + 5) + 8, 8 * 16))
        for dc in range(8):
            ii = e.matmul(bank(0, 0, 512, p0=0, p1=8),
                          wr_sb[:, dc * 8:(dc + 1) * 8],
                          x1T[:, dc * 512:(dc + 1) * 512],
                          start=(dc == 0), stop=(dc == 7))
        t_sc.inc(ii)
        e.wait_ge(a_sc.h, g + 1)
        for k in range(4):
            ii = e.transpose(bank(1, k * 128, 8, p0=0, p1=128),
                             scev[0:8, k * 128:(k + 1) * 128], ident_sb[0:8, 0:8])
        t_sct.inc(ii)
        for k in range(4):
            qq = g0 + k
            e.wait_ge(v_oh.h, 4 * qq + 4)
            e.wait_ge(v_ro.h, qq + 1)
            if k == 0 and g >= 1:
                e.wait_ge(v_sl.h, 7 * (g0 - 1) + 2)
            e.matmul(bank(2, 384 + k * 8, 8, p0=0, p1=128), lstrict_sb[:],
                     oh_sb[:], start=True, stop=False)
            e.matmul(bank(2, 384 + k * 8, 8, p0=0, p1=128), onesr_sb[0:1, 0:128],
                     rowoff[:], start=False, stop=True)
            t_rk.inc(e.matmul(bank(2, 416 + k * 8, 8, p0=0, p1=1),
                              onesr_sb[:, 0:1], oh_sb[:],
                              start=True, stop=True))

    def router_sc(e, g):
        e.wait_ge(t_sc.h, g + 1)
        a_sc.inc(e.activation(scev[:], bank(0, 0, 512, p0=0, p1=8),
                              AF.Identity, bias=br_sb[:]))

    def router_vec(e, g):
        g0 = 4 * g
        e.wait_ge(t_sct.h, g + 1)
        for k in range(4):
            v_tm.inc(e.tensor_copy(stm[:, k * 8:(k + 1) * 8],
                                   bank(1, k * 128, 8, p0=0, p1=128)))
        for k in range(4):
            qq = g0 + k
            e.wait_ge(v_tm.h, 4 * g + 4)
            if qq > 0:
                e.wait_ge(t_rk.h, qq)
            v_oh.inc(e.max(mx8[:], stm[:, k * 8:(k + 1) * 8]))
            e.wait_ge(v_oh.h, 4 * qq + 1)
            v_oh.inc(e.max_index(idx8[:], mx8[:], stm[:, k * 8:(k + 1) * 8]))
            e.wait_ge(v_oh.h, 4 * qq + 2)
            v_oh.inc(e.tensor_copy(eidf[:], idx8[:, 0:1]))
            e.wait_ge(v_oh.h, 4 * qq + 3)
            v_oh.inc(e.tensor_scalar(oh_sb[:], iota8_sb[:], eidf[:], None,
                                     op0=ALU.is_equal))
            if qq > 0:
                e.wait_ge(v_sl.h, 7 * qq - 5)
            v_ro.inc(e.tensor_copy(rowoff[:], runoff[:, (qq % 2) * 8:(qq % 2) * 8 + 8]))
            e.wait_ge(t_rk.h, qq + 1)
            v_sl.inc(e.tensor_tensor(scr8[:], oh_sb[:].bitcast(F32),
                                     bank(2, 384 + k * 8, 8, p0=0, p1=128),
                                     op=ALU.mult))
            e.wait_ge(v_sl.h, 7 * qq + 1)
            e.reduce_sum(ranksel[:], scr8[:], axis=AX.X)
            v_sl.inc(e.tensor_tensor(runoff[:, ((qq + 1) % 2) * 8:((qq + 1) % 2) * 8 + 8],
                                     runoff[:, (qq % 2) * 8:(qq % 2) * 8 + 8],
                                     bank(2, 416 + k * 8, 8, p0=0, p1=1),
                                     op=ALU.add))
            e.wait_ge(v_sl.h, 7 * qq + 2)
            e.tensor_scalar(ovf_sb[:], ranksel[:], float(CAP), None, op0=ALU.is_ge)
            v_sl.inc(e.tensor_scalar(scr1[:], eidf[:], float(CAP), None, op0=ALU.mult))
            e.wait_ge(v_sl.h, 7 * qq + 3)
            e.wait_ge(g_io.h, qq + 1)
            e.tensor_tensor(slotf[:], ranksel[:], scr1[:], op=ALU.add)
            v_sl.inc(e.tensor_scalar(scr2[:], tokidf[:, qq % 8:qq % 8 + 1],
                                     float(TRASH), None, op0=ALU.add))
            e.wait_ge(v_sl.h, 7 * qq + 4)
            v_sl.inc(e.tensor_tensor(scr3[:], scr2[:], slotf[:], op=ALU.subtract))
            e.wait_ge(v_sl.h, 7 * qq + 5)
            v_sl.inc(e.tensor_scalar(slotf2[:], scr3[:], ovf_sb[:], slotf[:],
                                     op0=ALU.mult, op1=ALU.add))
            e.wait_ge(v_sl.h, 7 * qq + 6)
            if g >= 2:
                e.wait_ge(s_scat.h, 16 * 4 * (g - 1))
            v_sl.inc(e.tensor_copy(slotu[:, (g % 2) * 4 + k:(g % 2) * 4 + k + 1],
                                   slotf2[:]))

    def router_gps(e, g):
        for k in range(4):
            qq = 4 * g + k
            e.wait_ge(v_sl.h, 7 * qq + 7)
            e.wait_ge(v_x1.h, qq + 1)
            if qq == 0:
                e.wait_ge(s_init.h, 22 * 16)
                e.wait_ge(g_ms.h, 2)
            s_scat.inc(e.indirect_dma_start(
                xsort[:, :],
                bass.IndirectOffsetOnAxis(ap=slotu[:, (g % 2) * 4 + k:(g % 2) * 4 + k + 1], axis=0),
                x1aug[:, (qq % 6) * AUGW:(qq % 6) * AUGW + AUGW], None), 16)

    def rtr(q):
        return q >= 5 and q % 4 == 1

    with nc.Block() as blk:
        @blk.tensor
        def _(e):
            for q in range(18):
                if 1 <= q <= 16:
                    p2_PTPV(e, q - 1, skip_pt=rtr(q - 1))
                if q <= 15:
                    p2_S(e, q)
                if 1 <= q <= 16:
                    p2_Wout(e, q - 1)
                if 2 <= q <= 17:
                    p2_x1T(e, q - 2)
                if rtr(q):
                    if q <= 15:
                        p2_PT(e, q)
                    router_pe(e, (q - 5) // 4)

        @blk.vector
        def _(e):
            for q in range(18):
                if q <= 15:
                    p2_softmax_v(e, q)
                if 1 <= q <= 16:
                    p2_ln1_v(e, q - 1)
                if q <= 15:
                    p2_evacpt_v(e, q)
                if rtr(q):
                    router_vec(e, (q - 5) // 4)
                if q <= 15:
                    p2_evacat_v(e, q)

        @blk.scalar
        def _(e):
            for q in range(18):
                if q <= 15:
                    p2_exps_a(e, q)
                if 1 <= q <= 16:
                    p2_tail_a(e, q - 1)
                if 2 <= q <= 17:
                    p2_x1e_a(e, q - 2)
                if rtr(q):
                    router_sc(e, (q - 5) // 4)

        @blk.gpsimd
        def _(e):
            for q in range(18):
                if q <= 15:
                    if q >= 6:
                        e.wait_ge(s_scat.h, 64 * ((q - 6) // 4 + 1))
                    e.iota(x1aug[:, (q % 6) * AUGW + 1024:(q % 6) * AUGW + 1025], [[1, 1]],
                           base=q * 128, channel_multiplier=1,
                           allow_small_or_imprecise_dtypes=True)
                    g_io.inc(e.iota(tokidf[:, q % 8:q % 8 + 1], [[1, 1]], base=q * 128,
                                    channel_multiplier=1,
                                    allow_small_or_imprecise_dtypes=True))
                if rtr(q):
                    router_gps(e, (q - 5) // 4)

    es_p2.close()
    es_attn.close()

    wstream23 = sb(es_ffn, "wstream23", [128, 2 * 8192], BF16)
    wstream45 = sb(es_ffn, "wstream45", [128, 2 * 8192], BF16)
    b1_sb = sb(es_ffn, "b1_sb", [128, NE * 32], F32)
    b2_sb = sb(es_ffn, "b2_sb", [128, NE * 8], F32)
    x_sel = sb(es_ffn, "x_sel", [128, 2 * NRT * AUGW], F32)
    xselT = sb(es_ffn, "xselT", [128, 8 * TOKW], BF16)
    hT = sb(es_ffn, "hT", [128, 32 * TOKW], BF16)
    yT = sb(es_ffn, "yT", [128, 8 * TOKW], BF16)
    z_sb = sb(es_ffn, "z_sb", [128, AUGW], F32)
    zn_sb = sb(es_ffn, "zn_sb", [128, 2 * 1024], F32)
    bnst2 = sb(es_ffn, "bnst2", [128, 12], F32)
    bnag2 = sb(es_ffn, "bnag2", [128, 4], F32)
    rstd2 = sb(es_ffn, "rstd2", [128, 2], F32)
    slotO = sb(es_ffn, "slotO", [128, 2], U32)
    ident2_sb = sb(es_ffn, "ident2_sb", [128, 128], F32)
    identb2_sb = sb(es_ffn, "identb2_sb", [128, 128], BF16)
    eps3_sb = sb(es_ffn, "eps3_sb", [128, 1], F32)
    rx_sb = sb(es_ffn, "rx_sb", [128, 1], F32)
    rf_sb = sb(es_ffn, "rf_sb", [128, 1], F32)
    ru_sb = sb(es_ffn, "ru_sb", [128, 1], F32)
    ry_sb = sb(es_ffn, "ry_sb", [128, 1], F32)
    rt1_sb = sb(es_ffn, "rt1_sb", [128, 1], F32)

    XS4 = NRT * AUGW

    with nc.Block() as blk:
        @blk.sync
        def _(e):
            e.wait_ge(s_scat.h, 16 * 16)
            s_b12.inc(e.dma_start(b1_sb[:], b1_d), 16)
            s_b12.inc(e.dma_start(b2_sb[:], b2_d), 16)
            s_ld.inc(e.dma_start(ident2_sb[:], ident_d), 16)

        @blk.scalar
        def _(e):
            e.wait_ge(s_scat.h, 16 * 16)
            for ee in range(2):
                for rt in range(NRT):
                    s_g[ee % 2].inc(e.dma_start(
                        x_sel[0:RT_R[rt], ee * XS4 + rt * AUGW:ee * XS4 + (rt + 1) * AUGW],
                        xsort[ee * CAP + RT_O[rt]:ee * CAP + RT_O[rt] + RT_R[rt], :]), 16)

        @blk.gpsimd
        def _(e):
            e.wait_ge(s_scat.h, 16 * 16)
            s_ldg.inc(e.dma_start(identb2_sb[:], ident_d), 16)
            for i in range(2, 6):
                wq_load(e, i)
            g_ms.inc(e.memset(eps3_sb[:], 1.0e-5))

    def p3_gelu_a(e, ee, fcg):
        e.wait_ge(t_f1.h, 32 * ee + fcg + 1)
        if fcg == 0:
            if ee == 0:
                e.wait_ge(s_b12.h, 32)
                e.wait_ge(g_ms.h, 3)
            else:
                e.wait_ge(t_f2.h, 4 * ee)
        a_g.inc(e.activation(hT[:, fcg * TOKW:(fcg + 1) * TOKW],
                             bank(2 + (fcg % 2), 0, TOKW), AF.Gelu,
                             bias=b1_sb[:, ee * 32 + fcg:ee * 32 + fcg + 1]))

    with nc.Block() as blk:
        @blk.scalar
        def _(e):
            for fcg in range(32):
                p3_gelu_a(e, 0, fcg)
            for ee in range(NE):
                for dc in range(8):
                    e.wait_ge(t_f2.h, 4 * (ee + 1))
                    if dc == 0:
                        e.wait_ge(t_yt.h, NRT * ee)
                    a_y.inc(e.activation(yT[:, dc * TOKW:(dc + 1) * TOKW],
                                         bank(dc, 0, TOKW), AF.Identity,
                                         bias=b2_sb[:, ee * 8 + dc:ee * 8 + dc + 1]))
                if ee + 1 < NE:
                    for fcg in range(32):
                        p3_gelu_a(e, ee + 1, fcg)
                if ee + 2 < NE:
                    e.wait_ge(v_ln.h, NRT * (ee + 1))
                    par2 = (ee + 2) % 2
                    for rt in range(NRT):
                        s_g[par2].inc(e.dma_start(
                            x_sel[0:RT_R[rt], par2 * XS4 + rt * AUGW:par2 * XS4 + (rt + 1) * AUGW],
                            xsort[(ee + 2) * CAP + RT_O[rt]:(ee + 2) * CAP + RT_O[rt] + RT_R[rt], :]), 16)

        @blk.tensor
        def _(e):
            for ee in range(NE):
                par = ee % 2
                if ee == 0:
                    e.wait_ge(s_ld.h, 7 * 16)
                    e.wait_ge(s_ldg.h, 9 * 16)
                e.wait_ge(s_g[ee % 2].h, 16 * NRT * (ee // 2 + 1))
                if ee >= 1:
                    e.wait_ge(a_y.h, 8 * ee - 4)
                for rt in range(NRT):
                    if rt >= 2:
                        e.wait_ge(v_e3.h, 2 * NRT * ee + 2 * (rt - 1))
                    for dc in range(8):
                        ii = e.transpose(bank(2 * (rt % 2) + dc // 4, (dc % 4) * 128, RT_R[rt]),
                                         x_sel[0:RT_R[rt], par * XS4 + rt * AUGW + dc * 128:par * XS4 + rt * AUGW + (dc + 1) * 128],
                                         ident2_sb[0:RT_R[rt], 0:RT_R[rt]])
                    t_tp3.inc(ii)
                e.wait_ge(v_e3.h, 2 * NRT * (ee + 1))
                for j in range(4):
                    i = 8 * ee + j
                    e.wait_ge(s_w[i % 6].h, 16 * (i // 6 + 1))
                    for fc in range(8):
                        fcg = 8 * j + fc
                        if fcg >= 2:
                            e.wait_ge(a_g.h, 32 * ee + fcg - 1)
                        for dc in range(8):
                            ii = e.matmul(bank(2 + (fcg % 2), 0, TOKW),
                                          wslot(i)[:, dc * 1024 + fc * 128:
                                                   dc * 1024 + (fc + 1) * 128],
                                          xselT[:, dc * TOKW:(dc + 1) * TOKW],
                                          start=(dc == 0), stop=(dc == 7))
                        t_f1.inc(ii)
                e.wait_ge(a_g.h, 32 * (ee + 1))
                if ee >= 1:
                    e.wait_ge(v_zz.h, 15 * (NRT * ee - 1) + 1)
                for j in range(4):
                    i = 8 * ee + 4 + j
                    e.wait_ge(s_w[i % 6].h, 16 * (i // 6 + 1))
                    for fc in range(8):
                        fcg = 8 * j + fc
                        for dc in range(8):
                            ii = e.matmul(bank(dc, 0, TOKW),
                                          wslot(i)[:, fc * 1024 + dc * 128:
                                                   fc * 1024 + (dc + 1) * 128],
                                          hT[:, fcg * TOKW:(fcg + 1) * TOKW],
                                          start=(fcg == 0), stop=(fcg == 31))
                    t_f2.inc(ii)
                e.wait_ge(a_y.h, 8 * (ee + 1))
                for rt in range(NRT):
                    if rt >= 2:
                        e.wait_ge(v_zz.h, 15 * (NRT * ee + rt - 2) + 1)
                    for dc in range(8):
                        ii = e.transpose(bank(4 + 2 * (rt % 2) + dc // 4, (dc % 4) * 64, 64, 0, RT_R[rt]).bitcast(BF16),
                                         yT[:, dc * TOKW + RT_O[rt]:dc * TOKW + RT_O[rt] + RT_R[rt]],
                                         identb2_sb[:])
                    t_yt.inc(ii)

        @blk.vector
        def _(e):
            for rt in range(NRT):
                e.wait_ge(t_tp3.h, rt + 1)
                for half in range(2):
                    dst = xselT[:].rearrange("p (dc x) -> p dc x", dc=8)[
                        :, 4 * half:4 * half + 4, RT_O[rt]:RT_O[rt] + RT_R[rt]]
                    v_e3.inc(e.tensor_copy(dst, bank(2 * (rt % 2) + half, 0, 512).rearrange("p (j x) -> p j x", j=4)[:, :, 0:RT_R[rt]]))
            for ee in range(NE):
                par = ee % 2
                for rt in range(NRT):
                    R = NRT * ee + rt
                    Z0 = 15 * R
                    e.wait_ge(t_yt.h, NRT * ee + rt + 1)
                    if R >= 1:
                        e.wait_ge(v_ln.h, R)
                    for dh in range(2):
                        e.tensor_tensor(z_sb[:, dh * 512:(dh + 1) * 512],
                                        x_sel[:, par * XS4 + rt * AUGW + dh * 512:par * XS4 + rt * AUGW + (dh + 1) * 512],
                                        bank(4 + 2 * (rt % 2) + dh, 0, 256).bitcast(BF16), op=ALU.add)
                    v_zz.inc(e.tensor_copy(z_sb[:, 1024:1025],
                                           x_sel[:, par * XS4 + rt * AUGW + 1024:par * XS4 + rt * AUGW + 1025]))
                    e.wait_ge(v_zz.h, Z0 + 1)
                    e.bn_stats(bnst2[:, 0:6], z_sb[:, 0:512])
                    v_zz.inc(e.bn_stats(bnst2[:, 6:12], z_sb[:, 512:1024]))
                    e.wait_ge(v_zz.h, Z0 + 2)
                    v_zz.inc(e.bn_aggr(bnag2[:, 0:2], bnst2[:]))
                    e.wait_ge(v_zz.h, Z0 + 3)
                    v_zz.inc(e.tensor_scalar(rx_sb[:], bnag2[:, 1:2], 1.0e-5, None, op0=ALU.add))
                    e.wait_ge(v_zz.h, Z0 + 4)
                    v_zz.inc(e.tensor_copy(rf_sb[:], rx_sb[:].bitcast(U32)))
                    e.wait_ge(v_zz.h, Z0 + 5)
                    v_zz.inc(e.tensor_scalar(ru_sb[:], rf_sb[:], -0.5, 1597463007.0,
                                             op0=ALU.mult, op1=ALU.add))
                    e.wait_ge(v_zz.h, Z0 + 6)
                    v_zz.inc(e.tensor_copy(ry_sb[:].bitcast(U32), ru_sb[:]))
                    for _ in range(2):
                        e.wait_ge(v_zz.h, v_zz.n)
                        v_zz.inc(e.tensor_tensor(rt1_sb[:], ry_sb[:], ry_sb[:], op=ALU.mult))
                        e.wait_ge(v_zz.h, v_zz.n)
                        v_zz.inc(e.tensor_tensor(rt1_sb[:], rt1_sb[:], rx_sb[:], op=ALU.mult))
                        e.wait_ge(v_zz.h, v_zz.n)
                        v_zz.inc(e.tensor_scalar(rt1_sb[:], rt1_sb[:], -0.5, 1.5,
                                                 op0=ALU.mult, op1=ALU.add))
                        e.wait_ge(v_zz.h, v_zz.n)
                        v_zz.inc(e.tensor_tensor(ry_sb[:], ry_sb[:], rt1_sb[:], op=ALU.mult))
                    e.wait_ge(v_zz.h, Z0 + 15)
                    if R >= 2:
                        e.wait_ge(s_outd[R % 2].h, 16 * (R // 2))
                    e.tensor_scalar(zn_sb[:, (R % 2) * 1024:(R % 2) * 1024 + 1024],
                                    z_sb[:, 0:1024], bnag2[:, 0:1], ry_sb[:],
                                    op0=ALU.subtract, op1=ALU.mult)
                    v_ln.inc(e.tensor_copy(slotO[:, R % 2:R % 2 + 1], z_sb[:, 1024:1025]))
                if ee + 1 < NE:
                    for rt in range(NRT):
                        e.wait_ge(t_tp3.h, NRT * (ee + 1) + rt + 1)
                        for half in range(2):
                            dst = xselT[:].rearrange("p (dc x) -> p dc x", dc=8)[
                                :, 4 * half:4 * half + 4, RT_O[rt]:RT_O[rt] + RT_R[rt]]
                            v_e3.inc(e.tensor_copy(dst, bank(2 * (rt % 2) + half, 0, 512).rearrange("p (j x) -> p j x", j=4)[:, :, 0:RT_R[rt]]))

        @blk.gpsimd
        def _(e):
            for ee in range(NE):
                for i in range(8 * ee + 6, min(8 * ee + 14, 64)):
                    ip = i - 6
                    eep, jp = ip // 8, ip % 8
                    if jp < 4:
                        e.wait_ge(t_f1.h, 32 * eep + 8 * (jp + 1))
                    else:
                        e.wait_ge(t_f2.h, 4 * eep + (jp - 4) + 1)
                    wq_load(e, i)
                for rt in range(NRT):
                    R = NRT * ee + rt
                    e.wait_ge(v_ln.h, NRT * ee + rt + 1)
                    s_outd[R % 2].inc(e.indirect_dma_start(
                        out_d[:, :],
                        bass.IndirectOffsetOnAxis(ap=slotO[0:RT_R[rt], R % 2:R % 2 + 1], axis=0),
                        zn_sb[0:RT_R[rt], (R % 2) * 1024:(R % 2) * 1024 + 1024], None,
                        bounds_check=TOK_OWN - 1, oob_is_err=False), 16)

    with nc.Block() as blk:
        @blk.gpsimd
        def _(e):
            e.wait_ge(s_outd[0].h, 16 * 12)
            e.wait_ge(s_outd[1].h, 16 * 12)

    es_ffn.close()
    es_top.close()
    return nc


def _host_consts(core):
    b, s = core // 2, core % 2
    r = np.arange(128)[:, None]
    m = np.arange(128)[None, :]
    blkA = np.where(m > r, 0.0, NEG).astype(np.float32)
    blkB = np.zeros((128, 128), np.float32)
    blkC = np.where(m <= r, 0.0, NEG).astype(np.float32)
    negb = np.full((128, 128), NEG, np.float32)
    generic = np.concatenate([blkA, blkB, blkC], axis=1)
    if s == 0:
        edge0 = np.concatenate([negb, negb, blkC], axis=1)
        edge1 = np.concatenate([negb, blkB, blkC], axis=1)
    else:
        edge0 = generic
        edge1 = generic
    masks = np.stack([generic, edge0, edge1])

    pos = (s * 2048 - 256 + np.arange(TOK_IN)).astype(np.float32)
    inv = (1.0 / 10000.0 ** (np.arange(0, 64, 2) / 64.0)).astype(np.float32)
    dmod = np.arange(128) % 64
    fidx = dmod % 32
    ang = pos[None, :] * inv[fidx][:, None]
    ropec = np.cos(ang).astype(np.float32)
    sign = np.where(dmod < 32, -1.0, 1.0).astype(np.float32)
    ropes = (sign[:, None] * np.sin(ang)).astype(np.float32)
    return masks, ropec * 0.125, ropes * 0.125, ropec, ropes


def kernel(L0_attn_x, L0_attn_Wqkv, L0_attn_Wout, L0_ffn_wr, L0_ffn_br,
           L0_ffn_w1, L0_ffn_b1, L0_ffn_w2, L0_ffn_b2):
    x = np.ascontiguousarray(L0_attn_x, dtype=np.float32)
    ident = np.eye(128, dtype=np.float32)
    lstrict = np.triu(np.ones((128, 128), np.float32), 1)
    iota8 = np.broadcast_to(np.arange(NE, dtype=np.float32), (128, NE)).copy()
    onesr = np.ones((128, 128), np.float32)
    shared = dict(
        wqkv=np.ascontiguousarray(L0_attn_Wqkv, np.float32),
        wout=np.ascontiguousarray(L0_attn_Wout, np.float32),
        wr=np.ascontiguousarray(L0_ffn_wr, np.float32),
        br=np.ascontiguousarray(L0_ffn_br, np.float32).reshape(NE, 1),
        w1=np.ascontiguousarray(L0_ffn_w1, np.float32),
        b1=np.ascontiguousarray(np.asarray(L0_ffn_b1, np.float32).reshape(NE, 32, 128).transpose(2, 0, 1).reshape(128, NE * 32)),
        w2=np.ascontiguousarray(L0_ffn_w2, np.float32),
        b2=np.ascontiguousarray(np.asarray(L0_ffn_b2, np.float32).reshape(NE, 8, 128).transpose(2, 0, 1).reshape(128, NE * 8)),
        ident=ident, lstrict=lstrict, iota8=iota8, onesr=onesr,
    )
    in_maps = []
    for c in range(8):
        b, s = c // 2, c % 2
        own = x[b, s * 2048:(s + 1) * 2048]
        halo = np.zeros((256, D), np.float32) if s == 0 else x[b, 1792:2048]
        x_c = np.concatenate([halo, own], axis=0)
        masks, rcq, rsq, rck, rsk = _host_consts(c)
        m = dict(shared)
        m.update(x_in=np.ascontiguousarray(x_c), masks=masks,
                 ropecq=rcq, ropesq=rsq, ropeck=rck, ropesk=rsk)
        in_maps.append(m)

    nc = _build_nc()
    res = run_bass_kernel_spmd(nc, in_maps, core_ids=list(range(8)))
    out = np.empty((4, 4096, D), np.float32)
    for c in range(8):
        out[c // 2, (c % 2) * 2048:(c % 2 + 1) * 2048] = res.results[c]["out"]
    return out
